# version 14
# baseline (speedup 1.0000x reference)
import math
import numpy as np
import ml_dtypes
from contextlib import ExitStack

import concourse.bass as bass
import concourse.mybir as mybir
from concourse.bass_utils import run_bass_kernel_spmd

F32 = mybir.dt.float32
BF16 = mybir.dt.bfloat16
AF = mybir.ActivationFunctionType
ALU = mybir.AluOpType
AX = mybir.AxisListType
BF16NP = ml_dtypes.bfloat16

NCORE = 8
B, L, DM = 2, 8192, 1024
TOK = B * L
TOKC = TOK // NCORE
FCH = DM // 128
NTB = TOKC // 128
DI, DS, DCONV, DTR = 2048, 16, 4, 64
DIC = DI // NCORE
NH, HD = 16, 64
NE0 = 8
NE1, DE1 = 6, 512
EPS = 1e-5


def _slopes16():
    return [2.0 ** (-0.5 * (i + 1)) for i in range(16)]


class Ctr:
    def __init__(self, es, nc, name):
        self.h = es.enter_context(nc.semaphore(name))
        self.val = 0

    def inc(self, instr, n=1):
        instr.then_inc(self.h, n)
        self.val += n
        return self.val


class DmaToken(tuple):
    @property
    def h(self):
        return self[0].h

    @property
    def val(self):
        return self[1]


class DmaPool:

    def __init__(self, es, nc, engname, k, name):
        self.eng = getattr(nc, engname)
        self.slots = [Ctr(es, nc, f"{name}{i}") for i in range(k)]
        self.k = k
        self.i = 0

    def start(self, out, in_, **kw):
        c = self.slots[self.i % self.k]
        self.i += 1
        if c.val > 0:
            self.eng.wait_ge(c.h, c.val)
        c.inc(self.eng.dma_start(out=out, in_=in_, **kw), 16)
        return DmaToken((c, c.val))

    def drain(self, eng):
        for c in self.slots:
            if c.val > 0:
                eng.wait_ge(c.h, c.val)


class Chain:

    def __init__(self, es, nc, engname):
        self.eng = getattr(nc, engname)
        self.c = Ctr(es, nc, f"ch_{engname}")
        self.serial = engname != "tensor"

    def dep(self, *deps):
        for d in deps:
            if isinstance(d, tuple):
                self.eng.wait_ge(d[0].h, d[1])
            else:
                self.eng.wait_ge(d.h, d.val)
        return self

    def tag(self, instr, n=1):
        return self.c.inc(instr, n)

    def __getattr__(self, name):
        f = getattr(self.eng, name)
        if name.startswith(("wait", "reg", "nop", "dma", "collective")):
            return f

        def wrapped(*a, **k):
            if self.serial:
                self.eng.wait_ge(self.c.h, self.c.val)
            r = f(*a, **k)
            if self.serial:
                self.c.inc(r)
            return r
        return wrapped


class P:
    def __init__(self, name):
        self.nc = bass.Bass("TRN2", target_bir_lowering=False)
        self.es = ExitStack()
        self.name = name

    def __enter__(self):
        self.es.__enter__()
        self.E = self.es.enter_context
        self.blk = self.E(self.nc.Block())
        self.pre = ""
        self.scope = None
        return self

    def open_chains(self):
        nc, es = self.nc, self.es
        self.V = Chain(es, nc, "vector")
        self.S = Chain(es, nc, "scalar")
        self.G = Chain(es, nc, "gpsimd")
        self.T = Chain(es, nc, "tensor")
        self.dmap = DmaPool(es, nc, "sync", 12, "sd")
        self.gdmap = DmaPool(es, nc, "gpsimd", 8, "gd")

    def sdma(self, out, in_, **kw):
        return self.dmap.start(out, in_, **kw)

    def gdma_start(self, out, in_, **kw):
        return self.gdmap.start(out, in_, **kw)

    def wtok(self, eng, tok):
        eng.wait_ge(tok.h, tok.val)

    def drain_all(self, eng):
        self.dmap.drain(eng)
        self.gdmap.drain(eng)

    def close(self):
        self.es.__exit__(None, None, None)

    def sb(self, name, shape, dtype, es=None):
        if es is None:
            es = self.scope if getattr(self, "scope", None) is not None else None
        enter = self.E if es is None else es.enter_context
        return enter(self.nc.sbuf_tensor(self.pre + name, shape, dtype))

    def psum(self, name, shape, dtype):
        return self.nc.psum_tensor(self.pre + name, shape, dtype)

    def enter(self, cm):
        es = self.scope if getattr(self, "scope", None) is not None else self.es
        return es.enter_context(cm)

    def ctr(self, name):
        return Ctr(self.es, self.nc, self.pre + name)

    def constcol(self, val):
        if not hasattr(self, "_cc"):
            self._cc = {}
        if val not in self._cc:
            t = self.sb(f"cc_{len(self._cc)}", [128, 1], F32)
            self.V.memset(t[:], float(val))
            self._cc[val] = t
        return self._cc[val]


def emit_identity(p, tile):
    p.G.iota(tile[:], pattern=[[-1, 128]], base=0, channel_multiplier=1,
             allow_small_or_imprecise_dtypes=True)
    p.V.dep(p.G.c).tensor_scalar(tile[:], tile[:], 0.0, None, op0=ALU.is_equal)
    return p.V.c.val


def emit_ln(p, name, xi_dram, xln, lnb, scoped=False):
    nc = p.nc
    les = ExitStack() if scoped else None
    sb = lambda n, sh, dt: p.sb(n, sh, dt, es=les)
    NT = TOKC // 512
    ones_b = sb(f"{name}_ones", [128, 1], BF16)
    xb = sb(f"{name}_xb", [128, TOKC], BF16)
    xsq = sb(f"{name}_xsq", [128, TOKC], BF16)
    xf = [sb(f"{name}_xf{i}", [128, TOKC], F32) for i in range(2)]
    srow = sb(f"{name}_srow", [1, TOKC], F32)
    tmp_r = sb(f"{name}_tmpr", [1, TOKC], F32)
    inv_r = sb(f"{name}_invr", [1, TOKC], F32)
    negm_r = sb(f"{name}_negmr", [1, TOKC], F32)
    inv_bc = sb(f"{name}_invbc", [128, TOKC], F32)
    negm_bc = sb(f"{name}_negmbc", [128, TOKC], F32)
    tmp_f = sb(f"{name}_tmpf", [128, TOKC], F32)
    V, S, G, T = p.V, p.S, p.G, p.T

    with nc.psum_tensor(f"{name}_pss", [1, TOKC], F32) as pss, \
         nc.psum_tensor(f"{name}_pss2", [1, TOKC], F32) as pss2:
        V.memset(ones_b[:], 1.0)
        mm_done = []
        for fc in range(FCH):
            if fc > 0:
                nc.gpsimd.wait_ge(p.T.c.h, mm_done[fc - 1])
            ldb = p.gdma_start(xb[:], xi_dram[fc * 128:(fc + 1) * 128, :])
            S.dep(ldb)
            if fc > 0:
                S.dep((p.T.c, mm_done[fc - 1]))
            S.activation(out=xsq[:], in_=xb[:], func=AF.Square)
            T.dep(p.S.c, p.V.c)
            for tc in range(NT):
                sl = slice(tc * 512, (tc + 1) * 512)
                T.matmul(pss[:, sl], ones_b[:], xb[:, sl],
                         start=(fc == 0), stop=(fc == FCH - 1))
                mm = T.matmul(pss2[:, sl], ones_b[:], xsq[:, sl],
                              start=(fc == 0), stop=(fc == FCH - 1))
            mm_done.append(T.tag(mm))
        V.dep(p.T.c).tensor_scalar_mul(srow[:], pss[:], 1.0 / DM)
        V.tensor_scalar_mul(negm_r[:], srow[:], -1.0)
        V.tensor_mul(tmp_r[:], srow[:], srow[:])
        V.tensor_scalar_mul(tmp_r[:], tmp_r[:], -1.0)
        V.scalar_tensor_tensor(tmp_r[:], pss2[:], 1.0 / DM, tmp_r[:],
                               op0=ALU.mult, op1=ALU.add)
        epsc = p.constcol(EPS)
        S.dep(p.V.c).activation(out=tmp_r[:], in_=tmp_r[:], func=AF.Ln,
                                bias=epsc[0:1, 0:1])
        S.activation(out=inv_r[:], in_=tmp_r[:], func=AF.Exp, scale=-0.5)
        nc.sync.wait_ge(p.S.c.h, p.S.c.val)
        nc.sync.wait_ge(p.V.c.h, p.V.c.val)
        t1 = p.sdma(lnb[0:1, :], inv_r[:])
        t2 = p.sdma(lnb[1:2, :], negm_r[:])
        p.wtok(nc.sync, t1)
        p.wtok(nc.sync, t2)
        b1 = p.sdma(inv_bc[:].unsqueeze(1), lnb[0:1, :].partition_broadcast(128))
        b2 = p.sdma(negm_bc[:].unsqueeze(1), lnb[1:2, :].partition_broadcast(128))
        xldv = []
        for fc in range(min(2, FCH)):
            xldv.append(p.sdma(xf[fc % 2][:], xi_dram[fc * 128:(fc + 1) * 128, :]))
        for fc in range(FCH):
            V.dep(b1, b2, xldv[fc])
            V.tensor_add(tmp_f[:], xf[fc % 2][:], negm_bc[:])
            V.tensor_mul(xln[fc][:], tmp_f[:], inv_bc[:])
            if fc + 2 < FCH:
                nc.sync.wait_ge(p.V.c.h, p.V.c.val)
                xldv.append(p.sdma(xf[fc % 2][:],
                                   xi_dram[(fc + 2) * 128:(fc + 3) * 128, :]))
    if scoped:
        les.close()
    return p.V.c.val


def build_transpose(direction):
    p = P(direction).__enter__()
    nc = p.nc
    if direction == "ingress":
        src = nc.dram_tensor("x_rows", [TOKC, DM], F32, kind="ExternalInput").ap()
        dst = nc.dram_tensor("x_fm", [DM, TOKC], F32, kind="ExternalOutput").ap()
        NO, NI = NTB, FCH
    else:
        src = nc.dram_tensor("x_fm", [DM, TOKC], F32, kind="ExternalInput").ap()
        dst = nc.dram_tensor("x_rows", [TOKC, DM], F32, kind="ExternalOutput").ap()
        NO, NI = FCH, NTB

    @p.blk.sync
    def _(sync):
        p.open_chains()
        emit_transpose(p, src, dst, NO, NI)
        p.drain_all(nc.sync)
    nc_ = p.nc
    p.close()
    return nc_


def emit_transpose(p, src, dst, NO, NI):
    nc = p.nc
    if True:
        V, T = p.V, p.T
        rt = [p.sb(f"rt{i}", [128, 128 * NI], F32) for i in range(2)]
        idf = p.sb("idf", [128, 128], F32)
        ev = [p.sb(f"ev{i}", [128, 128], F32) for i in range(2)]
        ps = [p.enter(nc.psum_tensor(p.pre + f"ps{i}", [128, 128], F32))
              for i in range(2)]
        id_done = emit_identity(p, idf)
        ldv, stv, evv, tpv = {}, {}, {}, {}
        ldv[0] = p.sdma(rt[0][:], src[0:128, :])
        if NO > 1:
            ldv[1] = p.sdma(rt[1][:], src[128:256, :])
        for o in range(NO):
            for i in range(NI):
                k = o * NI + i
                pb = k % 2
                deps = [ldv[o], (p.V.c, id_done)]
                if k >= 2:
                    deps.append((p.V.c, evv[k - 2]))
                T.dep(*deps)
                tpv[k] = T.tag(T.transpose(ps[pb][:],
                                           rt[o % 2][:, i * 128:(i + 1) * 128], idf[:]))
                V.dep((p.T.c, tpv[k]))
                if k >= 2:
                    V.dep(stv[k - 2])
                V.tensor_copy(ev[pb][:], ps[pb][:])
                evv[k] = p.V.c.val
                nc.sync.wait_ge(p.V.c.h, evv[k])
                stv[k] = p.sdma(dst[i * 128:(i + 1) * 128, o * 128:(o + 1) * 128],
                                ev[pb][:])
            if o + 2 < NO:
                nc.sync.wait_ge(p.T.c.h, tpv[o * NI + NI - 1])
                ldv[o + 2] = p.sdma(rt[o % 2][:], src[(o + 2) * 128:(o + 3) * 128, :])


def build_ingress():
    return build_transpose("ingress")


def build_egress():
    return build_transpose("egress")


def build_ffn0():
    p = P("ffn0").__enter__()
    nc = p.nc
    xi = nc.dram_tensor("x_fm", [DM, TOKC], F32, kind="ExternalInput").ap()
    rw = nc.dram_tensor("router_w", [NE0, DM], F32, kind="ExternalInput").ap()
    ew = nc.dram_tensor("expert_w", [NE0, DM, DM], F32, kind="ExternalInput").ap()
    xo = nc.dram_tensor("x_out", [DM, TOKC], F32, kind="ExternalOutput").ap()
    lnb = nc.dram_tensor("lnb", [2, TOKC], F32).ap()
    grow_d = nc.dram_tensor("grow_d", [NE0, TOKC], BF16).ap()
    brow_d = nc.dram_tensor("brow_d", [NTB, 128], BF16).ap()

    @p.blk.sync
    def _(sync):
        p.open_chains()
        p.constcol(EPS)
        emit_ffn0(p, xi, rw, ew, xo, lnb, grow_d, brow_d)
        p.drain_all(nc.sync)
    nc_ = p.nc
    p.close()
    return nc_


def emit_ffn0(p, xi, rw, ew, xo, lnb, grow_d, brow_d):
    nc = p.nc
    if True:
        V, S, G, T = p.V, p.S, p.G, p.T
        xln = [p.sb(f"xln{fc}", [128, TOKC], BF16) for fc in range(FCH)]
        ln_done = emit_ln(p, p.pre + "ln", xi, xln, lnb)

        rwb = p.sb("rwb", [NE0, DM], BF16)
        rwT = p.sb("rwT", [128, FCH * NE0], BF16)
        idb = p.sb("idb", [128, 128], BF16)
        lg = p.sb("lg", [128, NTB * NE0], F32)
        gate = p.sb("gate", [128, NTB * NE0], F32)
        bias_c = p.sb("bias_c", [128, NTB], F32)
        grow = p.sb("grow", [NE0, TOKC], F32)
        brow = p.sb("brow", [NTB, 128], F32)
        idf = p.sb("idf", [128, 128], F32)
        sc1 = p.sb("sc1", [128, 1], F32)
        sc2 = p.sb("sc2", [128, 1], F32)
        sc3 = p.sb("sc3", [128, 1], F32)
        sc4 = p.sb("sc4", [128, 1], F32)
        e_t = p.sb("e_t", [128, NE0], F32)
        msk1 = p.sb("msk1", [128, NE0], F32)
        msk2 = p.sb("msk2", [128, NE0], F32)
        tmp8 = p.sb("tmp8", [128, NE0], F32)
        emit_identity(p, idb)
        id_done = emit_identity(p, idf)
        rwldv = p.gdma_start(rwb[:], rw[:])
        with nc.psum_tensor(p.pre + "ps_rT", [128, NE0], BF16) as ps_rT, \
             nc.psum_tensor(p.pre + "ps_lg", [128, NE0], F32) as ps_lg, \
             nc.psum_tensor(p.pre + "ps_gT", [NE0, 128], F32) as ps_gT, \
             nc.psum_tensor(p.pre + "ps_bT", [NTB, 128], F32) as ps_bT:
            for fc in range(FCH):
                T.dep(rwldv, (p.V.c, id_done), p.V.c)
                tv = T.tag(T.transpose(ps_rT[:], rwb[:, fc * 128:(fc + 1) * 128],
                                       idb[0:NE0, 0:NE0]))
                V.dep((p.T.c, tv)).tensor_copy(rwT[:, fc * NE0:(fc + 1) * NE0],
                                               ps_rT[:])
            for tb in range(NTB):
                T.dep(p.V.c)
                for fc in range(FCH):
                    mm = T.matmul(ps_lg[:], xln[fc][:, tb * 128:(tb + 1) * 128],
                                  rwT[:, fc * NE0:(fc + 1) * NE0],
                                  start=(fc == 0), stop=(fc == FCH - 1))
                tv = T.tag(mm)
                V.dep((p.T.c, tv)).tensor_copy(lg[:, tb * NE0:(tb + 1) * NE0], ps_lg[:])
            lg_done = p.V.c.val
            for tb in range(NTB):
                sl = slice(tb * NE0, (tb + 1) * NE0)
                V.tensor_reduce(sc1[:], lg[:, sl], axis=AX.X, op=ALU.max, negate=True)
                S.dep(p.V.c).activation(out=e_t[:], in_=lg[:, sl], func=AF.Exp,
                                        bias=sc1[:, 0:1])
                V.dep(p.S.c).tensor_reduce(sc2[:], e_t[:], axis=AX.X, op=ALU.add)
                V.reciprocal(sc2[:], sc2[:])
                V.tensor_scalar(e_t[:], e_t[:], sc2[:, 0:1], None, op0=ALU.mult)
                V.tensor_reduce(sc1[:], e_t[:], axis=AX.X, op=ALU.max)
                V.tensor_scalar(msk1[:], e_t[:], sc1[:, 0:1], None, op0=ALU.is_equal)
                V.scalar_tensor_tensor(tmp8[:], msk1[:], -1e30, e_t[:],
                                       op0=ALU.mult, op1=ALU.add)
                V.tensor_reduce(sc2[:], tmp8[:], axis=AX.X, op=ALU.max)
                V.tensor_scalar(msk2[:], e_t[:], sc2[:, 0:1], None, op0=ALU.is_equal)
                V.tensor_scalar(tmp8[:], msk1[:], -1.0, 1.0, op0=ALU.mult, op1=ALU.add)
                V.tensor_mul(msk2[:], msk2[:], tmp8[:])
                V.tensor_add(sc3[:], sc1[:], sc2[:])
                V.tensor_scalar_add(sc3[:], sc3[:], 1e-6)
                V.reciprocal(sc3[:], sc3[:])
                V.tensor_mul(sc4[:], sc1[:], sc3[:])
                V.tensor_scalar_mul(sc4[:], sc4[:], 0.5)
                V.tensor_mul(sc3[:], sc2[:], sc3[:])
                V.tensor_scalar_mul(sc3[:], sc3[:], 0.5)
                gsl = gate[:, sl]
                V.tensor_scalar(gsl, msk1[:], sc4[:, 0:1], None, op0=ALU.mult)
                V.scalar_tensor_tensor(gsl, msk2[:], sc3[:, 0:1], gsl,
                                       op0=ALU.mult, op1=ALU.add)
                V.tensor_mul(sc4[:], sc4[:], sc1[:])
                V.scalar_tensor_tensor(bias_c[:, tb:tb + 1], sc3[:], sc2[:, 0:1],
                                       sc4[:], op0=ALU.mult, op1=ALU.add)
            for tb in range(NTB):
                T.dep(p.V.c)
                tv = T.tag(T.transpose(ps_gT[:], gate[:, tb * NE0:(tb + 1) * NE0],
                                       idf[:]))
                V.dep((p.T.c, tv)).tensor_copy(grow[:, tb * 128:(tb + 1) * 128],
                                               ps_gT[:])
            T.dep(p.V.c)
            tv = T.tag(T.transpose(ps_bT[:], bias_c[:], idf[:]))
            V.dep((p.T.c, tv)).tensor_copy(brow[:], ps_bT[:])
        nc.gpsimd.wait_ge(p.V.c.h, p.V.c.val)
        gv1 = p.gdma_start(grow_d[:], grow[:])
        gv = p.gdma_start(brow_d[:], brow[:])
        gbc = [p.sb(f"gbc{i}", [128, 512], BF16) for i in range(2)]
        bias_row = p.sb("bias_row", [1, TOKC], BF16)
        onesb = p.sb("onesb", [1, 128], BF16)
        p.wtok(nc.sync, gv1)
        p.wtok(nc.sync, gv)
        biasldv = p.sdma(bias_row[:], brow_d.rearrange("a b -> (a b)").unsqueeze(0))
        V.memset(onesb[:], 1.0)
        ones_done = p.V.c.val

        ewb = [p.sb(f"ewb{i}", [128, FCH * DM], BF16) for i in range(2)]
        xe = [p.sb(f"xe{i}", [128, FCH * 512], BF16) for i in range(2)]
        ev = [p.sb(f"evo{i}", [128, 512], F32) for i in range(2)]
        xres = [p.sb(f"xres{i}", [128, 512], F32) for i in range(2)]
        s_mm = p.ctr("s_mm")
        with ExitStack() as pes:
            pso = [pes.enter_context(nc.psum_tensor(p.pre + f"ps_o{oc}", [128, 512], F32))
                   for oc in range(FCH)]
            wldv, xev = {}, {}
            evv2, stv2 = {}, {}
            for tg in range(4):
                tsl = slice(tg * 512, (tg + 1) * 512)
                for e in range(NE0):
                    k = tg * NE0 + e
                    if k >= 2:
                        nc.gpsimd.wait_ge(s_mm.h, (k - 1) * FCH)
                        nc.sync.wait_ge(s_mm.h, (k - 1) * FCH)
                    ewr = ew[e].rearrange("(h p) o -> p h o", p=128)
                    ewbr = ewb[k % 2][:].rearrange("p (h o) -> p h o", h=FCH)
                    wldv[k] = [p.gdma_start(ewbr[:, hc, :], ewr[:, hc, :])
                               for hc in range(FCH)]
                    gld = p.sdma(gbc[k % 2][:].unsqueeze(1),
                                 grow_d[e:e + 1, tsl].partition_broadcast(128))
                    V.dep(gld)
                    if k >= 2:
                        V.dep((s_mm, (k - 1) * FCH))
                    for fc in range(FCH):
                        V.tensor_mul(xe[k % 2][:, fc * 512:(fc + 1) * 512],
                                     xln[fc][:, tsl], gbc[k % 2][:])
                    xev[k] = p.V.c.val
                    T.dep((p.V.c, xev[k]), *wldv[k])
                    if e == 0:
                        T.dep((p.V.c, ones_done), biasldv)
                        if tg > 0:
                            T.dep((p.V.c, evv2[(tg - 1, FCH - 1)]))
                    ewv = ewb[k % 2].ap().rearrange("p (h o) -> p h o", h=FCH)
                    for oc in range(FCH):
                        for hc in range(FCH):
                            mm = T.matmul(pso[oc][:],
                                          ewv[:, hc, oc * 128:(oc + 1) * 128],
                                          xe[k % 2][:, hc * 512:(hc + 1) * 512],
                                          start=(e == 0 and hc == 0), stop=False)
                        if e == NE0 - 1:
                            mm = T.matmul(pso[oc][:], onesb[:], bias_row[:, tsl],
                                          start=False, stop=True)
                        s_mm.inc(mm)
                for oc in range(FCH):
                    j = tg * FCH + oc
                    pb = j % 2
                    if j >= 2:
                        nc.sync.wait_ge(p.V.c.h, evv2[(tg, oc - 2) if oc >= 2
                                        else (tg - 1, FCH + oc - 2)])
                    rld = p.sdma(xres[pb][:], xi[oc * 128:(oc + 1) * 128, tsl])
                    V.dep((s_mm, (tg * NE0 + NE0 - 1) * FCH + oc + 1),
                          rld)
                    if j >= 2:
                        V.dep(stv2[j - 2])
                    V.tensor_add(ev[pb][:], pso[oc][:], xres[pb][:])
                    evv2[(tg, oc)] = p.V.c.val
                    nc.sync.wait_ge(p.V.c.h, evv2[(tg, oc)])
                    stv2[j] = p.sdma(xo[oc * 128:(oc + 1) * 128, tsl], ev[pb][:])


def build_lnb():
    p = P("lnb").__enter__()
    nc = p.nc
    xi = nc.dram_tensor("x_fm", [DM, TOKC], F32, kind="ExternalInput").ap()
    xlo = nc.dram_tensor("xln_d", [DM, TOKC], BF16, kind="ExternalOutput").ap()
    lnb = nc.dram_tensor("lnb", [2, TOKC], F32).ap()

    @p.blk.sync
    def _(sync):
        p.open_chains()
        p.constcol(EPS)
        emit_lnb(p, xi, xlo, lnb)
        p.drain_all(nc.sync)
    nc_ = p.nc
    p.close()
    return nc_


def emit_lnb(p, xi, xlo, lnb):
    nc = p.nc
    xln = [p.sb(f"xln{fc}", [128, TOKC], BF16) for fc in range(FCH)]
    emit_ln(p, p.pre + "ln", xi, xln, lnb)
    nc.sync.wait_ge(p.V.c.h, p.V.c.val)
    for fc in range(FCH):
        p.sdma(xlo[fc * 128:(fc + 1) * 128, :], xln[fc][:])


CH = 1024
NCH = L // CH


def build_mamba():
    p = P("mamba").__enter__()
    nc = p.nc
    xlf = nc.dram_tensor("xln_full", [DM, TOK], BF16, kind="ExternalInput").ap()
    wx = nc.dram_tensor("wx", [DIC, DM], F32, kind="ExternalInput").ap()
    wz = nc.dram_tensor("wz", [DIC, DM], F32, kind="ExternalInput").ap()
    cw = nc.dram_tensor("conv_w", [128, 2 * DCONV], F32, kind="ExternalInput").ap()
    cb = nc.dram_tensor("conv_b", [128, 2], F32, kind="ExternalInput").ap()
    xpw = nc.dram_tensor("xproj", [96, DIC], F32, kind="ExternalInput").ap()
    dtw = nc.dram_tensor("dtproj", [DIC, DTR], F32, kind="ExternalInput").ap()
    dtb = nc.dram_tensor("dt_b", [128, 2], F32, kind="ExternalInput").ap()
    alog = nc.dram_tensor("a_log", [128, 2 * DS], F32, kind="ExternalInput").ap()
    dvec = nc.dram_tensor("d_vec", [128, 2], F32, kind="ExternalInput").ap()
    ygo = nc.dram_tensor("y_g", [DIC, TOK], BF16, kind="ExternalOutput").ap()
    ar_in = nc.dram_tensor("ar_in", [192, CH], F32)
    ar_out = nc.dram_tensor("ar_out", [192, CH], F32)
    bc_d = nc.dram_tensor("bc_d", [64, CH], BF16)

    @p.blk.sync
    def _(sync):
        p.open_chains()
        p.constcol(1.0)
        emit_mamba(p, xlf, wx, wz, cw, cb, xpw, dtw, dtb, alog, dvec, ygo,
                   ar_in, ar_out, bc_d, yg_flat=True)
        p.drain_all(nc.sync)
    nc_ = p.nc
    p.close()
    return nc_


def emit_mamba(p, xlf, wx, wz, cw, cb, xpw, dtw, dtb, alog, dvec, ygo,
               ar_in, ar_out, bc_d, yg_flat=True, xlf_blocked=False,
               yag_cb=None):
    nc = p.nc
    if True:
        V, S, G, T = p.V, p.S, p.G, p.T
        idb = p.sb("idb", [128, 128], BF16)
        id_done = emit_identity(p, idb)
        wxb = p.sb("wxb", [128, 2 * DM], BF16)
        wzb = p.sb("wzb", [128, 2 * DM], BF16)
        xpb = p.sb("xpb", [96, DIC], BF16)
        dtwb = p.sb("dtwb", [128, 2 * DTR], BF16)
        wxT = p.sb("wxT", [128, FCH * DIC], BF16)
        wzT = p.sb("wzT", [128, FCH * DIC], BF16)
        xpT = p.sb("xpT", [128, 2 * 96], BF16)
        dtT = p.sb("dtT", [64, DIC], BF16)
        negA = p.sb("negA", [128, 2 * DS], F32)
        convw = p.sb("convw", [128, 2 * DCONV], F32)
        convb = p.sb("convb", [128, 2], F32)
        dtbc = p.sb("dtbc", [128, 2], F32)
        dcol = p.sb("dcol", [128, 2], F32)
        onec = p.constcol(1.0)
        l1 = p.gdma_start(wxb[:].rearrange("p (r f) -> p r f", r=2),
                          wx.rearrange("(r p) f -> p r f", p=128))
        l2 = p.gdma_start(wzb[:].rearrange("p (r f) -> p r f", r=2),
                          wz.rearrange("(r p) f -> p r f", p=128))
        l3 = p.gdma_start(xpb[:], xpw[:])
        l4 = p.gdma_start(dtwb[:].rearrange("p (r k) -> p r k", r=2),
                          dtw.rearrange("(r p) k -> p r k", p=128))
        l5 = p.sdma(convw[:], cw[:])
        l6 = p.sdma(convb[:], cb[:])
        l7 = p.sdma(dtbc[:], dtb[:])
        l8 = p.sdma(dcol[:], dvec[:])
        l9 = p.sdma(negA[:], alog[:])
        S.dep(l9).activation(out=negA[:], in_=negA[:], func=AF.Exp)
        V.dep(p.S.c).tensor_scalar_mul(negA[:], negA[:], -1.0)
        with nc.psum_tensor(p.pre + "ps_t", [128, 128], BF16) as ps_t, \
             nc.psum_tensor(p.pre + "ps_t2", [128, 96], BF16) as ps_t2, \
             nc.psum_tensor(p.pre + "ps_t3", [64, 128], BF16) as ps_t3:
            wxv = wxT.ap().rearrange("p (f r) -> p f r", f=FCH)
            wzv = wzT.ap().rearrange("p (f r) -> p f r", f=FCH)
            for (src, dstv, lv) in ((wxb, wxv, l1), (wzb, wzv, l2)):
                sv = src.ap().rearrange("p (r f) -> p r f", r=2)
                for rc in range(2):
                    for fc in range(FCH):
                        T.dep(lv, (p.V.c, id_done), p.V.c)
                        tv = T.tag(T.transpose(
                            ps_t[:], sv[:, rc, fc * 128:(fc + 1) * 128], idb[:]))
                        V.dep((p.T.c, tv)).tensor_copy(
                            dstv[:, fc, rc * 128:(rc + 1) * 128], ps_t[:])
            xpv = xpT.ap().rearrange("p (c k) -> p c k", c=2)
            for dc in range(2):
                T.dep(l3, (p.V.c, id_done), p.V.c)
                tv = T.tag(T.transpose(ps_t2[:], xpb[:, dc * 128:(dc + 1) * 128],
                                       idb[0:96, 0:96]))
                V.dep((p.T.c, tv)).tensor_copy(xpv[:, dc, :], ps_t2[:])
            dv = dtwb.ap().rearrange("p (r k) -> p r k", r=2)
            for rc in range(2):
                T.dep(l4, (p.V.c, id_done), p.V.c)
                tv = T.tag(T.transpose(ps_t3[:], dv[:, rc, :], idb[0:128, 0:128]))
                V.dep((p.T.c, tv)).tensor_copy(dtT[:, rc * 128:(rc + 1) * 128],
                                               ps_t3[:])
        prep_done_V = p.V.c.val
        p.V.dep(l5, l7, l8)
        p.S.dep(l6)

        xin = p.sb("xin", [128, FCH * CH], BF16)
        xraw = p.sb("xraw", [128, 2 * (DCONV - 1 + CH)], F32)
        xs_f = [p.sb(f"xs_f{s}", [128, 2 * CH], F32) for s in range(2)]
        xs_b = [p.sb(f"xs_b{s}", [128, 2 * CH], BF16) for s in range(2)]
        silz = [p.sb(f"silz{s}", [128, 2 * CH], F32) for s in range(2)]
        delt = p.sb("delt", [128, 2 * CH], F32)
        dabs = p.sb("dabs", [128, CH], F32)
        u_b = p.sb("u_b", [128, 2 * CH], BF16)
        xd_sb = p.sb("xd_sb", [96, CH], F32)
        dtin = p.sb("dtin", [64, CH], BF16)
        bbc = [p.sb(f"bbc{i}", [128, CH], BF16) for i in range(2)]
        cbc = [p.sb(f"cbc{i}", [128, CH], BF16) for i in range(2)]
        dA_t = [p.sb(f"dA{i}", [128, CH], BF16) for i in range(2)]
        dBu_t = [p.sb(f"dBu{i}", [128, CH], BF16) for i in range(2)]
        h_t = p.sb("h_t", [128, CH], BF16)
        yc_t = [p.sb(f"yc_t{i}", [128, CH], BF16) for i in range(2)]
        yacc = p.sb("yacc", [128, 2 * CH], F32)
        hcar = p.sb("hcar", [128, 32], BF16)
        ygt = p.sb("ygt", [128, 2 * CH], BF16)
        xinv = xin.ap().rearrange("p (f l) -> p f l", f=FCH)
        xrawv = xraw.ap().rearrange("p (r l) -> p r l", r=2)
        xsfv = [t.ap().rearrange("p (r l) -> p r l", r=2) for t in xs_f]
        xsbv = [t.ap().rearrange("p (r l) -> p r l", r=2) for t in xs_b]
        silzv = [t.ap().rearrange("p (r l) -> p r l", r=2) for t in silz]
        deltv = delt.ap().rearrange("p (r l) -> p r l", r=2)
        ubv = u_b.ap().rearrange("p (r l) -> p r l", r=2)
        yav = yacc.ap().rearrange("p (r l) -> p r l", r=2)
        ygv = ygt.ap().rearrange("p (r l) -> p r l", r=2)
        negAv = negA.ap().rearrange("p (r n) -> p r n", r=2)
        convwv = convw.ap().rearrange("p (r k) -> p r k", r=2)
        NIT = B * NCH
        cc_sems = [p.ctr(f"cc{i}") for i in range(NIT)]
        AS = {}
        BS = {}

        def emit_A(it):
            bi, chk = divmod(it, NCH)
            g0 = bi * L + chk * CH
            s = it % 2
            first = chk == 0
            if it > 0:
                nc.sync.wait_ge(p.T.c.h, AS[it - 1]["xz"])
            if xlf_blocked:
                xr0 = DM * (g0 // TOKC)
                xc0 = g0 % TOKC
                xld = [p.sdma(xinv[:, fc, :],
                              xlf[xr0 + fc * 128:xr0 + (fc + 1) * 128,
                                  xc0:xc0 + CH])
                       for fc in range(FCH)]
            else:
                xld = [p.sdma(xinv[:, fc, :],
                              xlf[fc * 128:(fc + 1) * 128, g0:g0 + CH])
                       for fc in range(FCH)]
            if first:
                if it > 0:
                    V.dep((p.G.c, AS[it - 1]["hist"]))
                V.memset(xraw[:], 0.0)
            hist_ready = p.V.c.val
            with nc.psum_tensor(p.pre + f"ps_x{it}", [128, 2 * CH], F32) as ps_x, \
                 nc.psum_tensor(p.pre + f"ps_z{it}", [128, 2 * CH], F32) as ps_z:
                psxv = ps_x.ap().rearrange("p (r l) -> p r l", r=2)
                pszv = ps_z.ap().rearrange("p (r l) -> p r l", r=2)
                T.dep(*xld)
                T.dep((p.V.c, max(prep_done_V, hist_ready)))
                if it >= 1:
                    T.dep((p.S.c, AS[it - 1]["evac"]))
                if it - 2 in BS:
                    T.dep((p.V.c, BS[it - 2]["delta"]))
                for rc in range(2):
                    for th in range(CH // 512):
                        tsl = slice(th * 512, (th + 1) * 512)
                        for fc in range(FCH):
                            T.matmul(psxv[:, rc, tsl],
                                     wxv[:, fc, rc * 128:(rc + 1) * 128],
                                     xinv[:, fc, tsl],
                                     start=(fc == 0), stop=(fc == FCH - 1))
                            mm = T.matmul(pszv[:, rc, tsl],
                                          wzv[:, fc, rc * 128:(rc + 1) * 128],
                                          xinv[:, fc, tsl],
                                          start=(fc == 0), stop=(fc == FCH - 1))
                xz_done = T.tag(mm)
                S.dep((p.T.c, xz_done))
                if it >= 1:
                    S.dep((p.G.c, AS[it - 1]["hist"]))
                if it - 2 in BS:
                    S.dep((p.V.c, BS[it - 2]["final"]))
                for rc in range(2):
                    S.activation(out=xrawv[:, rc, DCONV - 1:DCONV - 1 + CH],
                                 in_=psxv[:, rc, :], func=AF.Identity)
                    S.activation(out=silzv[s][:, rc, :], in_=pszv[:, rc, :],
                                 func=AF.Silu)
                evac_done = p.S.c.val
            V.dep((p.S.c, evac_done))
            for rc in range(2):
                V.tensor_scalar(xsfv[s][:, rc, :], xrawv[:, rc, 0:CH],
                                convw[:, rc * DCONV:rc * DCONV + 1], None,
                                op0=ALU.mult)
                for j in range(1, DCONV):
                    V.scalar_tensor_tensor(
                        xsfv[s][:, rc, :], xrawv[:, rc, j:j + CH],
                        convw[:, rc * DCONV + j:rc * DCONV + j + 1],
                        xsfv[s][:, rc, :], op0=ALU.mult, op1=ALU.add)
            conv_done = p.V.c.val
            G.dep(p.V.c)
            for rc in range(2):
                G.tensor_copy(xrawv[:, rc, 0:DCONV - 1],
                              xrawv[:, rc, CH:CH + DCONV - 1])
            hist_G = p.G.c.val
            S.dep((p.V.c, conv_done))
            for rc in range(2):
                S.activation(out=xsfv[s][:, rc, :], in_=xsfv[s][:, rc, :],
                             func=AF.Silu, bias=convb[:, rc:rc + 1])
            silu_done = p.S.c.val
            G.dep((p.S.c, silu_done))
            if it - 2 in BS:
                G.dep((p.V.c, BS[it - 2]["delta"]))
            for rc in range(2):
                G.tensor_copy(xsbv[s][:, rc, :], xsfv[s][:, rc, :])
            xsb_done = p.G.c.val
            with nc.psum_tensor(p.pre + f"ps_xd{it}", [96, CH], F32) as ps_xd:
                T.dep((p.G.c, xsb_done))
                if it >= 1:
                    T.dep((p.V.c, AS[it - 1]["xdv"]))
                for th in range(CH // 512):
                    tsl = slice(th * 512, (th + 1) * 512)
                    for dc in range(2):
                        mm = T.matmul(ps_xd[:, tsl], xpv[:, dc, :],
                                      xsbv[s][:, dc, tsl],
                                      start=(dc == 0), stop=(dc == 1))
                xd_done = T.tag(mm)
                V.dep((p.T.c, xd_done))
                if it >= 1:
                    V.dep(AS[it - 1]["arw"])
                V.tensor_copy(xd_sb[:], ps_xd[:])
                xdv = p.V.c.val
            nc.sync.wait_ge(p.V.c.h, xdv)
            if it >= 2:
                nc.sync.wait_ge(cc_sems[it - 2].h, 1)
            arw = p.sdma(ar_in[96 * s:96 * (s + 1), :], xd_sb[:])
            p.wtok(nc.gpsimd, arw)
            if it - 2 in BS:
                for t in BS[it - 2]["arread"]:
                    p.wtok(nc.gpsimd, t)
            nc.gpsimd.collective_compute(
                "AllReduce", ALU.add, replica_groups=[list(range(NCORE))],
                ins=[ar_in[96 * s:96 * (s + 1), :].opt()],
                outs=[ar_out[96 * s:96 * (s + 1), :].opt()],
            ).then_inc(cc_sems[it].h, 1)
            cc_sems[it].val = 1
            AS[it] = {"xz": xz_done, "evac": evac_done, "hist": hist_G,
                      "xdv": xdv, "arw": arw}

        def emit_B(it):
            bi, chk = divmod(it, NCH)
            g0 = bi * L + chk * CH
            s = it % 2
            soff = 32 * s
            first = chk == 0
            nc.gpsimd.wait_ge(cc_sems[it].h, 1)
            if it >= 1:
                nc.gpsimd.wait_ge(p.T.c.h, BS[it - 1]["dl"])
            if it - 2 in BS:
                nc.gpsimd.wait_ge(p.V.c.h, BS[it - 2]["final"])
            dld = p.gdma_start(dtin[:], ar_out[96 * s:96 * s + 64, :])
            bcd = p.gdma_start(bc_d[soff:soff + 32, :],
                               ar_out[96 * s + 64:96 * s + 96, :])
            with nc.psum_tensor(p.pre + f"ps_dl{it}", [128, 2 * CH], F32) as ps_dl:
                T.dep(dld)
                if it + 1 in AS:
                    T.dep((p.S.c, AS[it + 1]["evac"]))
                psdlv = ps_dl.ap().rearrange("p (r l) -> p r l", r=2)
                for dt_ in range(2):
                    for th in range(CH // 512):
                        tsl = slice(th * 512, (th + 1) * 512)
                        mm = T.matmul(psdlv[:, dt_, tsl],
                                      dtT[:, dt_ * 128:(dt_ + 1) * 128],
                                      dtin[:, tsl], start=True, stop=True)
                dl_done = T.tag(mm)
                for dt_ in range(2):
                    V.dep((p.T.c, dl_done))
                    V.tensor_scalar(deltv[:, dt_, :], psdlv[:, dt_, :],
                                    dtbc[:, dt_:dt_ + 1], None, op0=ALU.add)
                    V.scalar_tensor_tensor(dabs[:], deltv[:, dt_, :], -1.0,
                                           deltv[:, dt_, :],
                                           op0=ALU.mult, op1=ALU.max)
                    S.dep(p.V.c).activation(out=dabs[:], in_=dabs[:],
                                            func=AF.Exp, scale=-1.0)
                    S.activation(out=dabs[:], in_=dabs[:], func=AF.Ln,
                                 bias=onec[:, 0:1])
                    V.dep(p.S.c).scalar_tensor_tensor(
                        deltv[:, dt_, :], deltv[:, dt_, :], 0.0, dabs[:],
                        op0=ALU.max, op1=ALU.add)
                    V.tensor_mul(ubv[:, dt_, :], deltv[:, dt_, :],
                                 xsbv[s][:, dt_, :])
            delta_done = p.V.c.val

            p.wtok(nc.sync, bcd)
            if it >= 1:
                nc.sync.wait_ge(p.V.c.h, BS[it - 1]["final"])
            bld = {}
            for pair in range(2):
                bld[pair] = [
                    p.sdma(bbc[pair][:].unsqueeze(1),
                           bc_d[soff + pair:soff + pair + 1, :]
                           .partition_broadcast(128)),
                    p.sdma(cbc[pair][:].unsqueeze(1),
                           bc_d[soff + 16 + pair:soff + 17 + pair, :]
                           .partition_broadcast(128))]
            tts_done = {}
            yacc_tag = {}
            for n in range(DS):
                for dt_ in range(2):
                    pr = n * 2 + dt_
                    S.dep((p.V.c, delta_done))
                    if pr >= 2:
                        S.dep((p.V.c, tts_done[pr - 2]))
                    S.activation(out=dA_t[pr % 2][:], in_=deltv[:, dt_, :],
                                 func=AF.Exp,
                                 scale=negA[:, dt_ * DS + n:dt_ * DS + n + 1])
                    V.dep(*bld[n % 2])
                    V.tensor_mul(dBu_t[pr % 2][:], ubv[:, dt_, :],
                                 bbc[n % 2][:])
                    V.dep(p.S.c)
                    V.tensor_tensor_scan(
                        h_t[:], dA_t[pr % 2][:], dBu_t[pr % 2][:],
                        initial=(0.0 if first else hcar[:, pr:pr + 1]),
                        op0=ALU.mult, op1=ALU.add)
                    tts_done[pr] = p.V.c.val
                    V.tensor_copy(hcar[:, pr:pr + 1], h_t[:, CH - 1:CH])
                    if pr >= 2:
                        V.dep((p.G.c, yacc_tag[pr - 2]))
                    V.tensor_mul(yc_t[pr % 2][:], h_t[:], cbc[n % 2][:])
                    G.dep(p.V.c)
                    if n == 0:
                        G.tensor_copy(yav[:, dt_, :], yc_t[pr % 2][:])
                    else:
                        G.tensor_add(yav[:, dt_, :], yav[:, dt_, :],
                                     yc_t[pr % 2][:])
                    yacc_tag[pr] = p.G.c.val
                if n + 2 < DS:
                    nc.sync.wait_ge(p.V.c.h, p.V.c.val)
                    bld[n % 2] = [
                        p.sdma(bbc[n % 2][:].unsqueeze(1),
                               bc_d[soff + n + 2:soff + n + 3, :]
                               .partition_broadcast(128)),
                        p.sdma(cbc[n % 2][:].unsqueeze(1),
                               bc_d[soff + 18 + n:soff + 19 + n, :]
                               .partition_broadcast(128))]
            V.dep(p.G.c)
            if it >= 1:
                V.dep(*BS[it - 1]["yg"])
            for dt_ in range(2):
                V.scalar_tensor_tensor(yav[:, dt_, :], xsfv[s][:, dt_, :],
                                       dcol[:, dt_:dt_ + 1], yav[:, dt_, :],
                                       op0=ALU.mult, op1=ALU.add)
                V.tensor_mul(ygv[:, dt_, :], yav[:, dt_, :], silzv[s][:, dt_, :])
            final_V = p.V.c.val
            nc.sync.wait_ge(p.V.c.h, final_V)
            if yg_flat:
                ygst = [p.sdma(ygo[dt_ * 128:(dt_ + 1) * 128, g0:g0 + CH],
                               ygv[:, dt_, :]) for dt_ in range(2)]
            else:
                rsh = g0 // TOKC
                lc = g0 % TOKC
                ygst = [p.sdma(ygo[DIC * rsh + dt_ * 128:
                                   DIC * rsh + (dt_ + 1) * 128, lc:lc + CH],
                               ygv[:, dt_, :]) for dt_ in range(2)]
            if yag_cb is not None and it % 2 == 1:
                yag_cb(g0 // TOKC, BS[it - 1]["yg"] + ygst)
            BS[it] = {"dl": dl_done, "delta": delta_done, "final": final_V,
                      "arread": [dld, bcd], "yg": ygst}

        emit_A(0)
        for it in range(NIT):
            if it + 1 < NIT:
                emit_A(it + 1)
            emit_B(it)


def build_outproj():
    p = P("outproj").__enter__()
    nc = p.nc
    xi = nc.dram_tensor("x_fm", [DM, TOKC], F32, kind="ExternalInput").ap()
    yf = nc.dram_tensor("y_full", [DI, TOKC], BF16, kind="ExternalInput").ap()
    opw = nc.dram_tensor("opw", [DM, DI], F32, kind="ExternalInput").ap()
    xo = nc.dram_tensor("x_out", [DM, TOKC], F32, kind="ExternalOutput").ap()

    @p.blk.sync
    def _(sync):
        p.open_chains()
        emit_outproj(p, xi, yf, opw, xo)
        p.drain_all(nc.sync)
    nc_ = p.nc
    p.close()
    return nc_


def emit_outproj(p, xi, yf, opw, xo, dyn=None):
    nc = p.nc
    NDC = DI // 128
    if True:
        V, S, G, T = p.V, p.S, p.G, p.T
        idb = p.sb("idb", [128, 128], BF16)
        id_done = emit_identity(p, idb)
        owb = p.sb("owb", [128, FCH * DI], BF16)
        opT = p.sb("opT", [128, NDC * DM], BF16)
        owbr = owb[:].rearrange("p (c d) -> p c d", c=FCH)
        opwr = opw.rearrange("(c p) d -> p c d", p=128)
        lw = [p.gdma_start(owbr[:, c, :], opwr[:, c, :]) for c in range(FCH)]
        owv = owb.ap().rearrange("p (c d) -> p c d", c=FCH)
        opv = opT.ap().rearrange("p (c o) -> p c o", c=NDC)
        with nc.psum_tensor(p.pre + "ps_t", [128, 128], BF16) as ps_t:
            for oc in range(FCH):
                for dc in range(NDC):
                    T.dep(lw[oc], (p.V.c, id_done), p.V.c)
                    tv = T.tag(T.transpose(ps_t[:],
                                           owv[:, oc, dc * 128:(dc + 1) * 128],
                                           idb[:]))
                    V.dep((p.T.c, tv)).tensor_copy(
                        opv[:, dc, oc * 128:(oc + 1) * 128], ps_t[:])
        prep_done = p.V.c.val
        yb = p.sb("yb", [128, NDC * 512], BF16)
        ev = [p.sb(f"ev{i}", [128, 512], F32) for i in range(2)]
        xres = [p.sb(f"xres{i}", [128, 512], F32) for i in range(2)]
        ybv = yb.ap().rearrange("p (c l) -> p c l", c=NDC)
        evv, stv = {}, {}
        with ExitStack() as pes:
            pso = [pes.enter_context(nc.psum_tensor(p.pre + f"pso{oc}", [128, 512], F32))
                   for oc in range(FCH)]
            for tg in range(4):
                tsl = slice(tg * 512, (tg + 1) * 512)
                yld = []
                for dc in range(NDC):
                    if tg > 0 and dc == 0:
                        nc.sync.wait_ge(p.T.c.h, p.T.c.val)
                        nc.gpsimd.wait_ge(p.T.c.h, p.T.c.val)
                    if dyn is None:
                        yld.append(p.sdma(ybv[:, dc, :],
                                          yf[dc * 128:(dc + 1) * 128, tsl]))
                    else:
                        yld.append(p.gdma_start(
                            ybv[:, dc, :],
                            yf[dc * 128:(dc + 1) * 128,
                               tg * 512:][:, dyn["c512"]]))
                T.dep(*yld)
                T.dep((p.V.c, prep_done))
                if tg > 0:
                    T.dep((p.V.c, evv[(tg - 1, FCH - 1)]))
                for oc in range(FCH):
                    for dc in range(NDC):
                        mm = T.matmul(pso[oc][:], opv[:, dc, oc * 128:(oc + 1) * 128],
                                      ybv[:, dc, :],
                                      start=(dc == 0), stop=(dc == NDC - 1))
                    T.tag(mm)
                mm_base = p.T.c.val - FCH
                for oc in range(FCH):
                    j = tg * FCH + oc
                    pb = j % 2
                    if j >= 2:
                        nc.sync.wait_ge(p.V.c.h, evv[(tg, oc - 2) if oc >= 2
                                        else (tg - 1, FCH + oc - 2)])
                    rld = p.sdma(xres[pb][:], xi[oc * 128:(oc + 1) * 128, tsl])
                    V.dep((p.T.c, mm_base + oc + 1), rld)
                    if j >= 2:
                        V.dep(stv[j - 2])
                    V.tensor_add(ev[pb][:], pso[oc][:], xres[pb][:])
                    evv[(tg, oc)] = p.V.c.val
                    nc.sync.wait_ge(p.V.c.h, evv[(tg, oc)])
                    stv[j] = p.sdma(xo[oc * 128:(oc + 1) * 128, tsl], ev[pb][:])


def build_ffn1():
    p = P("ffn1").__enter__()
    nc = p.nc
    xi = nc.dram_tensor("x_fm", [DM, TOKC], F32, kind="ExternalInput").ap()
    pw = nc.dram_tensor("proj_w", [NE1, DM], F32, kind="ExternalInput").ap()
    anc = nc.dram_tensor("anchors", [NE1, NE1], F32, kind="ExternalInput").ap()
    tmp_in = nc.dram_tensor("temp", [1, 1], F32, kind="ExternalInput").ap()
    w1 = nc.dram_tensor("w1", [NE1, DM, DE1], F32, kind="ExternalInput").ap()
    w2 = nc.dram_tensor("w2", [NE1, DE1, DM], F32, kind="ExternalInput").ap()
    xo = nc.dram_tensor("x_out", [DM, TOKC], F32, kind="ExternalOutput").ap()
    lnb = nc.dram_tensor("lnb", [2, TOKC], F32).ap()
    grow_d = nc.dram_tensor("grow_d", [NE1, TOKC], BF16).ap()
    rbounce = nc.dram_tensor("rbounce", [1, 1], F32).ap()

    @p.blk.sync
    def _(sync):
        p.open_chains()
        p.constcol(EPS)
        emit_ffn1(p, xi, pw, anc, tmp_in, w1, w2, xo, lnb, grow_d, rbounce)
        p.drain_all(nc.sync)
    nc_ = p.nc
    p.close()
    return nc_


def emit_ffn1(p, xi, pw, anc, tmp_in, w1, w2, xo, lnb, grow_d, rbounce):
    nc = p.nc
    if True:
        V, S, G, T = p.V, p.S, p.G, p.T
        xln = [p.sb(f"xln{fc}", [128, TOKC], BF16) for fc in range(FCH)]
        ln_done = emit_ln(p, p.pre + "ln", xi, xln, lnb, scoped=True)
        nc.sync.wait_ge(p.V.c.h, ln_done)
        nc.gpsimd.wait_ge(p.V.c.h, ln_done)
        p.S.dep((p.V.c, ln_done))
        p.G.dep((p.V.c, ln_done))
        p.T.dep((p.V.c, ln_done))

        idb = p.sb("idb", [128, 128], BF16)
        idf = p.sb("idf", [128, 128], F32)
        emit_identity(p, idb)
        id_done = emit_identity(p, idf)
        pwb = p.sb("pwb", [NE1, DM], BF16)
        pwT = p.sb("pwT", [128, FCH * NE1], BF16)
        ancb = p.sb("ancb", [NE1, NE1], BF16)
        ancT = p.sb("ancT", [NE1, NE1], BF16)
        zq = p.sb("zq", [NE1, TOKC], BF16)
        lgf = p.sb("lgf", [NE1, TOKC], F32)
        lg = p.sb("lg", [128, NTB * NE1], F32)
        gate = p.sb("gate", [128, NTB * NE1], F32)
        grow = p.sb("grow", [NE1, TOKC], F32)
        rcol = p.sb("rcol", [1, 1], F32)
        r6 = p.sb("r6", [NE1, 1], F32)
        sc1 = p.sb("sc1", [128, 1], F32)
        sc2 = p.sb("sc2", [128, 1], F32)
        sc3 = p.sb("sc3", [128, 1], F32)
        sc4 = p.sb("sc4", [128, 1], F32)
        sc5 = p.sb("sc5", [128, 1], F32)
        e_t = p.sb("e_t", [128, NE1], F32)
        msk1 = p.sb("msk1", [128, NE1], F32)
        msk2 = p.sb("msk2", [128, NE1], F32)
        tmp6 = p.sb("tmp6", [128, NE1], F32)
        lv1 = p.gdma_start(pwb[:], pw[:])
        lv2 = p.gdma_start(ancb[:], anc[:])
        lv3 = p.sdma(rcol[:], tmp_in[:])
        V.dep(lv3).tensor_scalar_max(rcol[:], rcol[:], 0.1)
        V.reciprocal(rcol[:], rcol[:])
        V.tensor_scalar_mul(rcol[:], rcol[:], 0.5)
        nc.sync.wait_ge(p.V.c.h, p.V.c.val)
        rbv = p.sdma(rbounce[:], rcol[:])
        p.wtok(nc.sync, rbv)
        r6v = p.sdma(r6[:].unsqueeze(1), rbounce[0:1, :].partition_broadcast(NE1))
        with nc.psum_tensor(p.pre + "ps_pT", [128, NE1], BF16) as ps_pT, \
             nc.psum_tensor(p.pre + "ps_aT", [NE1, NE1], BF16) as ps_aT, \
             nc.psum_tensor(p.pre + "ps_z", [NE1, 512], F32) as ps_z, \
             nc.psum_tensor(p.pre + "ps_d", [NE1, 512], F32) as ps_d, \
             nc.psum_tensor(p.pre + "ps_lT", [128, NE1], F32) as ps_lT:
            for fc in range(FCH):
                T.dep(lv1, (p.V.c, id_done), p.V.c)
                tv = T.tag(T.transpose(ps_pT[:], pwb[:, fc * 128:(fc + 1) * 128],
                                       idb[0:NE1, 0:NE1]))
                V.dep((p.T.c, tv)).tensor_copy(pwT[:, fc * NE1:(fc + 1) * NE1],
                                               ps_pT[:])
            T.dep(lv2, p.V.c)
            tv = T.tag(T.transpose(ps_aT[:], ancb[:], idb[0:NE1, 0:NE1]))
            V.dep((p.T.c, tv)).tensor_copy(ancT[:], ps_aT[:])
            pwt_done = p.V.c.val
            for tc in range(4):
                sl = slice(tc * 512, (tc + 1) * 512)
                T.dep((p.V.c, max(pwt_done, ln_done)), p.S.c)
                for fc in range(FCH):
                    mm = T.matmul(ps_z[:], pwT[:, fc * NE1:(fc + 1) * NE1],
                                  xln[fc][:, sl], start=(fc == 0), stop=(fc == FCH - 1))
                tv = T.tag(mm)
                S.dep((p.T.c, tv)).activation(out=zq[:, sl], in_=ps_z[:],
                                              func=AF.Tanh, scale=1.0 / 0.3)
            for tc in range(4):
                sl = slice(tc * 512, (tc + 1) * 512)
                T.dep(p.S.c, p.V.c)
                mm = T.matmul(ps_d[:], ancT[:], zq[:, sl], start=True, stop=True)
                tv = T.tag(mm)
                V.dep((p.T.c, tv), r6v)
                V.tensor_scalar(lgf[:, sl], ps_d[:], -6.0, r6[:, 0:1],
                                op0=ALU.add, op1=ALU.mult)
            for tb in range(NTB):
                T.dep(p.V.c)
                tv = T.tag(T.transpose(ps_lT[:], lgf[:, tb * 128:(tb + 1) * 128],
                                       idf[0:NE1, 0:NE1]))
                V.dep((p.T.c, tv)).tensor_copy(lg[:, tb * NE1:(tb + 1) * NE1],
                                               ps_lT[:])
        with nc.psum_tensor(p.pre + "ps_gT", [NE1, 128], F32) as ps_gT:
            for tb in range(NTB):
                sl = slice(tb * NE1, (tb + 1) * NE1)
                V.tensor_reduce(sc1[:], lg[:, sl], axis=AX.X, op=ALU.max)
                V.tensor_scalar(msk1[:], lg[:, sl], sc1[:, 0:1], None,
                                op0=ALU.is_equal)
                V.scalar_tensor_tensor(tmp6[:], msk1[:], -1e30, lg[:, sl],
                                       op0=ALU.mult, op1=ALU.add)
                V.tensor_reduce(sc2[:], tmp6[:], axis=AX.X, op=ALU.max)
                V.tensor_scalar(msk2[:], lg[:, sl], sc2[:, 0:1], None,
                                op0=ALU.is_equal)
                V.tensor_scalar(tmp6[:], msk1[:], -1.0, 1.0, op0=ALU.mult, op1=ALU.add)
                V.tensor_mul(msk2[:], msk2[:], tmp6[:])
                V.tensor_reduce(sc5[:], msk1[:], axis=AX.X, op=ALU.add)
                V.tensor_scalar(sc5[:], sc5[:], 2.0, None, op0=ALU.is_ge)
                V.tensor_sub(sc3[:], sc1[:], sc2[:])
                S.dep(p.V.c).activation(out=sc3[:], in_=sc3[:], func=AF.Sigmoid)
                V.dep(p.S.c)
                V.tensor_scalar(sc4[:], sc3[:], -1.0, 1.0, op0=ALU.mult, op1=ALU.add)
                V.tensor_scalar(tmp6[:, 0:1], sc5[:], -1.0, 1.0,
                                op0=ALU.mult, op1=ALU.add)
                V.tensor_mul(sc4[:], sc4[:], tmp6[:, 0:1])
                V.tensor_mul(sc3[:], sc3[:], tmp6[:, 0:1])
                V.scalar_tensor_tensor(sc3[:], sc5[:], 0.5, sc3[:],
                                       op0=ALU.mult, op1=ALU.add)
                V.tensor_scalar_mul(sc3[:], sc3[:], 0.5)
                V.tensor_scalar_mul(sc4[:], sc4[:], 0.5)
                gsl = gate[:, sl]
                V.tensor_scalar(gsl, msk1[:], sc3[:, 0:1], None, op0=ALU.mult)
                V.scalar_tensor_tensor(gsl, msk2[:], sc4[:, 0:1], gsl,
                                       op0=ALU.mult, op1=ALU.add)
            for tb in range(NTB):
                T.dep(p.V.c)
                tv = T.tag(T.transpose(ps_gT[:], gate[:, tb * NE1:(tb + 1) * NE1],
                                       idf[:]))
                V.dep((p.T.c, tv)).tensor_copy(grow[:, tb * 128:(tb + 1) * 128],
                                               ps_gT[:])
        nc.gpsimd.wait_ge(p.V.c.h, p.V.c.val)
        gv = p.gdma_start(grow_d[:], grow[:])
        p.wtok(nc.sync, gv)

        w1b = [p.sb(f"w1b{e}", [128, FCH * DE1], BF16) for e in range(NE1)]
        w2b = [p.sb(f"w2b{e}", [128, 4 * DM], BF16) for e in range(NE1)]
        w1v, w2v, w1ld, w2ld = [], [], [], []
        for e in range(NE1):
            w1ld.append(p.gdma_start(
                w1b[e][:].rearrange("p (h o) -> p h o", h=FCH),
                w1[e].rearrange("(h p) o -> p h o", p=128)))
            w2ld.append(p.gdma_start(
                w2b[e][:].rearrange("p (c o) -> p c o", c=4),
                w2[e].rearrange("(c p) o -> p c o", p=128)))
            w1v.append(w1b[e].ap().rearrange("p (h o) -> p h o", h=FCH))
            w2v.append(w2b[e].ap().rearrange("p (c o) -> p c o", c=4))
        hs = [p.sb(f"hs{e}", [128, 4 * 512], BF16) for e in range(NE1)]
        heb = p.sb("heb", [128, 4 * 512], BF16)
        gbc = [p.sb(f"gbc{i}", [128, 512], BF16) for i in range(2)]
        ev = [p.sb(f"evo{i}", [128, 512], F32) for i in range(2)]
        xres = [p.sb(f"xres{i}", [128, 512], F32) for i in range(2)]
        evv, stv = {}, {}
        gmul_done, hs_done = {}, {}
        with ExitStack() as pes:
            ps_h = pes.enter_context(nc.psum_tensor(p.pre + "ps_h", [128, 2048], F32))
            phv = ps_h.ap().rearrange("p (c l) -> p c l", c=4)
            for tg in range(4):
                tsl = slice(tg * 512, (tg + 1) * 512)
                for e in range(NE1):
                    k = tg * NE1 + e
                    if k >= 2:
                        nc.sync.wait_ge(p.V.c.h, gmul_done[k - 2])
                    gld = p.sdma(gbc[k % 2][:].unsqueeze(1),
                                 grow_d[e:e + 1, tsl].partition_broadcast(128))
                    T.dep(w1ld[e], (p.V.c, ln_done), p.S.c, p.V.c)
                    for hc4 in range(4):
                        for fc in range(FCH):
                            mm = T.matmul(phv[:, hc4, :],
                                          w1v[e][:, fc, hc4 * 128:(hc4 + 1) * 128],
                                          xln[fc][:, tsl],
                                          start=(fc == 0), stop=(fc == FCH - 1))
                    tv = T.tag(mm)
                    S.dep((p.T.c, tv), p.V.c)
                    for hc4 in range(4):
                        S.activation(out=heb[:, hc4 * 512:(hc4 + 1) * 512],
                                     in_=phv[:, hc4, :], func=AF.Gelu_apprx_tanh)
                    V.dep(p.S.c, gld)
                    if tg >= 1:
                        V.dep((p.T.c, hs_done[tg - 1]))
                    for hc4 in range(4):
                        V.tensor_mul(hs[e][:, hc4 * 512:(hc4 + 1) * 512],
                                     heb[:, hc4 * 512:(hc4 + 1) * 512], gbc[k % 2][:])
                    gmul_done[k] = p.V.c.val
                hsv = p.V.c.val
                with ExitStack() as pes2:
                    pso = [pes2.enter_context(nc.psum_tensor(p.pre + f"pso{tg}_{oc}",
                                                             [128, 512], F32))
                           for oc in range(4)]
                    for wave in range(2):
                        T.dep((p.V.c, hsv), *w2ld)
                        if (tg, wave) != (0, 0):
                            T.dep(p.V.c)
                        for oc in range(wave * 4, wave * 4 + 4):
                            po = pso[oc - wave * 4]
                            for e in range(NE1):
                                for dc in range(4):
                                    mm = T.matmul(
                                        po[:],
                                        w2v[e][:, dc, oc * 128:(oc + 1) * 128],
                                        hs[e][:, dc * 512:(dc + 1) * 512],
                                        start=(e == 0 and dc == 0),
                                        stop=(e == NE1 - 1 and dc == 3))
                            T.tag(mm)
                        mmb = p.T.c.val - 4
                        for oc in range(wave * 4, wave * 4 + 4):
                            j = tg * FCH + oc
                            pb = j % 2
                            if j >= 2:
                                nc.sync.wait_ge(p.V.c.h, evv[j - 2])
                            rld = p.sdma(xres[pb][:], xi[oc * 128:(oc + 1) * 128, tsl])
                            V.dep((p.T.c, mmb + (oc - wave * 4) + 1), rld)
                            if j >= 2:
                                V.dep(stv[j - 2])
                            V.tensor_add(ev[pb][:], pso[oc - wave * 4][:], xres[pb][:])
                            evv[j] = p.V.c.val
                            nc.sync.wait_ge(p.V.c.h, evv[j])
                            stv[j] = p.sdma(xo[oc * 128:(oc + 1) * 128, tsl], ev[pb][:])
                hs_done[tg] = p.T.c.val


def build_qkv():
    p = P("qkv").__enter__()
    nc = p.nc
    xi = nc.dram_tensor("x_fm", [DM, TOKC], F32, kind="ExternalInput").ap()
    wq = nc.dram_tensor("wq", [DM, DM], F32, kind="ExternalInput").ap()
    wk = nc.dram_tensor("wk", [DM, DM], F32, kind="ExternalInput").ap()
    wv = nc.dram_tensor("wv", [DM, DM], F32, kind="ExternalInput").ap()
    bq = nc.dram_tensor("bq", [128, FCH], F32, kind="ExternalInput").ap()
    bk = nc.dram_tensor("bk", [128, FCH], F32, kind="ExternalInput").ap()
    bv = nc.dram_tensor("bv", [1, DM], F32, kind="ExternalInput").ap()
    q_d = nc.dram_tensor("q_d", [DM, TOKC], BF16, kind="ExternalOutput").ap()
    k_d = nc.dram_tensor("k_d", [DM, TOKC], BF16, kind="ExternalOutput").ap()
    v_d = nc.dram_tensor("v_d", [TOKC, DM], BF16, kind="ExternalOutput").ap()
    qn_d = nc.dram_tensor("qn_d", [NH, TOKC], F32, kind="ExternalOutput").ap()
    kn_d = nc.dram_tensor("kn_d", [NH, TOKC], F32, kind="ExternalOutput").ap()
    lnb = nc.dram_tensor("lnb", [2, TOKC], F32).ap()

    @p.blk.sync
    def _(sync):
        p.open_chains()
        p.constcol(EPS)
        emit_qkv(p, xi, wq, wk, wv, bq, bk, bv, q_d, k_d, v_d, qn_d, kn_d, lnb)
        p.drain_all(nc.sync)
    nc_ = p.nc
    p.close()
    return nc_


def emit_qkv(p, xi, wq, wk, wv, bq, bk, bv, q_d, k_d, v_d, qn_d, kn_d, lnb,
             norms=True):
    nc = p.nc
    if True:
        V, S, G, T = p.V, p.S, p.G, p.T
        xln = [p.sb(f"xln{fc}", [128, TOKC], BF16) for fc in range(FCH)]
        ln_done = emit_ln(p, p.pre + "ln", xi, xln, lnb, scoped=True)
        nc.sync.wait_ge(p.V.c.h, ln_done)
        nc.gpsimd.wait_ge(p.V.c.h, ln_done)
        p.S.dep((p.V.c, ln_done))
        p.G.dep((p.V.c, ln_done))
        p.T.dep((p.V.c, ln_done))

        wb = [p.sb(f"wb{i}", [128, FCH * DM], BF16) for i in range(2)]
        bcol = p.sb("bcol", [128, 2 * FCH], F32)
        bvr = p.sb("bvr", [1, DM], BF16)
        hsel = p.sb("hsel", [128, 2], BF16)
        onesb = p.sb("onesb", [1, 128], BF16)
        evt = [p.sb(f"evt{i}", [128, 512], BF16) for i in range(2)]
        sqs = p.sb("sqs", [128, 512], BF16)
        nrm = p.sb("nrm", [2, 512], F32)
        bql = p.sdma(bcol[:, 0:FCH], bq[:])
        bkl = p.sdma(bcol[:, FCH:2 * FCH], bk[:])
        bvl = p.gdma_start(bvr[:], bv[:])
        V.memset(hsel[:], 0.0)
        V.memset(hsel[0:64, 0:1], 1.0)
        V.memset(hsel[64:128, 1:2], 1.0)
        V.memset(onesb[:], 1.0)
        sel_done = p.V.c.val
        V.dep(bql, bkl)

        wb0r = wb[0][:].rearrange("p (c o) -> p c o", c=FCH)
        wb1r = wb[1][:].rearrange("p (c o) -> p c o", c=FCH)
        wqr = wq.rearrange("(c p) o -> p c o", p=128)
        wkr = wk.rearrange("(c p) o -> p c o", p=128)
        wql = [p.gdma_start(wb0r[:, c, :], wqr[:, c, :]) for c in range(FCH)]
        wkl = [p.gdma_start(wb1r[:, c, :], wkr[:, c, :]) for c in range(FCH)]
        wloads = [wql, wkl]
        wvs = [wb[0].ap().rearrange("p (c o) -> p c o", c=FCH),
               wb[1].ap().rearrange("p (c o) -> p c o", c=FCH)]
        evo, stq, nst = {}, {}, {}
        with nc.psum_tensor(p.pre + "ps_a", [128, 512], F32) as ps_a, \
             nc.psum_tensor(p.pre + "ps_b", [128, 512], F32) as ps_b, \
             nc.psum_tensor(p.pre + "ps_n", [2, 512], F32) as ps_n, \
             nc.psum_tensor(p.pre + "ps_v1", [128, 512], F32) as ps_v1, \
             nc.psum_tensor(p.pre + "ps_v2", [128, 512], F32) as ps_v2:
            psqk = [ps_a, ps_b]
            for which, (outd, nout, scl, bofs) in enumerate(
                    [(q_d, qn_d, 0.125, 0), (k_d, kn_d, 1.0, FCH)]):
                for oc in range(FCH):
                    for tg in range(4):
                        jj = which * 32 + oc * 4 + tg
                        tsl = slice(tg * 512, (tg + 1) * 512)
                        po = psqk[jj % 2]
                        if jj % 32 == 0:
                            T.dep(*wloads[which])
                        if jj >= 2:
                            T.dep((p.V.c, evo[jj - 2]))
                        for fc in range(FCH):
                            mm = T.matmul(po[:],
                                          wvs[which][:, fc, oc * 128:(oc + 1) * 128],
                                          xln[fc][:, tsl],
                                          start=(fc == 0), stop=(fc == FCH - 1))
                        tv = T.tag(mm)
                        pb = jj % 2
                        V.dep((p.T.c, tv))
                        if jj >= 2:
                            V.dep(stq[jj - 2])
                        V.tensor_scalar(evt[pb][:], po[:],
                                        bcol[:, bofs + oc:bofs + oc + 1], scl,
                                        op0=ALU.add, op1=ALU.mult)
                        evo[jj] = p.V.c.val
                        nc.sync.wait_ge(p.V.c.h, evo[jj])
                        stq[jj] = p.sdma(outd[oc * 128:(oc + 1) * 128, tsl],
                                         evt[pb][:])
                        if norms:
                            S.dep((p.V.c, evo[jj]))
                            if jj >= 1:
                                S.dep(nst[jj - 1])
                            S.activation(out=sqs[:], in_=evt[pb][:], func=AF.Square)
                            T.dep(p.S.c, (p.V.c, sel_done))
                            mm = T.matmul(ps_n[:], hsel[:], sqs[:],
                                          start=True, stop=True)
                            tv = T.tag(mm)
                            S.dep((p.T.c, tv))
                            S.activation(out=nrm[:], in_=ps_n[:], func=AF.Sqrt)
                            nc.sync.wait_ge(p.S.c.h, p.S.c.val)
                            nst[jj] = p.sdma(nout[2 * oc:2 * oc + 2, tsl], nrm[:])
            qk_done = p.T.c.val
            nc.gpsimd.wait_ge(p.T.c.h, qk_done)
            wvr = wv.rearrange("(c p) o -> p c o", p=128)
            wvl = [p.gdma_start(wb0r[:, c, :], wvr[:, c, :])
                   for c in range(FCH)]
            psv = [ps_v1, ps_v2]
            evv, stv = {}, {}
            for tb in range(NTB):
                for dh in range(2):
                    j = tb * 2 + dh
                    po = psv[j % 2]
                    dsl = slice(dh * 512, (dh + 1) * 512)
                    if j == 0:
                        T.dep(*wvl)
                    T.dep(bvl, (p.V.c, sel_done))
                    if j >= 2:
                        T.dep((p.V.c, evv[j - 2]))
                    for fc in range(FCH):
                        T.matmul(po[:], xln[fc][:, tb * 128:(tb + 1) * 128],
                                 wvs[0][:, fc, dsl], start=(fc == 0), stop=False)
                    mm = T.matmul(po[:], onesb[:], bvr[0:1, dsl],
                                  start=False, stop=True)
                    tv = T.tag(mm)
                    pb = j % 2
                    V.dep((p.T.c, tv))
                    if j >= 2:
                        V.dep(*stv[j - 2])
                    else:
                        V.dep(stq[62 + j])
                    V.tensor_copy(evt[pb][:], po[:])
                    evv[j] = p.V.c.val
                    nc.sync.wait_ge(p.V.c.h, evv[j])
                    stv[j] = [p.sdma(v_d[tb * 128:(tb + 1) * 128, dsl],
                                     evt[pb][:])]


def build_attn():
    p = P("attn").__enter__()
    nc = p.nc
    qh = nc.dram_tensor("qh", [128, TOK], BF16, kind="ExternalInput").ap()
    kh = nc.dram_tensor("kh", [128, TOK], BF16, kind="ExternalInput").ap()
    vh = nc.dram_tensor("vh", [TOK, 128], BF16, kind="ExternalInput").ap()
    qn2 = nc.dram_tensor("qn2", [2, TOK], F32, kind="ExternalInput").ap()
    kn2 = nc.dram_tensor("kn2", [2, TOK], F32, kind="ExternalInput").ap()
    al2 = nc.dram_tensor("al2", [2, L], F32, kind="ExternalInput").ap()
    oh_d = nc.dram_tensor("oh_d", [128, TOK], BF16, kind="ExternalOutput").ap()
    den_d = nc.dram_tensor("den_d", [2, TOK], F32, kind="ExternalOutput").ap()
    QC = 512
    NQC = L // QC
    asp_d = nc.dram_tensor("asp_d", [2, 4, L], BF16)
    qex_d = nc.dram_tensor("qex_d", [4, QC], BF16)

    @p.blk.sync
    def _(sync):
        p.open_chains()
        emit_attn(p, qh, kh, vh, qn2, kn2, al2, oh_d, den_d, asp_d, qex_d,
                  flat_io=True)
        p.drain_all(nc.sync)
    nc_ = p.nc
    p.close()
    return nc_


ATQC = 512
ATNQC = L // ATQC


def emit_attn(p, qh, kh, vh, qn2, kn2, al2, oh_d, den_d, asp_d, qex_d,
              flat_io=True, dyn=None):
    nc = p.nc
    QC = ATQC
    NQC = ATNQC
    NKC = L // 512
    V, S, G, T = p.V, p.S, p.G, p.T
    mi = p.sb("mi", [128, QC], F32)
    invm = [p.sb(f"invm{d}", [128, QC], BF16) for d in range(4)]
    G.iota(mi[:], pattern=[[-1, QC]], base=0, channel_multiplier=1,
           allow_small_or_imprecise_dtypes=True)
    V.dep(p.G.c)
    for d in range(4):
        V.tensor_scalar(invm[d][:], mi[:], float(-d * 128), None, op0=ALU.is_gt)
    masks_done = p.V.c.val
    ACW = 2048
    arow = p.sb("arow", [1, ACW], F32)
    hi_b = p.sb("hi_b", [1, ACW], BF16)
    hi_f = p.sb("hi_f", [1, ACW], F32)
    lo_b = p.sb("lo_b", [1, ACW], BF16)
    asp_all = []
    for hb in range(2):
        for pc in range(L // ACW):
            csl = slice(pc * ACW, (pc + 1) * ACW)
            av = p.sdma(arow[:], al2[hb:hb + 1, csl])
            V.dep(av, *asp_all[-3:])
            if hb == 0 and pc == 0:
                V.memset(lo_b[:], 1.0)
                nc.sync.wait_ge(p.V.c.h, p.V.c.val)
                for hb2 in range(2):
                    for pc2 in range(L // ACW):
                        asp_all.append(p.sdma(
                            asp_d[hb2, 0:1, pc2 * ACW:(pc2 + 1) * ACW],
                            lo_b[:]))
                V.dep(*asp_all)
            V.tensor_copy(hi_b[:], arow[:])
            V.tensor_copy(hi_f[:], hi_b[:])
            V.tensor_sub(arow[:], arow[:], hi_f[:])
            nc.sync.wait_ge(p.V.c.h, p.V.c.val)
            st1 = p.sdma(asp_d[hb, 1:2, csl], hi_b[:])
            V.dep(st1)
            V.tensor_copy(hi_b[:], arow[:])
            V.tensor_copy(hi_f[:], hi_b[:])
            V.tensor_sub(arow[:], arow[:], hi_f[:])
            V.tensor_copy(lo_b[:], arow[:])
            nc.sync.wait_ge(p.V.c.h, p.V.c.val)
            st2 = p.sdma(asp_d[hb, 2:3, csl], hi_b[:])
            st3 = p.sdma(asp_d[hb, 3:4, csl], lo_b[:])
            asp_all += [st1, st2, st3]

    kx = p.sb("kx", [68, L], BF16)
    qfull = p.sb("qfull", [64, L], BF16)
    vx = p.sb("vx", [128, 64 * 65], BF16)
    sqk = [p.sb(f"sqk{i}", [64, 512], BF16) for i in range(2)]
    knrow = p.sb("knrow", [1, L], BF16)
    qnrow = p.sb("qnrow", [1, L], BF16)
    qx2 = [p.sb(f"qx2_{i}", [68, QC], BF16) for i in range(2)]
    pt2 = [p.sb(f"pt2_{i}", [128, QC], BF16) for i in range(3)]
    smf2 = [p.sb(f"smf2_{i}", [128, QC], F32) for i in range(2)]
    ot = [p.sb(f"ot{i}", [64, QC], BF16) for i in range(2)]
    dent = [p.sb(f"dent{i}", [1, QC], F32) for i in range(2)]
    alr = [p.sb(f"alr{i}", [1, QC], F32) for i in range(2)]
    mrow = p.sb("mrow", [1, QC], F32)
    mx = p.sb("mx", [1, 1], F32)
    knr = p.sb("knr", [1, 1], F32)
    ones64 = p.sb("ones64", [64, 1], BF16)
    vxv = vx.ap().rearrange("p (j w) -> p j w", j=64)
    V.memset(ones64[:], 1.0)
    for j in range(2):
        V.memset(qx2[j][64:68, :], 1.0)
    init_done = p.V.c.val

    gnm = 0
    nm_mm = {}
    nm_sq = {}
    gkb = 0
    qk_free = {}
    pv_tag = {}
    ex_tag = {}
    smf_user = {}
    ot_st = {}
    al_user = {}
    psp_free = {}
    qx_free = {}
    prev_seq = None

    with nc.psum_tensor(p.pre + "ps_s1", [128, QC], F32) as ps_s1, \
         nc.psum_tensor(p.pre + "ps_s2", [128, QC], F32) as ps_s2, \
         nc.psum_tensor(p.pre + "ps_s3", [128, QC], F32) as ps_s3, \
         nc.psum_tensor(p.pre + "ps_p1", [65, QC], F32) as ps_p1, \
         nc.psum_tensor(p.pre + "ps_p2", [65, QC], F32) as ps_p2, \
         nc.psum_tensor(p.pre + "ps_n1", [1, 512], F32) as ps_n1, \
         nc.psum_tensor(p.pre + "ps_n2", [1, 512], F32) as ps_n2:
        pss = [ps_s1, ps_s2, ps_s3]
        psp = [ps_p1, ps_p2]
        psn = [ps_n1, ps_n2]
        for hb in range(2):
            for bi in range(B):
                seq0 = bi * L
                if prev_seq is not None:
                    nc.sync.wait_ge(p.T.c.h, prev_seq[0])
                    nc.sync.wait_ge(p.V.c.h, prev_seq[1])
                kl1 = p.sdma(kx[0:64, 0:L], kh[hb * 64:(hb + 1) * 64,
                                               seq0:seq0 + L])
                qlf = p.sdma(qfull[:, 0:L], qh[hb * 64:(hb + 1) * 64,
                                               seq0:seq0 + L])
                for r3 in asp_all:
                    p.wtok(nc.sync, r3)
                kl2 = p.sdma(kx[64:68, 0:L], asp_d[hb, :, :])
                vl = p.sdma(vxv[:, :, 0:64],
                            vh[seq0:seq0 + L, hb * 64:(hb + 1) * 64]
                            .rearrange("(j p) d -> p j d", p=128))
                V.dep(vl)
                if prev_seq is not None:
                    V.dep((p.T.c, prev_seq[0]))
                V.memset(vxv[:, :, 64:65], 1.0)
                vx_done = p.V.c.val
                V.dep(kl1, qlf, (p.V.c, init_done))
                if prev_seq is not None:
                    S.dep((p.V.c, prev_seq[1]))
                for idx in range(2 * NKC):
                    qside = idx >= NKC
                    tc = idx % NKC
                    tsl = slice(tc * 512, (tc + 1) * 512)
                    src = qfull[:, tsl] if qside else kx[0:64, tsl]
                    dst = (qnrow if qside else knrow)[:, tsl]
                    b2 = gnm % 2
                    if gnm >= 2:
                        V.dep((p.T.c, nm_mm[gnm - 2]))
                    V.tensor_mul(sqk[b2][:], src, src)
                    T.dep(p.V.c)
                    if gnm >= 2:
                        T.dep((p.S.c, nm_sq[gnm - 2]))
                    mm = T.matmul(psn[b2][:], ones64[:], sqk[b2][:],
                                  start=True, stop=True)
                    nm_mm[gnm] = T.tag(mm)
                    S.dep((p.T.c, nm_mm[gnm]))
                    S.activation(out=dst, in_=psn[b2][:], func=AF.Sqrt)
                    nm_sq[gnm] = p.S.c.val
                    gnm += 1
                norms_done = p.S.c.val
                V.memset(knr[:], 0.0)
                for qc in range(NQC):
                    j = qc % 2
                    qsl = slice(qc * QC, (qc + 1) * QC)
                    if j in al_user:
                        nc.sync.wait_ge(p.V.c.h, al_user[j])
                    alv = p.sdma(alr[j][:], al2[hb:hb + 1, qsl])
                    V.dep((p.S.c, norms_done))
                    V.tensor_reduce(mx[:], knrow[:, qsl], axis=AX.X, op=ALU.max)
                    V.tensor_tensor(knr[:], knr[:], mx[:], op=ALU.max)
                    V.tensor_scalar(mrow[:], qnrow[:, qsl], knr[0:1, 0:1],
                                    None, op0=ALU.mult)
                    V.dep(alv)
                    V.tensor_add(mrow[:], mrow[:], alr[j][:])
                    al_user[j] = p.V.c.val
                    if j in qx_free:
                        V.dep((p.T.c, qx_free[j]))
                    V.tensor_copy(qx2[j][0:64, :], qfull[:, qsl])
                    V.tensor_scalar_mul(qx2[j][64:65, :], mrow[:], -1.0)
                    qx_ready = p.V.c.val
                    jmax = (qc + 1) * 4
                    pvb = psp[j]

                    def emit_pv(g, kb):
                        T.dep((p.S.c, ex_tag[g]))
                        if kb == 0:
                            T.dep((p.V.c, vx_done))
                            if j in psp_free:
                                T.dep((p.V.c, psp_free[j]))
                        mm = T.matmul(pvb[:], vxv[:, kb, :], pt2[g % 3][:],
                                      start=(kb == 0), stop=(kb == jmax - 1))
                        pv_tag[g] = T.tag(mm)

                    for kb in range(jmax):
                        b3 = gkb % 3
                        sb_ = pss[b3]
                        if kb == 0:
                            T.dep((p.V.c, qx_ready))
                            if qc == 0:
                                T.dep(kl1, kl2)
                        if gkb - 3 in qk_free:
                            T.dep(qk_free[gkb - 3])
                        mm = T.matmul(sb_[:], kx[:, kb * 128:(kb + 1) * 128],
                                      qx2[j][:], start=True, stop=True)
                        tv_qk = T.tag(mm)
                        if kb > 0:
                            emit_pv(gkb - 1, kb - 1)
                        diag = kb >= qc * 4
                        if diag:
                            b2 = gkb % 2
                            V.dep((p.T.c, tv_qk), (p.V.c, masks_done))
                            if b2 in smf_user:
                                V.dep((p.S.c, smf_user[b2]))
                            V.scalar_tensor_tensor(smf2[b2][:],
                                                   invm[kb - qc * 4][:],
                                                   -30000.0, sb_[:],
                                                   op0=ALU.mult, op1=ALU.add)
                            expin = smf2[b2]
                            qk_free[gkb] = (p.V.c, p.V.c.val)
                            S.dep(p.V.c)
                        else:
                            expin = sb_
                            S.dep((p.T.c, tv_qk))
                        if gkb - 3 in pv_tag:
                            S.dep((p.T.c, pv_tag[gkb - 3]))
                        S.activation(out=pt2[gkb % 3][:], in_=expin[:],
                                     func=AF.Exp)
                        ex_tag[gkb] = p.S.c.val
                        if diag:
                            smf_user[gkb % 2] = ex_tag[gkb]
                        else:
                            qk_free[gkb] = (p.S.c, ex_tag[gkb])
                        gkb += 1
                    emit_pv(gkb - 1, jmax - 1)
                    qx_free[j] = p.T.c.val
                    V.dep((p.T.c, pv_tag[gkb - 1]))
                    if j in ot_st:
                        V.dep(*ot_st[j])
                    V.tensor_copy(ot[j][:], pvb[0:64, :])
                    V.tensor_copy(dent[j][:], pvb[64:65, :])
                    psp_free[j] = p.V.c.val
                    nc.sync.wait_ge(p.V.c.h, psp_free[j])
                    st1 = p.sdma(oh_d[hb * 64:(hb + 1) * 64,
                                      seq0 + qc * QC:seq0 + (qc + 1) * QC],
                                 ot[j][:])
                    st2 = p.sdma(den_d[hb:hb + 1,
                                       seq0 + qc * QC:seq0 + (qc + 1) * QC],
                                 dent[j][:])
                    ot_st[j] = (st1, st2)
                prev_seq = (p.T.c.val, p.V.c.val)


def build_wo():
    p = P("wo").__enter__()
    nc = p.nc
    xi = nc.dram_tensor("x_fm", [DM, TOKC], F32, kind="ExternalInput").ap()
    oc_in = nc.dram_tensor("ocat", [DM, TOKC], BF16, kind="ExternalInput").ap()
    dn_in = nc.dram_tensor("den_all", [NH, TOKC], F32, kind="ExternalInput").ap()
    wo = nc.dram_tensor("wo", [DM, DM], F32, kind="ExternalInput").ap()
    bo = nc.dram_tensor("bo", [128, FCH], F32, kind="ExternalInput").ap()
    xo = nc.dram_tensor("x_out", [DM, TOKC], F32, kind="ExternalOutput").ap()
    rb_d = nc.dram_tensor("rb_d", [NH, TOKC], F32)

    @p.blk.sync
    def _(sync):
        p.open_chains()
        emit_wo(p, xi, oc_in, dn_in, wo, bo, xo, rb_d)
        p.drain_all(nc.sync)
    nc_ = p.nc
    p.close()
    return nc_


def emit_wo(p, xi, oc_in, dn_in, wo, bo, xo, rb_d, dyn=None):
    nc = p.nc
    if True:
        V, S, G, T = p.V, p.S, p.G, p.T
        dn = p.sb("dn", [NH, TOKC], F32)
        wob = p.sb("wob", [128, FCH * DM], BF16)
        bocol = p.sb("bocol", [128, FCH], F32)
        rbc = p.sb("rbc", [128, TOKC], F32)
        on_t = [p.sb(f"on{fc}", [128, TOKC], BF16) for fc in range(FCH)]
        oct_ = [p.sb(f"oct{i}", [128, TOKC], BF16) for i in range(2)]
        ev = [p.sb(f"ev{i}", [128, 512], F32) for i in range(2)]
        xres = [p.sb(f"xres{i}", [128, 512], F32) for i in range(2)]
        if dyn is None:
            dl = p.sdma(dn[:], dn_in[:])
        else:
            dl = p.gdma_start(dn[:], dn_in[:, dyn["cT"]])
        wobr = wob[:].rearrange("p (c o) -> p c o", c=FCH)
        wor = wo.rearrange("(c p) o -> p c o", p=128)
        wl = [p.gdma_start(wobr[:, c, :], wor[:, c, :]) for c in range(FCH)]
        bl = p.sdma(bocol[:], bo[:])
        V.dep(dl).reciprocal(dn[:], dn[:])
        nc.sync.wait_ge(p.V.c.h, p.V.c.val)
        rbst = p.sdma(rb_d[:], dn[:])
        p.wtok(nc.sync, rbst)
        for fc in range(FCH):
            if fc >= 1:
                nc.sync.wait_ge(p.V.c.h, p.V.c.val)
                nc.gpsimd.wait_ge(p.V.c.h, p.V.c.val)
            if dyn is None:
                ol = p.sdma(oct_[fc % 2][:], oc_in[fc * 128:(fc + 1) * 128, :])
            else:
                ol = p.gdma_start(oct_[fc % 2][:],
                                  oc_in[fc * 128:(fc + 1) * 128, :][:, dyn["cT"]])
            b1 = p.sdma(rbc[0:64, :].unsqueeze(1),
                        rb_d[2 * fc:2 * fc + 1, :].partition_broadcast(64))
            b2 = p.sdma(rbc[64:128, :].unsqueeze(1),
                        rb_d[2 * fc + 1:2 * fc + 2, :].partition_broadcast(64))
            V.dep(ol, b1, b2)
            V.tensor_mul(on_t[fc][:], oct_[fc % 2][:], rbc[:])
        norm_done = p.V.c.val
        wv_ = wob.ap().rearrange("p (c o) -> p c o", c=FCH)
        evv, stv = {}, {}
        with ExitStack() as pes:
            pso = [pes.enter_context(nc.psum_tensor(p.pre + f"pso{i}", [128, 512], F32))
                   for i in range(FCH)]
            for tg in range(4):
                tsl = slice(tg * 512, (tg + 1) * 512)
                T.dep((p.V.c, norm_done), bl)
                if tg == 0:
                    T.dep(*wl)
                if tg > 0:
                    T.dep((p.V.c, evv[(tg - 1, FCH - 1)]))
                for oc in range(FCH):
                    for fc in range(FCH):
                        mm = T.matmul(pso[oc][:], wv_[:, fc, oc * 128:(oc + 1) * 128],
                                      on_t[fc][:, tsl], start=(fc == 0),
                                      stop=(fc == FCH - 1))
                    T.tag(mm)
                mmb = p.T.c.val - FCH
                for oc in range(FCH):
                    j = tg * FCH + oc
                    pb = j % 2
                    if j >= 2:
                        nc.sync.wait_ge(p.V.c.h, evv[(tg, oc - 2) if oc >= 2
                                        else (tg - 1, FCH + oc - 2)])
                    rld = p.sdma(xres[pb][:], xi[oc * 128:(oc + 1) * 128, tsl])
                    V.dep((p.T.c, mmb + oc + 1), rld, bl)
                    if j >= 2:
                        V.dep(stv[j - 2])
                    V.scalar_tensor_tensor(ev[pb][:], pso[oc][:],
                                           bocol[:, oc:oc + 1], xres[pb][:],
                                           op0=ALU.add, op1=ALU.add)
                    evv[(tg, oc)] = p.V.c.val
                    nc.sync.wait_ge(p.V.c.h, evv[(tg, oc)])
                    stv[j] = p.sdma(xo[oc * 128:(oc + 1) * 128, tsl], ev[pb][:])


def build_fused():
    p = P("fused").__enter__()
    nc = p.nc
    ei = lambda n, sh, dt: nc.dram_tensor(n, sh, dt, kind="ExternalInput").ap()
    it_ = lambda n, sh, dt: nc.dram_tensor(n, sh, dt).ap()
    sh_ = lambda n, sh, dt: nc.dram_tensor(n, sh, dt,
                                           addr_space="Shared").ap()
    x_rows = ei("x_rows", [TOKC, DM], F32)
    cid_in = ei("cid", [1, 1], mybir.dt.int32)
    rw = ei("router_w", [NE0, DM], F32)
    ew = ei("expert_w", [NE0, DM, DM], F32)
    wx = ei("wx", [DIC, DM], F32)
    wz = ei("wz", [DIC, DM], F32)
    cw = ei("conv_w", [128, 2 * DCONV], F32)
    cb = ei("conv_b", [128, 2], F32)
    xpw = ei("xproj", [96, DIC], F32)
    dtw = ei("dtproj", [DIC, DTR], F32)
    dtb = ei("dt_b", [128, 2], F32)
    alog = ei("a_log", [128, 2 * DS], F32)
    dvec = ei("d_vec", [128, 2], F32)
    opw = ei("opw", [DM, DI], F32)
    pw = ei("proj_w", [NE1, DM], F32)
    anc = ei("anchors", [NE1, NE1], F32)
    tmp_in = ei("temp", [1, 1], F32)
    w1 = ei("w1", [NE1, DM, DE1], F32)
    w2 = ei("w2", [NE1, DE1, DM], F32)
    wq = ei("wq", [DM, DM], F32)
    wk = ei("wk", [DM, DM], F32)
    wv = ei("wv", [DM, DM], F32)
    bq = ei("bq", [128, FCH], F32)
    bk = ei("bk", [128, FCH], F32)
    bv = ei("bv", [1, DM], F32)
    wow = ei("wo", [DM, DM], F32)
    bo = ei("bo", [128, FCH], F32)
    al2 = ei("al2", [2, L], F32)
    out_rows = nc.dram_tensor("out_rows", [TOKC, DM], F32,
                              kind="ExternalOutput").ap()
    xA = it_("xA", [DM, TOKC], F32)
    xB = it_("xB", [DM, TOKC], F32)
    lnb = it_("lnb", [2, TOKC], F32)
    xln_d = it_("xln_d", [DM, TOKC], BF16)
    ag_xln = sh_("ag_xln", [NCORE * DM, TOKC], BF16)
    f0_grow = it_("f0_grow", [NE0, TOKC], BF16)
    f0_brow = it_("f0_brow", [NTB, 128], BF16)
    yg_in = it_("yg_in", [NCORE * DIC, TOKC], BF16)
    yg_ag = sh_("yg_ag", [NCORE * DI, TOKC], BF16)
    y_loc = it_("y_loc", [DI, TOKC], BF16)
    f1_grow = it_("f1_grow", [NE1, TOKC], BF16)
    f1_rb = it_("f1_rb", [1, 1], F32)
    q_in = it_("q_in", [DM, TOKC], BF16)
    k_in = it_("k_in", [DM, TOKC], BF16)
    v_in = it_("v_in", [TOKC, DM], BF16)
    q_ag = sh_("q_ag", [NCORE * DM, TOKC], BF16)
    k_ag = sh_("k_ag", [NCORE * DM, TOKC], BF16)
    v_ag = sh_("v_ag", [TOK, DM], BF16)
    q_loc = it_("q_loc", [128, TOK], BF16)
    k_loc = it_("k_loc", [128, TOK], BF16)
    v_loc = it_("v_loc", [TOK, 128], BF16)
    oh_in = it_("oh_in", [128, TOK], BF16)
    den_in = it_("den_in", [2, TOK], F32)
    oh_ag = sh_("oh_ag", [DM, TOK], BF16)
    den_ag = sh_("den_ag", [NH, TOK], F32)
    oh_loc = it_("oh_loc", [DM, TOKC], BF16)
    den_loc = it_("den_loc", [NH, TOKC], F32)
    asp_d = it_("asp_d", [2, 4, L], BF16)
    qex_d = it_("qex_d", [4, ATQC], BF16)
    ar_in = nc.dram_tensor("ar_in", [192, CH], F32)
    ar_out = nc.dram_tensor("ar_out", [192, CH], F32)
    bc_d = it_("bc_d", [64, CH], BF16)
    worb = it_("worb", [NH, TOKC], F32)

    @p.blk.sync
    def _(sync):
        p.open_chains()
        p.constcol(EPS)
        p.constcol(1.0)
        engs = [nc.vector, nc.scalar, nc.gpsimd, nc.tensor, nc.sync]
        cidt = p.E(nc.sbuf_tensor("cidt", [1, 1], mybir.dt.int32))
        r_h = p.E(nc.gpsimd.register("r_h"))
        r_c = p.E(nc.gpsimd.register("r_c"))
        cl = p.gdma_start(cidt[:], cid_in[:])
        p.wtok(nc.gpsimd, cl)
        nc.gpsimd.reg_load(r_h, cidt[0:1, 0:1])
        nc.gpsimd.reg_load(r_c, cidt[0:1, 0:1])
        nc.gpsimd.reg_mul(r_h, r_h, 128)
        nc.gpsimd.reg_mul(r_c, r_c, TOKC)
        hofs = nc.gpsimd.snap(r_h)
        cofs = nc.gpsimd.snap(r_c)
        r_y = p.E(nc.gpsimd.register("r_y"))
        nc.gpsimd.reg_load(r_y, cidt[0:1, 0:1])
        nc.gpsimd.reg_mul(r_y, r_y, DI)
        yofs = nc.gpsimd.snap(r_y)
        ds_cT = bass.ds(cofs, TOKC)
        ds_h128 = bass.ds(hofs, 128)
        ds_y = bass.ds(yofs, DI)
        ds_qk = [bass.ds(hofs + DM * r, 128) for r in range(NCORE)]

        def barrier():
            for e in engs:
                for c in (p.V.c, p.S.c, p.G.c, p.T.c):
                    e.wait_ge(c.h, c.val)
                p.dmap.drain(e)
                p.gdmap.drain(e)

        def phase(name, fn):
            p.pre = name + "_"
            p.scope = ExitStack()
            fn()
            p.scope.close()
            p.scope = None
            barrier()

        def coll_async(name, ins, outs):
            c = p.ctr(name)
            nc.gpsimd.collective_compute(
                "AllGather", ALU.bypass, replica_groups=[list(range(NCORE))],
                ins=[t.opt() for t in ins], outs=[t.opt() for t in outs],
            ).then_inc(c.h, 1)
            c.val = 1
            return c

        def coll(name, ins, outs):
            c = coll_async(name, ins, outs)
            for e in engs:
                e.wait_ge(c.h, 1)

        def slab(pairs):
            toks = [p.gdma_start(dst, src) for dst, src in pairs]
            for e in engs:
                for t in toks:
                    p.wtok(e, t)

        phase("ing", lambda: emit_transpose(p, x_rows, xA, NTB, FCH))
        phase("f0a", lambda: emit_ffn0(p, xA, rw, ew, xB, lnb, f0_grow, f0_brow))
        phase("mln", lambda: emit_lnb(p, xB, xln_d, lnb))
        coll("cc_xln", [xln_d], [ag_xln])
        yag_sems = [p.ctr(f"yag{r}") for r in range(NCORE)]

        def yag_cb(rsh, toks):
            for t in toks:
                p.wtok(nc.gpsimd, t)
            nc.gpsimd.collective_compute(
                "AllGather", ALU.bypass, replica_groups=[list(range(NCORE))],
                ins=[yg_in[DIC * rsh:DIC * (rsh + 1), :].opt()],
                outs=[yg_ag[DI * rsh:DI * (rsh + 1), :].opt()],
            ).then_inc(yag_sems[rsh].h, 1)
            yag_sems[rsh].val = 1

        p.pre = "mmb_"
        phase("mmb", lambda: emit_mamba(p, ag_xln, wx, wz, cw, cb, xpw, dtw,
                                        dtb, alog, dvec, yg_in, ar_in, ar_out,
                                        bc_d, yg_flat=False, xlf_blocked=True,
                                        yag_cb=yag_cb))
        for e in engs:
            for cgs in yag_sems:
                e.wait_ge(cgs.h, 1)
        slab([(y_loc[:], yg_ag[0:, :][ds_y, :])])
        phase("opj", lambda: emit_outproj(p, xB, y_loc, opw, xA))
        phase("f0b", lambda: emit_ffn0(p, xA, rw, ew, xB, lnb, f0_grow, f0_brow))
        phase("f1a", lambda: emit_ffn1(p, xB, pw, anc, tmp_in, w1, w2, xA, lnb,
                                       f1_grow, f1_rb))
        phase("qkv", lambda: emit_qkv(p, xA, wq, wk, wv, bq, bk, bv, q_in,
                                      k_in, v_in, None, None, lnb, norms=False))
        cq = coll_async("cc_q", [q_in], [q_ag])
        ck = coll_async("cc_k", [k_in], [k_ag])
        cv = coll_async("cc_v", [v_in], [v_ag])
        toks = []
        nc.gpsimd.wait_ge(cq.h, 1)
        toks += [p.gdma_start(q_loc[:, r * TOKC:(r + 1) * TOKC],
                              q_ag[0:, :][ds_qk[r], :]) for r in range(NCORE)]
        nc.gpsimd.wait_ge(ck.h, 1)
        toks += [p.gdma_start(k_loc[:, r * TOKC:(r + 1) * TOKC],
                              k_ag[0:, :][ds_qk[r], :]) for r in range(NCORE)]
        nc.gpsimd.wait_ge(cv.h, 1)
        toks += [p.gdma_start(v_loc[r * 4096:(r + 1) * 4096, :],
                              v_ag[r * 4096:(r + 1) * 4096, 0:][:, ds_h128])
                 for r in range(4)]
        for e in engs:
            for t in toks:
                p.wtok(e, t)
        phase("att", lambda: emit_attn(p, q_loc, k_loc, v_loc, None, None,
                                       al2, oh_in, den_in, asp_d, qex_d,
                                       flat_io=True))
        coh = coll_async("cc_oh", [oh_in], [oh_ag])
        cdn = coll_async("cc_dn", [den_in], [den_ag])
        toks2 = []
        nc.gpsimd.wait_ge(coh.h, 1)
        toks2.append(p.gdma_start(oh_loc[:], oh_ag[:, :][:, ds_cT]))
        nc.gpsimd.wait_ge(cdn.h, 1)
        toks2.append(p.gdma_start(den_loc[:], den_ag[:, :][:, ds_cT]))
        for e in engs:
            for t in toks2:
                p.wtok(e, t)
        phase("wo", lambda: emit_wo(p, xA, oh_loc, den_loc, wow, bo, xB, worb))
        phase("f1b", lambda: emit_ffn1(p, xB, pw, anc, tmp_in, w1, w2, xA, lnb,
                                       f1_grow, f1_rb))
        phase("egr", lambda: emit_transpose(p, xA, out_rows, FCH, NTB))
    nc_ = p.nc
    p.close()
    return nc_


def fused_in_maps(inputs):
    f32 = lambda k: np.ascontiguousarray(np.asarray(inputs[k], np.float32))
    hs = f32("L0_attn_hidden_states").reshape(TOK, DM)
    ipw = f32("L0_attn_in_proj_w")
    cw = f32("L0_attn_conv_w")[:, 0, :]
    cbv = f32("L0_attn_conv_b")[:, None]
    xpw = f32("L0_attn_x_proj_w")
    dtw = f32("L0_attn_dt_proj_w")
    dtbv = f32("L0_attn_dt_proj_b")[:, None]
    alog = f32("L0_attn_A_log")
    dv = f32("L0_attn_D")[:, None]
    barr = lambda k: np.ascontiguousarray(f32(k).reshape(FCH, 128).T)
    slopes = _slopes16()
    pos = np.arange(L, dtype=np.float32)
    tempv = np.ascontiguousarray(
        np.asarray(inputs["L1_ffn_routing_temp"], np.float32).reshape(1, 1))
    maps = []
    for c in range(NCORE):
        sl = slice(c * DIC, (c + 1) * DIC)
        al2 = np.ascontiguousarray(np.stack(
            [-np.float32(slopes[2 * c + j]) * (L - 1 - pos) for j in range(2)], 0))
        maps.append({
            "x_rows": np.ascontiguousarray(hs[c * TOKC:(c + 1) * TOKC]),
            "cid": np.full((1, 1), c, np.int32),
            "router_w": f32("L0_ffn_router_w"),
            "expert_w": f32("L0_ffn_expert_weights"),
            "wx": np.ascontiguousarray(ipw[sl]),
            "wz": np.ascontiguousarray(ipw[DI + c * DIC:DI + (c + 1) * DIC]),
            "conv_w": np.ascontiguousarray(
                cw[sl].reshape(2, 128, DCONV).transpose(1, 0, 2).reshape(128, -1)),
            "conv_b": np.ascontiguousarray(cbv[sl].reshape(2, 128).T),
            "xproj": np.ascontiguousarray(xpw[:, sl]),
            "dtproj": np.ascontiguousarray(dtw[sl]),
            "dt_b": np.ascontiguousarray(dtbv[sl].reshape(2, 128).T),
            "a_log": np.ascontiguousarray(
                alog[sl].reshape(2, 128, DS).transpose(1, 0, 2).reshape(128, -1)),
            "d_vec": np.ascontiguousarray(dv[sl].reshape(2, 128).T),
            "opw": f32("L0_attn_out_proj_w"),
            "proj_w": f32("L1_ffn_proj_w"),
            "anchors": f32("L1_ffn_anchors"),
            "temp": tempv,
            "w1": f32("L1_ffn_w1"),
            "w2": f32("L1_ffn_w2"),
            "wq": f32("L1_attn_Wq"), "wk": f32("L1_attn_Wk"),
            "wv": f32("L1_attn_Wv"), "wo": f32("L1_attn_Wo"),
            "bq": barr("L1_attn_bq"), "bk": barr("L1_attn_bk"),
            "bv": np.ascontiguousarray(f32("L1_attn_bv")[None, :]),
            "bo": barr("L1_attn_bo"),
            "al2": al2,
        })
    return maps


def kernel_fused(**inputs):
    maps = fused_in_maps(inputs)
    res = _run("fused", build_fused, maps)
    out = np.concatenate([res[c]["out_rows"] for c in range(NCORE)], 0)
    return out.reshape(B, L, DM)


_PROGS = {}


def _run(name, builder, in_maps):
    if name not in _PROGS:
        _PROGS[name] = builder()
    res = run_bass_kernel_spmd(_PROGS[name], in_maps, core_ids=list(range(NCORE)))
    return res.results


def kernel(**inputs):
    return kernel_fused(**inputs)


def kernel_v0(**inputs):
    hs = np.ascontiguousarray(np.asarray(inputs["L0_attn_hidden_states"], np.float32))
    xrows = hs.reshape(TOK, DM)
    res = _run("ingress", build_ingress,
               [{"x_rows": np.ascontiguousarray(xrows[c * TOKC:(c + 1) * TOKC])}
                for c in range(NCORE)])
    x_fm = [res[c]["x_fm"] for c in range(NCORE)]

    rw = np.ascontiguousarray(np.asarray(inputs["L0_ffn_router_w"], np.float32))
    ew = np.ascontiguousarray(np.asarray(inputs["L0_ffn_expert_weights"], np.float32))
    res = _run("ffn0", build_ffn0,
               [{"x_fm": x_fm[c], "router_w": rw, "expert_w": ew}
                for c in range(NCORE)])
    x_fm = [res[c]["x_out"] for c in range(NCORE)]

    x_fm = mamba_sublayer(inputs, x_fm)

    res = _run("ffn0", build_ffn0,
               [{"x_fm": x_fm[c], "router_w": rw, "expert_w": ew}
                for c in range(NCORE)])
    x_fm = [res[c]["x_out"] for c in range(NCORE)]

    x_fm = ffn1_sublayer(inputs, x_fm)
    x_fm = attn_sublayer(inputs, x_fm)
    x_fm = ffn1_sublayer(inputs, x_fm)

    res = _run("egress", build_egress, [{"x_fm": x_fm[c]} for c in range(NCORE)])
    out = np.concatenate([res[c]["x_rows"] for c in range(NCORE)], 0)
    return out.reshape(B, L, DM)


def attn_sublayer(inputs, x_fm):
    f32 = lambda k: np.ascontiguousarray(np.asarray(inputs[k], np.float32))
    wq, wk, wv, wo = (f32("L1_attn_Wq"), f32("L1_attn_Wk"),
                      f32("L1_attn_Wv"), f32("L1_attn_Wo"))
    barr = lambda k: np.ascontiguousarray(f32(k).reshape(FCH, 128).T)
    brow = lambda k: np.ascontiguousarray(f32(k)[None, :])
    maps = [{"x_fm": x_fm[c], "wq": wq, "wk": wk, "wv": wv,
             "bq": barr("L1_attn_bq"), "bk": barr("L1_attn_bk"),
             "bv": brow("L1_attn_bv")} for c in range(NCORE)]
    res = _run("qkv", build_qkv, maps)
    qf = np.concatenate([res[c]["q_d"] for c in range(NCORE)], 1)
    kf = np.concatenate([res[c]["k_d"] for c in range(NCORE)], 1)
    vf = np.concatenate([res[c]["v_d"] for c in range(NCORE)], 0)
    qnf = np.concatenate([res[c]["qn_d"] for c in range(NCORE)], 1)
    knf = np.concatenate([res[c]["kn_d"] for c in range(NCORE)], 1)
    slopes = _slopes16()
    pos = np.arange(L, dtype=np.float32)
    maps = []
    for c in range(NCORE):
        rs = slice(128 * c, 128 * (c + 1))
        al2 = np.stack([-np.float32(slopes[2 * c + j]) * (L - 1 - pos)
                        for j in range(2)], 0)
        maps.append({"qh": np.ascontiguousarray(qf[rs]),
                     "kh": np.ascontiguousarray(kf[rs]),
                     "vh": np.ascontiguousarray(vf[:, rs]),
                     "qn2": np.ascontiguousarray(qnf[2 * c:2 * c + 2]),
                     "kn2": np.ascontiguousarray(knf[2 * c:2 * c + 2]),
                     "al2": np.ascontiguousarray(al2)})
    res = _run("attn", build_attn, maps)
    ohf = np.concatenate([res[c]["oh_d"] for c in range(NCORE)], 0)
    denf = np.concatenate([res[c]["den_d"] for c in range(NCORE)], 0)
    maps = [{"x_fm": x_fm[c],
             "ocat": np.ascontiguousarray(ohf[:, c * TOKC:(c + 1) * TOKC]),
             "den_all": np.ascontiguousarray(denf[:, c * TOKC:(c + 1) * TOKC]),
             "wo": wo, "bo": barr("L1_attn_bo")} for c in range(NCORE)]
    res = _run("wo", build_wo, maps)
    return [res[c]["x_out"] for c in range(NCORE)]


def ffn1_sublayer(inputs, x_fm):
    f32 = lambda k: np.ascontiguousarray(np.asarray(inputs[k], np.float32))
    pw = f32("L1_ffn_proj_w")
    anc = f32("L1_ffn_anchors")
    tempv = np.asarray(inputs["L1_ffn_routing_temp"], np.float32).reshape(1, 1)
    w1 = f32("L1_ffn_w1")
    w2 = f32("L1_ffn_w2")
    res = _run("ffn1", build_ffn1,
               [{"x_fm": x_fm[c], "proj_w": pw, "anchors": anc,
                 "temp": np.ascontiguousarray(tempv), "w1": w1, "w2": w2}
                for c in range(NCORE)])
    return [res[c]["x_out"] for c in range(NCORE)]


def mamba_sublayer(inputs, x_fm):
    f32 = lambda k: np.ascontiguousarray(np.asarray(inputs[k], np.float32))
    res = _run("lnb", build_lnb, [{"x_fm": x_fm[c]} for c in range(NCORE)])
    xln_full = np.concatenate([res[c]["xln_d"] for c in range(NCORE)], 1)
    xln_full = np.ascontiguousarray(xln_full)
    ipw = f32("L0_attn_in_proj_w")
    cw = f32("L0_attn_conv_w")[:, 0, :]
    cbv = f32("L0_attn_conv_b")[:, None]
    xpw = f32("L0_attn_x_proj_w")
    dtw = f32("L0_attn_dt_proj_w")
    dtbv = f32("L0_attn_dt_proj_b")[:, None]
    alog = f32("L0_attn_A_log")
    dv = f32("L0_attn_D")[:, None]
    opw = f32("L0_attn_out_proj_w")
    maps = []
    for c in range(NCORE):
        sl = slice(c * DIC, (c + 1) * DIC)
        maps.append({
            "xln_full": xln_full,
            "wx": np.ascontiguousarray(ipw[sl]),
            "wz": np.ascontiguousarray(ipw[DI + c * DIC:DI + (c + 1) * DIC]),
            "conv_w": np.ascontiguousarray(
                cw[sl].reshape(2, 128, DCONV).transpose(1, 0, 2).reshape(128, 2 * DCONV)),
            "conv_b": np.ascontiguousarray(cbv[sl].reshape(2, 128).T),
            "xproj": np.ascontiguousarray(xpw[:, sl]),
            "dtproj": np.ascontiguousarray(dtw[sl]),
            "dt_b": np.ascontiguousarray(dtbv[sl].reshape(2, 128).T),
            "a_log": np.ascontiguousarray(
                alog[sl].reshape(2, 128, DS).transpose(1, 0, 2).reshape(128, 2 * DS)),
            "d_vec": np.ascontiguousarray(dv[sl].reshape(2, 128).T),
        })
    res = _run("mamba", build_mamba, maps)
    y_full = np.concatenate([res[c]["y_g"] for c in range(NCORE)], 0)
    y_full = np.ascontiguousarray(y_full)
    res = _run("outproj", build_outproj,
               [{"x_fm": x_fm[c],
                 "y_full": np.ascontiguousarray(y_full[:, c * TOKC:(c + 1) * TOKC]),
                 "opw": opw} for c in range(NCORE)])
    return [res[c]["x_out"] for c in range(NCORE)]


# version 15
# speedup vs baseline: 1.0026x; 1.0026x over previous
import math
import numpy as np
import ml_dtypes
from contextlib import ExitStack

import concourse.bass as bass
import concourse.mybir as mybir
from concourse.bass_utils import run_bass_kernel_spmd

F32 = mybir.dt.float32
BF16 = mybir.dt.bfloat16
AF = mybir.ActivationFunctionType
ALU = mybir.AluOpType
AX = mybir.AxisListType
BF16NP = ml_dtypes.bfloat16

NCORE = 8
B, L, DM = 2, 8192, 1024
TOK = B * L
TOKC = TOK // NCORE
FCH = DM // 128
NTB = TOKC // 128
DI, DS, DCONV, DTR = 2048, 16, 4, 64
DIC = DI // NCORE
NH, HD = 16, 64
NE0 = 8
NE1, DE1 = 6, 512
EPS = 1e-5


def _slopes16():
    return [2.0 ** (-0.5 * (i + 1)) for i in range(16)]


class Ctr:
    def __init__(self, es, nc, name):
        self.h = es.enter_context(nc.semaphore(name))
        self.val = 0

    def inc(self, instr, n=1):
        instr.then_inc(self.h, n)
        self.val += n
        return self.val


class DmaToken(tuple):
    @property
    def h(self):
        return self[0].h

    @property
    def val(self):
        return self[1]


class DmaPool:

    def __init__(self, es, nc, engname, k, name):
        self.eng = getattr(nc, engname)
        self.slots = [Ctr(es, nc, f"{name}{i}") for i in range(k)]
        self.k = k
        self.i = 0

    def start(self, out, in_, **kw):
        c = self.slots[self.i % self.k]
        self.i += 1
        if c.val > 0:
            self.eng.wait_ge(c.h, c.val)
        c.inc(self.eng.dma_start(out=out, in_=in_, **kw), 16)
        return DmaToken((c, c.val))

    def drain(self, eng):
        for c in self.slots:
            if c.val > 0:
                eng.wait_ge(c.h, c.val)


class Chain:

    def __init__(self, es, nc, engname):
        self.eng = getattr(nc, engname)
        self.c = Ctr(es, nc, f"ch_{engname}")
        self.serial = engname != "tensor"

    def dep(self, *deps):
        for d in deps:
            if isinstance(d, tuple):
                self.eng.wait_ge(d[0].h, d[1])
            else:
                self.eng.wait_ge(d.h, d.val)
        return self

    def tag(self, instr, n=1):
        return self.c.inc(instr, n)

    def __getattr__(self, name):
        f = getattr(self.eng, name)
        if name.startswith(("wait", "reg", "nop", "dma", "collective")):
            return f

        def wrapped(*a, **k):
            if self.serial:
                self.eng.wait_ge(self.c.h, self.c.val)
            r = f(*a, **k)
            if self.serial:
                self.c.inc(r)
            return r
        return wrapped


class P:
    def __init__(self, name):
        self.nc = bass.Bass("TRN2", target_bir_lowering=False)
        self.es = ExitStack()
        self.name = name

    def __enter__(self):
        self.es.__enter__()
        self.E = self.es.enter_context
        self.blk = self.E(self.nc.Block())
        self.pre = ""
        self.scope = None
        return self

    def open_chains(self):
        nc, es = self.nc, self.es
        self.V = Chain(es, nc, "vector")
        self.S = Chain(es, nc, "scalar")
        self.G = Chain(es, nc, "gpsimd")
        self.T = Chain(es, nc, "tensor")
        self.dmap = DmaPool(es, nc, "sync", 12, "sd")
        self.gdmap = DmaPool(es, nc, "gpsimd", 8, "gd")

    def sdma(self, out, in_, **kw):
        return self.dmap.start(out, in_, **kw)

    def gdma_start(self, out, in_, **kw):
        return self.gdmap.start(out, in_, **kw)

    def wtok(self, eng, tok):
        eng.wait_ge(tok.h, tok.val)

    def drain_all(self, eng):
        self.dmap.drain(eng)
        self.gdmap.drain(eng)

    def close(self):
        self.es.__exit__(None, None, None)

    def sb(self, name, shape, dtype, es=None):
        if es is None:
            es = self.scope if getattr(self, "scope", None) is not None else None
        enter = self.E if es is None else es.enter_context
        return enter(self.nc.sbuf_tensor(self.pre + name, shape, dtype))

    def psum(self, name, shape, dtype):
        return self.nc.psum_tensor(self.pre + name, shape, dtype)

    def enter(self, cm):
        es = self.scope if getattr(self, "scope", None) is not None else self.es
        return es.enter_context(cm)

    def ctr(self, name):
        return Ctr(self.es, self.nc, self.pre + name)

    def constcol(self, val):
        if not hasattr(self, "_cc"):
            self._cc = {}
        if val not in self._cc:
            t = self.sb(f"cc_{len(self._cc)}", [128, 1], F32)
            self.V.memset(t[:], float(val))
            self._cc[val] = t
        return self._cc[val]


def emit_identity(p, tile):
    p.G.iota(tile[:], pattern=[[-1, 128]], base=0, channel_multiplier=1,
             allow_small_or_imprecise_dtypes=True)
    p.V.dep(p.G.c).tensor_scalar(tile[:], tile[:], 0.0, None, op0=ALU.is_equal)
    return p.V.c.val


def emit_ln(p, name, xi_dram, xln, lnb, scoped=False):
    nc = p.nc
    les = ExitStack() if scoped else None
    sb = lambda n, sh, dt: p.sb(n, sh, dt, es=les)
    NT = TOKC // 512
    ones_b = sb(f"{name}_ones", [128, 1], BF16)
    xb = sb(f"{name}_xb", [128, TOKC], BF16)
    xsq = sb(f"{name}_xsq", [128, TOKC], BF16)
    xf = [sb(f"{name}_xf{i}", [128, TOKC], F32) for i in range(2)]
    srow = sb(f"{name}_srow", [1, TOKC], F32)
    tmp_r = sb(f"{name}_tmpr", [1, TOKC], F32)
    inv_r = sb(f"{name}_invr", [1, TOKC], F32)
    negm_r = sb(f"{name}_negmr", [1, TOKC], F32)
    inv_bc = sb(f"{name}_invbc", [128, TOKC], F32)
    negm_bc = sb(f"{name}_negmbc", [128, TOKC], F32)
    tmp_f = sb(f"{name}_tmpf", [128, TOKC], F32)
    V, S, G, T = p.V, p.S, p.G, p.T

    with nc.psum_tensor(f"{name}_pss", [1, TOKC], F32) as pss, \
         nc.psum_tensor(f"{name}_pss2", [1, TOKC], F32) as pss2:
        V.memset(ones_b[:], 1.0)
        mm_done = []
        for fc in range(FCH):
            if fc > 0:
                nc.gpsimd.wait_ge(p.T.c.h, mm_done[fc - 1])
            ldb = p.gdma_start(xb[:], xi_dram[fc * 128:(fc + 1) * 128, :])
            S.dep(ldb)
            if fc > 0:
                S.dep((p.T.c, mm_done[fc - 1]))
            S.activation(out=xsq[:], in_=xb[:], func=AF.Square)
            T.dep(p.S.c, p.V.c)
            for tc in range(NT):
                sl = slice(tc * 512, (tc + 1) * 512)
                T.matmul(pss[:, sl], ones_b[:], xb[:, sl],
                         start=(fc == 0), stop=(fc == FCH - 1))
                mm = T.matmul(pss2[:, sl], ones_b[:], xsq[:, sl],
                              start=(fc == 0), stop=(fc == FCH - 1))
            mm_done.append(T.tag(mm))
        V.dep(p.T.c).tensor_scalar_mul(srow[:], pss[:], 1.0 / DM)
        V.tensor_scalar_mul(negm_r[:], srow[:], -1.0)
        V.tensor_mul(tmp_r[:], srow[:], srow[:])
        V.tensor_scalar_mul(tmp_r[:], tmp_r[:], -1.0)
        V.scalar_tensor_tensor(tmp_r[:], pss2[:], 1.0 / DM, tmp_r[:],
                               op0=ALU.mult, op1=ALU.add)
        epsc = p.constcol(EPS)
        S.dep(p.V.c).activation(out=tmp_r[:], in_=tmp_r[:], func=AF.Ln,
                                bias=epsc[0:1, 0:1])
        S.activation(out=inv_r[:], in_=tmp_r[:], func=AF.Exp, scale=-0.5)
        nc.sync.wait_ge(p.S.c.h, p.S.c.val)
        nc.sync.wait_ge(p.V.c.h, p.V.c.val)
        t1 = p.sdma(lnb[0:1, :], inv_r[:])
        t2 = p.sdma(lnb[1:2, :], negm_r[:])
        p.wtok(nc.sync, t1)
        p.wtok(nc.sync, t2)
        b1 = p.sdma(inv_bc[:].unsqueeze(1), lnb[0:1, :].partition_broadcast(128))
        b2 = p.sdma(negm_bc[:].unsqueeze(1), lnb[1:2, :].partition_broadcast(128))
        xldv = []
        for fc in range(min(2, FCH)):
            xldv.append(p.sdma(xf[fc % 2][:], xi_dram[fc * 128:(fc + 1) * 128, :]))
        for fc in range(FCH):
            V.dep(b1, b2, xldv[fc])
            V.tensor_add(tmp_f[:], xf[fc % 2][:], negm_bc[:])
            V.tensor_mul(xln[fc][:], tmp_f[:], inv_bc[:])
            if fc + 2 < FCH:
                nc.sync.wait_ge(p.V.c.h, p.V.c.val)
                xldv.append(p.sdma(xf[fc % 2][:],
                                   xi_dram[(fc + 2) * 128:(fc + 3) * 128, :]))
    if scoped:
        les.close()
    return p.V.c.val


def build_transpose(direction):
    p = P(direction).__enter__()
    nc = p.nc
    if direction == "ingress":
        src = nc.dram_tensor("x_rows", [TOKC, DM], F32, kind="ExternalInput").ap()
        dst = nc.dram_tensor("x_fm", [DM, TOKC], F32, kind="ExternalOutput").ap()
        NO, NI = NTB, FCH
    else:
        src = nc.dram_tensor("x_fm", [DM, TOKC], F32, kind="ExternalInput").ap()
        dst = nc.dram_tensor("x_rows", [TOKC, DM], F32, kind="ExternalOutput").ap()
        NO, NI = FCH, NTB

    @p.blk.sync
    def _(sync):
        p.open_chains()
        emit_transpose(p, src, dst, NO, NI)
        p.drain_all(nc.sync)
    nc_ = p.nc
    p.close()
    return nc_


def emit_transpose(p, src, dst, NO, NI):
    nc = p.nc
    if True:
        V, T = p.V, p.T
        rt = [p.sb(f"rt{i}", [128, 128 * NI], F32) for i in range(2)]
        idf = p.sb("idf", [128, 128], F32)
        ev = [p.sb(f"ev{i}", [128, 128], F32) for i in range(2)]
        ps = [p.enter(nc.psum_tensor(p.pre + f"ps{i}", [128, 128], F32))
              for i in range(2)]
        id_done = emit_identity(p, idf)
        ldv, stv, evv, tpv = {}, {}, {}, {}
        ldv[0] = p.sdma(rt[0][:], src[0:128, :])
        if NO > 1:
            ldv[1] = p.sdma(rt[1][:], src[128:256, :])
        for o in range(NO):
            for i in range(NI):
                k = o * NI + i
                pb = k % 2
                deps = [ldv[o], (p.V.c, id_done)]
                if k >= 2:
                    deps.append((p.V.c, evv[k - 2]))
                T.dep(*deps)
                tpv[k] = T.tag(T.transpose(ps[pb][:],
                                           rt[o % 2][:, i * 128:(i + 1) * 128], idf[:]))
                V.dep((p.T.c, tpv[k]))
                if k >= 2:
                    V.dep(stv[k - 2])
                V.tensor_copy(ev[pb][:], ps[pb][:])
                evv[k] = p.V.c.val
                nc.sync.wait_ge(p.V.c.h, evv[k])
                stv[k] = p.sdma(dst[i * 128:(i + 1) * 128, o * 128:(o + 1) * 128],
                                ev[pb][:])
            if o + 2 < NO:
                nc.sync.wait_ge(p.T.c.h, tpv[o * NI + NI - 1])
                ldv[o + 2] = p.sdma(rt[o % 2][:], src[(o + 2) * 128:(o + 3) * 128, :])


def build_ingress():
    return build_transpose("ingress")


def build_egress():
    return build_transpose("egress")


def build_ffn0():
    p = P("ffn0").__enter__()
    nc = p.nc
    xi = nc.dram_tensor("x_fm", [DM, TOKC], F32, kind="ExternalInput").ap()
    rw = nc.dram_tensor("router_w", [NE0, DM], F32, kind="ExternalInput").ap()
    ew = nc.dram_tensor("expert_w", [NE0, DM, DM], F32, kind="ExternalInput").ap()
    xo = nc.dram_tensor("x_out", [DM, TOKC], F32, kind="ExternalOutput").ap()
    lnb = nc.dram_tensor("lnb", [2, TOKC], F32).ap()
    grow_d = nc.dram_tensor("grow_d", [NE0, TOKC], BF16).ap()
    brow_d = nc.dram_tensor("brow_d", [NTB, 128], BF16).ap()

    @p.blk.sync
    def _(sync):
        p.open_chains()
        p.constcol(EPS)
        emit_ffn0(p, xi, rw, ew, xo, lnb, grow_d, brow_d)
        p.drain_all(nc.sync)
    nc_ = p.nc
    p.close()
    return nc_


def emit_ffn0(p, xi, rw, ew, xo, lnb, grow_d, brow_d):
    nc = p.nc
    if True:
        V, S, G, T = p.V, p.S, p.G, p.T
        xln = [p.sb(f"xln{fc}", [128, TOKC], BF16) for fc in range(FCH)]
        ln_done = emit_ln(p, p.pre + "ln", xi, xln, lnb)

        rwb = p.sb("rwb", [NE0, DM], BF16)
        rwT = p.sb("rwT", [128, FCH * NE0], BF16)
        idb = p.sb("idb", [128, 128], BF16)
        lg = p.sb("lg", [128, NTB * NE0], F32)
        gate = p.sb("gate", [128, NTB * NE0], F32)
        bias_c = p.sb("bias_c", [128, NTB], F32)
        grow = p.sb("grow", [NE0, TOKC], F32)
        brow = p.sb("brow", [NTB, 128], F32)
        idf = p.sb("idf", [128, 128], F32)
        sc1 = p.sb("sc1", [128, 1], F32)
        sc2 = p.sb("sc2", [128, 1], F32)
        sc3 = p.sb("sc3", [128, 1], F32)
        sc4 = p.sb("sc4", [128, 1], F32)
        e_t = p.sb("e_t", [128, NE0], F32)
        msk1 = p.sb("msk1", [128, NE0], F32)
        msk2 = p.sb("msk2", [128, NE0], F32)
        tmp8 = p.sb("tmp8", [128, NE0], F32)
        emit_identity(p, idb)
        id_done = emit_identity(p, idf)
        rwldv = p.gdma_start(rwb[:], rw[:])
        with nc.psum_tensor(p.pre + "ps_rT", [128, NE0], BF16) as ps_rT, \
             nc.psum_tensor(p.pre + "ps_lg", [128, NE0], F32) as ps_lg, \
             nc.psum_tensor(p.pre + "ps_gT", [NE0, 128], F32) as ps_gT, \
             nc.psum_tensor(p.pre + "ps_bT", [NTB, 128], F32) as ps_bT:
            for fc in range(FCH):
                T.dep(rwldv, (p.V.c, id_done), p.V.c)
                tv = T.tag(T.transpose(ps_rT[:], rwb[:, fc * 128:(fc + 1) * 128],
                                       idb[0:NE0, 0:NE0]))
                V.dep((p.T.c, tv)).tensor_copy(rwT[:, fc * NE0:(fc + 1) * NE0],
                                               ps_rT[:])
            for tb in range(NTB):
                T.dep(p.V.c)
                for fc in range(FCH):
                    mm = T.matmul(ps_lg[:], xln[fc][:, tb * 128:(tb + 1) * 128],
                                  rwT[:, fc * NE0:(fc + 1) * NE0],
                                  start=(fc == 0), stop=(fc == FCH - 1))
                tv = T.tag(mm)
                V.dep((p.T.c, tv)).tensor_copy(lg[:, tb * NE0:(tb + 1) * NE0], ps_lg[:])
            lg_done = p.V.c.val
            for tb in range(NTB):
                sl = slice(tb * NE0, (tb + 1) * NE0)
                V.tensor_reduce(sc1[:], lg[:, sl], axis=AX.X, op=ALU.max, negate=True)
                S.dep(p.V.c).activation(out=e_t[:], in_=lg[:, sl], func=AF.Exp,
                                        bias=sc1[:, 0:1])
                V.dep(p.S.c).tensor_reduce(sc2[:], e_t[:], axis=AX.X, op=ALU.add)
                V.reciprocal(sc2[:], sc2[:])
                V.tensor_scalar(e_t[:], e_t[:], sc2[:, 0:1], None, op0=ALU.mult)
                V.tensor_reduce(sc1[:], e_t[:], axis=AX.X, op=ALU.max)
                V.tensor_scalar(msk1[:], e_t[:], sc1[:, 0:1], None, op0=ALU.is_equal)
                V.scalar_tensor_tensor(tmp8[:], msk1[:], -1e30, e_t[:],
                                       op0=ALU.mult, op1=ALU.add)
                V.tensor_reduce(sc2[:], tmp8[:], axis=AX.X, op=ALU.max)
                V.tensor_scalar(msk2[:], e_t[:], sc2[:, 0:1], None, op0=ALU.is_equal)
                V.tensor_scalar(tmp8[:], msk1[:], -1.0, 1.0, op0=ALU.mult, op1=ALU.add)
                V.tensor_mul(msk2[:], msk2[:], tmp8[:])
                V.tensor_add(sc3[:], sc1[:], sc2[:])
                V.tensor_scalar_add(sc3[:], sc3[:], 1e-6)
                V.reciprocal(sc3[:], sc3[:])
                V.tensor_mul(sc4[:], sc1[:], sc3[:])
                V.tensor_scalar_mul(sc4[:], sc4[:], 0.5)
                V.tensor_mul(sc3[:], sc2[:], sc3[:])
                V.tensor_scalar_mul(sc3[:], sc3[:], 0.5)
                gsl = gate[:, sl]
                V.tensor_scalar(gsl, msk1[:], sc4[:, 0:1], None, op0=ALU.mult)
                V.scalar_tensor_tensor(gsl, msk2[:], sc3[:, 0:1], gsl,
                                       op0=ALU.mult, op1=ALU.add)
                V.tensor_mul(sc4[:], sc4[:], sc1[:])
                V.scalar_tensor_tensor(bias_c[:, tb:tb + 1], sc3[:], sc2[:, 0:1],
                                       sc4[:], op0=ALU.mult, op1=ALU.add)
            for tb in range(NTB):
                T.dep(p.V.c)
                tv = T.tag(T.transpose(ps_gT[:], gate[:, tb * NE0:(tb + 1) * NE0],
                                       idf[:]))
                V.dep((p.T.c, tv)).tensor_copy(grow[:, tb * 128:(tb + 1) * 128],
                                               ps_gT[:])
            T.dep(p.V.c)
            tv = T.tag(T.transpose(ps_bT[:], bias_c[:], idf[:]))
            V.dep((p.T.c, tv)).tensor_copy(brow[:], ps_bT[:])
        nc.gpsimd.wait_ge(p.V.c.h, p.V.c.val)
        gv1 = p.gdma_start(grow_d[:], grow[:])
        gv = p.gdma_start(brow_d[:], brow[:])
        gbc = [p.sb(f"gbc{i}", [128, 512], BF16) for i in range(2)]
        bias_row = p.sb("bias_row", [1, TOKC], BF16)
        onesb = p.sb("onesb", [1, 128], BF16)
        p.wtok(nc.sync, gv1)
        p.wtok(nc.sync, gv)
        biasldv = p.sdma(bias_row[:], brow_d.rearrange("a b -> (a b)").unsqueeze(0))
        V.memset(onesb[:], 1.0)
        ones_done = p.V.c.val

        ewb = [p.sb(f"ewb{i}", [128, FCH * DM], BF16) for i in range(2)]
        xe = [p.sb(f"xe{i}", [128, FCH * 512], BF16) for i in range(2)]
        ev = [p.sb(f"evo{i}", [128, 512], F32) for i in range(2)]
        xres = [p.sb(f"xres{i}", [128, 512], F32) for i in range(2)]
        s_mm = p.ctr("s_mm")
        with ExitStack() as pes:
            pso = [pes.enter_context(nc.psum_tensor(p.pre + f"ps_o{oc}", [128, 512], F32))
                   for oc in range(FCH)]
            wldv, xev = {}, {}
            evv2, stv2 = {}, {}
            for tg in range(4):
                tsl = slice(tg * 512, (tg + 1) * 512)
                for e in range(NE0):
                    k = tg * NE0 + e
                    if k >= 2:
                        nc.gpsimd.wait_ge(s_mm.h, (k - 1) * FCH)
                        nc.sync.wait_ge(s_mm.h, (k - 1) * FCH)
                    ewr = ew[e].rearrange("(h p) o -> p h o", p=128)
                    ewbr = ewb[k % 2][:].rearrange("p (h o) -> p h o", h=FCH)
                    wldv[k] = [p.gdma_start(ewbr[:, hc, :], ewr[:, hc, :])
                               for hc in range(FCH)]
                    gld = p.sdma(gbc[k % 2][:].unsqueeze(1),
                                 grow_d[e:e + 1, tsl].partition_broadcast(128))
                    V.dep(gld)
                    if k >= 2:
                        V.dep((s_mm, (k - 1) * FCH))
                    for fc in range(FCH):
                        V.tensor_mul(xe[k % 2][:, fc * 512:(fc + 1) * 512],
                                     xln[fc][:, tsl], gbc[k % 2][:])
                    xev[k] = p.V.c.val
                    T.dep((p.V.c, xev[k]), *wldv[k])
                    if e == 0:
                        T.dep((p.V.c, ones_done), biasldv)
                        if tg > 0:
                            T.dep((p.V.c, evv2[(tg - 1, FCH - 1)]))
                    ewv = ewb[k % 2].ap().rearrange("p (h o) -> p h o", h=FCH)
                    for oc in range(FCH):
                        for hc in range(FCH):
                            mm = T.matmul(pso[oc][:],
                                          ewv[:, hc, oc * 128:(oc + 1) * 128],
                                          xe[k % 2][:, hc * 512:(hc + 1) * 512],
                                          start=(e == 0 and hc == 0), stop=False)
                        if e == NE0 - 1:
                            mm = T.matmul(pso[oc][:], onesb[:], bias_row[:, tsl],
                                          start=False, stop=True)
                        s_mm.inc(mm)
                for oc in range(FCH):
                    j = tg * FCH + oc
                    pb = j % 2
                    if j >= 2:
                        nc.sync.wait_ge(p.V.c.h, evv2[(tg, oc - 2) if oc >= 2
                                        else (tg - 1, FCH + oc - 2)])
                    rld = p.sdma(xres[pb][:], xi[oc * 128:(oc + 1) * 128, tsl])
                    V.dep((s_mm, (tg * NE0 + NE0 - 1) * FCH + oc + 1),
                          rld)
                    if j >= 2:
                        V.dep(stv2[j - 2])
                    V.tensor_add(ev[pb][:], pso[oc][:], xres[pb][:])
                    evv2[(tg, oc)] = p.V.c.val
                    nc.sync.wait_ge(p.V.c.h, evv2[(tg, oc)])
                    stv2[j] = p.sdma(xo[oc * 128:(oc + 1) * 128, tsl], ev[pb][:])


def build_lnb():
    p = P("lnb").__enter__()
    nc = p.nc
    xi = nc.dram_tensor("x_fm", [DM, TOKC], F32, kind="ExternalInput").ap()
    xlo = nc.dram_tensor("xln_d", [DM, TOKC], BF16, kind="ExternalOutput").ap()
    lnb = nc.dram_tensor("lnb", [2, TOKC], F32).ap()

    @p.blk.sync
    def _(sync):
        p.open_chains()
        p.constcol(EPS)
        emit_lnb(p, xi, xlo, lnb)
        p.drain_all(nc.sync)
    nc_ = p.nc
    p.close()
    return nc_


def emit_lnb(p, xi, xlo, lnb):
    nc = p.nc
    xln = [p.sb(f"xln{fc}", [128, TOKC], BF16) for fc in range(FCH)]
    emit_ln(p, p.pre + "ln", xi, xln, lnb)
    nc.sync.wait_ge(p.V.c.h, p.V.c.val)
    for fc in range(FCH):
        p.sdma(xlo[fc * 128:(fc + 1) * 128, :], xln[fc][:])


CH = 1024
NCH = L // CH


def build_mamba():
    p = P("mamba").__enter__()
    nc = p.nc
    xlf = nc.dram_tensor("xln_full", [DM, TOK], BF16, kind="ExternalInput").ap()
    wx = nc.dram_tensor("wx", [DIC, DM], F32, kind="ExternalInput").ap()
    wz = nc.dram_tensor("wz", [DIC, DM], F32, kind="ExternalInput").ap()
    cw = nc.dram_tensor("conv_w", [128, 2 * DCONV], F32, kind="ExternalInput").ap()
    cb = nc.dram_tensor("conv_b", [128, 2], F32, kind="ExternalInput").ap()
    xpw = nc.dram_tensor("xproj", [96, DIC], F32, kind="ExternalInput").ap()
    dtw = nc.dram_tensor("dtproj", [DIC, DTR], F32, kind="ExternalInput").ap()
    dtb = nc.dram_tensor("dt_b", [128, 2], F32, kind="ExternalInput").ap()
    alog = nc.dram_tensor("a_log", [128, 2 * DS], F32, kind="ExternalInput").ap()
    dvec = nc.dram_tensor("d_vec", [128, 2], F32, kind="ExternalInput").ap()
    ygo = nc.dram_tensor("y_g", [DIC, TOK], BF16, kind="ExternalOutput").ap()
    ar_in = nc.dram_tensor("ar_in", [192, CH], F32)
    ar_out = nc.dram_tensor("ar_out", [192, CH], F32)
    bc_d = nc.dram_tensor("bc_d", [64, CH], BF16)

    @p.blk.sync
    def _(sync):
        p.open_chains()
        p.constcol(1.0)
        emit_mamba(p, xlf, wx, wz, cw, cb, xpw, dtw, dtb, alog, dvec, ygo,
                   ar_in, ar_out, bc_d, yg_flat=True)
        p.drain_all(nc.sync)
    nc_ = p.nc
    p.close()
    return nc_


def emit_mamba(p, xlf, wx, wz, cw, cb, xpw, dtw, dtb, alog, dvec, ygo,
               ar_in, ar_out, bc_d, yg_flat=True, xlf_blocked=False,
               yag_cb=None):
    nc = p.nc
    if True:
        V, S, G, T = p.V, p.S, p.G, p.T
        idb = p.sb("idb", [128, 128], BF16)
        id_done = emit_identity(p, idb)
        wxb = p.sb("wxb", [128, 2 * DM], BF16)
        wzb = p.sb("wzb", [128, 2 * DM], BF16)
        xpb = p.sb("xpb", [96, DIC], BF16)
        dtwb = p.sb("dtwb", [128, 2 * DTR], BF16)
        wxT = p.sb("wxT", [128, FCH * DIC], BF16)
        wzT = p.sb("wzT", [128, FCH * DIC], BF16)
        xpT = p.sb("xpT", [128, 2 * 96], BF16)
        dtT = p.sb("dtT", [64, DIC], BF16)
        negA = p.sb("negA", [128, 2 * DS], F32)
        convw = p.sb("convw", [128, 2 * DCONV], F32)
        convb = p.sb("convb", [128, 2], F32)
        dtbc = p.sb("dtbc", [128, 2], F32)
        dcol = p.sb("dcol", [128, 2], F32)
        onec = p.constcol(1.0)
        l1 = p.gdma_start(wxb[:].rearrange("p (r f) -> p r f", r=2),
                          wx.rearrange("(r p) f -> p r f", p=128))
        l2 = p.gdma_start(wzb[:].rearrange("p (r f) -> p r f", r=2),
                          wz.rearrange("(r p) f -> p r f", p=128))
        l3 = p.gdma_start(xpb[:], xpw[:])
        l4 = p.gdma_start(dtwb[:].rearrange("p (r k) -> p r k", r=2),
                          dtw.rearrange("(r p) k -> p r k", p=128))
        l5 = p.sdma(convw[:], cw[:])
        l6 = p.sdma(convb[:], cb[:])
        l7 = p.sdma(dtbc[:], dtb[:])
        l8 = p.sdma(dcol[:], dvec[:])
        l9 = p.sdma(negA[:], alog[:])
        S.dep(l9).activation(out=negA[:], in_=negA[:], func=AF.Exp)
        V.dep(p.S.c).tensor_scalar_mul(negA[:], negA[:], -1.0)
        with nc.psum_tensor(p.pre + "ps_t", [128, 128], BF16) as ps_t, \
             nc.psum_tensor(p.pre + "ps_t2", [128, 96], BF16) as ps_t2, \
             nc.psum_tensor(p.pre + "ps_t3", [64, 128], BF16) as ps_t3:
            wxv = wxT.ap().rearrange("p (f r) -> p f r", f=FCH)
            wzv = wzT.ap().rearrange("p (f r) -> p f r", f=FCH)
            for (src, dstv, lv) in ((wxb, wxv, l1), (wzb, wzv, l2)):
                sv = src.ap().rearrange("p (r f) -> p r f", r=2)
                for rc in range(2):
                    for fc in range(FCH):
                        T.dep(lv, (p.V.c, id_done), p.V.c)
                        tv = T.tag(T.transpose(
                            ps_t[:], sv[:, rc, fc * 128:(fc + 1) * 128], idb[:]))
                        V.dep((p.T.c, tv)).tensor_copy(
                            dstv[:, fc, rc * 128:(rc + 1) * 128], ps_t[:])
            xpv = xpT.ap().rearrange("p (c k) -> p c k", c=2)
            for dc in range(2):
                T.dep(l3, (p.V.c, id_done), p.V.c)
                tv = T.tag(T.transpose(ps_t2[:], xpb[:, dc * 128:(dc + 1) * 128],
                                       idb[0:96, 0:96]))
                V.dep((p.T.c, tv)).tensor_copy(xpv[:, dc, :], ps_t2[:])
            dv = dtwb.ap().rearrange("p (r k) -> p r k", r=2)
            for rc in range(2):
                T.dep(l4, (p.V.c, id_done), p.V.c)
                tv = T.tag(T.transpose(ps_t3[:], dv[:, rc, :], idb[0:128, 0:128]))
                V.dep((p.T.c, tv)).tensor_copy(dtT[:, rc * 128:(rc + 1) * 128],
                                               ps_t3[:])
        prep_done_V = p.V.c.val
        p.V.dep(l5, l7, l8)
        p.S.dep(l6)

        xin = p.sb("xin", [128, FCH * CH], BF16)
        xraw = p.sb("xraw", [128, 2 * (DCONV - 1 + CH)], F32)
        xs_f = [p.sb(f"xs_f{s}", [128, 2 * CH], F32) for s in range(2)]
        xs_b = [p.sb(f"xs_b{s}", [128, 2 * CH], BF16) for s in range(2)]
        silz = [p.sb(f"silz{s}", [128, 2 * CH], F32) for s in range(2)]
        delt = p.sb("delt", [128, 2 * CH], F32)
        dabs = p.sb("dabs", [128, CH], F32)
        u_b = p.sb("u_b", [128, 2 * CH], BF16)
        xd_sb = p.sb("xd_sb", [96, CH], F32)
        dtin = p.sb("dtin", [64, CH], BF16)
        bbc = [p.sb(f"bbc{i}", [128, CH], BF16) for i in range(2)]
        cbc = [p.sb(f"cbc{i}", [128, CH], BF16) for i in range(2)]
        dA_t = [p.sb(f"dA{i}", [128, CH], BF16) for i in range(2)]
        dBu_t = [p.sb(f"dBu{i}", [128, CH], BF16) for i in range(2)]
        h_t = p.sb("h_t", [128, CH], BF16)
        yc_t = [p.sb(f"yc_t{i}", [128, CH], BF16) for i in range(2)]
        yacc = p.sb("yacc", [128, 2 * CH], F32)
        hcar = p.sb("hcar", [128, 32], BF16)
        ygt = p.sb("ygt", [128, 2 * CH], BF16)
        xinv = xin.ap().rearrange("p (f l) -> p f l", f=FCH)
        xrawv = xraw.ap().rearrange("p (r l) -> p r l", r=2)
        xsfv = [t.ap().rearrange("p (r l) -> p r l", r=2) for t in xs_f]
        xsbv = [t.ap().rearrange("p (r l) -> p r l", r=2) for t in xs_b]
        silzv = [t.ap().rearrange("p (r l) -> p r l", r=2) for t in silz]
        deltv = delt.ap().rearrange("p (r l) -> p r l", r=2)
        ubv = u_b.ap().rearrange("p (r l) -> p r l", r=2)
        yav = yacc.ap().rearrange("p (r l) -> p r l", r=2)
        ygv = ygt.ap().rearrange("p (r l) -> p r l", r=2)
        negAv = negA.ap().rearrange("p (r n) -> p r n", r=2)
        convwv = convw.ap().rearrange("p (r k) -> p r k", r=2)
        NIT = B * NCH
        cc_sems = [p.ctr(f"cc{i}") for i in range(NIT)]
        AS = {}
        BS = {}

        def emit_A(it):
            bi, chk = divmod(it, NCH)
            g0 = bi * L + chk * CH
            s = it % 2
            first = chk == 0
            if it > 0:
                nc.sync.wait_ge(p.T.c.h, AS[it - 1]["xz"])
            if xlf_blocked:
                xr0 = DM * (g0 // TOKC)
                xc0 = g0 % TOKC
                xld = [p.sdma(xinv[:, fc, :],
                              xlf[xr0 + fc * 128:xr0 + (fc + 1) * 128,
                                  xc0:xc0 + CH])
                       for fc in range(FCH)]
            else:
                xld = [p.sdma(xinv[:, fc, :],
                              xlf[fc * 128:(fc + 1) * 128, g0:g0 + CH])
                       for fc in range(FCH)]
            if first:
                if it > 0:
                    V.dep((p.G.c, AS[it - 1]["hist"]))
                V.memset(xraw[:], 0.0)
            hist_ready = p.V.c.val
            with nc.psum_tensor(p.pre + f"ps_x{it}", [128, 2 * CH], F32) as ps_x, \
                 nc.psum_tensor(p.pre + f"ps_z{it}", [128, 2 * CH], F32) as ps_z:
                psxv = ps_x.ap().rearrange("p (r l) -> p r l", r=2)
                pszv = ps_z.ap().rearrange("p (r l) -> p r l", r=2)
                T.dep(*xld)
                T.dep((p.V.c, max(prep_done_V, hist_ready)))
                if it >= 1:
                    T.dep((p.S.c, AS[it - 1]["evac"]))
                if it - 2 in BS:
                    T.dep((p.V.c, BS[it - 2]["delta"]))
                for rc in range(2):
                    for th in range(CH // 512):
                        tsl = slice(th * 512, (th + 1) * 512)
                        for fc in range(FCH):
                            T.matmul(psxv[:, rc, tsl],
                                     wxv[:, fc, rc * 128:(rc + 1) * 128],
                                     xinv[:, fc, tsl],
                                     start=(fc == 0), stop=(fc == FCH - 1))
                            mm = T.matmul(pszv[:, rc, tsl],
                                          wzv[:, fc, rc * 128:(rc + 1) * 128],
                                          xinv[:, fc, tsl],
                                          start=(fc == 0), stop=(fc == FCH - 1))
                xz_done = T.tag(mm)
                S.dep((p.T.c, xz_done))
                if it >= 1:
                    S.dep((p.G.c, AS[it - 1]["hist"]))
                if it - 2 in BS:
                    S.dep((p.V.c, BS[it - 2]["final"]))
                for rc in range(2):
                    S.activation(out=xrawv[:, rc, DCONV - 1:DCONV - 1 + CH],
                                 in_=psxv[:, rc, :], func=AF.Identity)
                    S.activation(out=silzv[s][:, rc, :], in_=pszv[:, rc, :],
                                 func=AF.Silu)
                evac_done = p.S.c.val
            V.dep((p.S.c, evac_done))
            for rc in range(2):
                V.tensor_scalar(xsfv[s][:, rc, :], xrawv[:, rc, 0:CH],
                                convw[:, rc * DCONV:rc * DCONV + 1], None,
                                op0=ALU.mult)
                for j in range(1, DCONV):
                    V.scalar_tensor_tensor(
                        xsfv[s][:, rc, :], xrawv[:, rc, j:j + CH],
                        convw[:, rc * DCONV + j:rc * DCONV + j + 1],
                        xsfv[s][:, rc, :], op0=ALU.mult, op1=ALU.add)
            conv_done = p.V.c.val
            G.dep(p.V.c)
            for rc in range(2):
                G.tensor_copy(xrawv[:, rc, 0:DCONV - 1],
                              xrawv[:, rc, CH:CH + DCONV - 1])
            hist_G = p.G.c.val
            S.dep((p.V.c, conv_done))
            for rc in range(2):
                S.activation(out=xsfv[s][:, rc, :], in_=xsfv[s][:, rc, :],
                             func=AF.Silu, bias=convb[:, rc:rc + 1])
            silu_done = p.S.c.val
            G.dep((p.S.c, silu_done))
            if it - 2 in BS:
                G.dep((p.V.c, BS[it - 2]["delta"]))
            for rc in range(2):
                G.tensor_copy(xsbv[s][:, rc, :], xsfv[s][:, rc, :])
            xsb_done = p.G.c.val
            with nc.psum_tensor(p.pre + f"ps_xd{it}", [96, CH], F32) as ps_xd:
                T.dep((p.G.c, xsb_done))
                if it >= 1:
                    T.dep((p.V.c, AS[it - 1]["xdv"]))
                for th in range(CH // 512):
                    tsl = slice(th * 512, (th + 1) * 512)
                    for dc in range(2):
                        mm = T.matmul(ps_xd[:, tsl], xpv[:, dc, :],
                                      xsbv[s][:, dc, tsl],
                                      start=(dc == 0), stop=(dc == 1))
                xd_done = T.tag(mm)
                V.dep((p.T.c, xd_done))
                if it >= 1:
                    V.dep(AS[it - 1]["arw"])
                V.tensor_copy(xd_sb[:], ps_xd[:])
                xdv = p.V.c.val
            nc.sync.wait_ge(p.V.c.h, xdv)
            if it >= 2:
                nc.sync.wait_ge(cc_sems[it - 2].h, 1)
            arw = p.sdma(ar_in[96 * s:96 * (s + 1), :], xd_sb[:])
            p.wtok(nc.gpsimd, arw)
            if it - 2 in BS:
                for t in BS[it - 2]["arread"]:
                    p.wtok(nc.gpsimd, t)
            nc.gpsimd.collective_compute(
                "AllReduce", ALU.add, replica_groups=[list(range(NCORE))],
                ins=[ar_in[96 * s:96 * (s + 1), :].opt()],
                outs=[ar_out[96 * s:96 * (s + 1), :].opt()],
            ).then_inc(cc_sems[it].h, 1)
            cc_sems[it].val = 1
            AS[it] = {"xz": xz_done, "evac": evac_done, "hist": hist_G,
                      "xdv": xdv, "arw": arw}

        def emit_B(it):
            bi, chk = divmod(it, NCH)
            g0 = bi * L + chk * CH
            s = it % 2
            soff = 32 * s
            first = chk == 0
            nc.gpsimd.wait_ge(cc_sems[it].h, 1)
            if it >= 1:
                nc.gpsimd.wait_ge(p.T.c.h, BS[it - 1]["dl"])
            if it - 2 in BS:
                nc.gpsimd.wait_ge(p.V.c.h, BS[it - 2]["final"])
            dld = p.gdma_start(dtin[:], ar_out[96 * s:96 * s + 64, :])
            bcd = p.gdma_start(bc_d[soff:soff + 32, :],
                               ar_out[96 * s + 64:96 * s + 96, :])
            with nc.psum_tensor(p.pre + f"ps_dl{it}", [128, 2 * CH], F32) as ps_dl:
                T.dep(dld)
                if it + 1 in AS:
                    T.dep((p.S.c, AS[it + 1]["evac"]))
                psdlv = ps_dl.ap().rearrange("p (r l) -> p r l", r=2)
                for dt_ in range(2):
                    for th in range(CH // 512):
                        tsl = slice(th * 512, (th + 1) * 512)
                        mm = T.matmul(psdlv[:, dt_, tsl],
                                      dtT[:, dt_ * 128:(dt_ + 1) * 128],
                                      dtin[:, tsl], start=True, stop=True)
                dl_done = T.tag(mm)
                for dt_ in range(2):
                    V.dep((p.T.c, dl_done))
                    V.tensor_scalar(deltv[:, dt_, :], psdlv[:, dt_, :],
                                    dtbc[:, dt_:dt_ + 1], None, op0=ALU.add)
                    V.scalar_tensor_tensor(dabs[:], deltv[:, dt_, :], -1.0,
                                           deltv[:, dt_, :],
                                           op0=ALU.mult, op1=ALU.max)
                    S.dep(p.V.c).activation(out=dabs[:], in_=dabs[:],
                                            func=AF.Exp, scale=-1.0)
                    S.activation(out=dabs[:], in_=dabs[:], func=AF.Ln,
                                 bias=onec[:, 0:1])
                    V.dep(p.S.c).scalar_tensor_tensor(
                        deltv[:, dt_, :], deltv[:, dt_, :], 0.0, dabs[:],
                        op0=ALU.max, op1=ALU.add)
                    V.tensor_mul(ubv[:, dt_, :], deltv[:, dt_, :],
                                 xsbv[s][:, dt_, :])
            delta_done = p.V.c.val

            p.wtok(nc.sync, bcd)
            if it >= 1:
                nc.sync.wait_ge(p.V.c.h, BS[it - 1]["final"])
            bld = {}
            for pair in range(2):
                bld[pair] = [
                    p.sdma(bbc[pair][:].unsqueeze(1),
                           bc_d[soff + pair:soff + pair + 1, :]
                           .partition_broadcast(128)),
                    p.sdma(cbc[pair][:].unsqueeze(1),
                           bc_d[soff + 16 + pair:soff + 17 + pair, :]
                           .partition_broadcast(128))]
            tts_done = {}
            yacc_tag = {}
            for n in range(DS):
                for dt_ in range(2):
                    pr = n * 2 + dt_
                    if pr < 2:
                        S.dep((p.V.c, delta_done))
                    else:
                        S.dep((p.V.c, tts_done[pr - 2]))
                    S.activation(out=dA_t[pr % 2][:], in_=deltv[:, dt_, :],
                                 func=AF.Exp,
                                 scale=negA[:, dt_ * DS + n:dt_ * DS + n + 1])
                    if dt_ == 0:
                        V.dep(*bld[n % 2])
                    V.tensor_mul(dBu_t[pr % 2][:], ubv[:, dt_, :],
                                 bbc[n % 2][:])
                    V.dep(p.S.c)
                    V.tensor_tensor_scan(
                        h_t[:], dA_t[pr % 2][:], dBu_t[pr % 2][:],
                        initial=(0.0 if first else hcar[:, pr:pr + 1]),
                        op0=ALU.mult, op1=ALU.add)
                    tts_done[pr] = p.V.c.val
                    V.tensor_copy(hcar[:, pr:pr + 1], h_t[:, CH - 1:CH])
                    if pr >= 2:
                        V.dep((p.G.c, yacc_tag[pr - 2]))
                    V.tensor_mul(yc_t[pr % 2][:], h_t[:], cbc[n % 2][:])
                    G.dep(p.V.c)
                    if n == 0:
                        G.tensor_copy(yav[:, dt_, :], yc_t[pr % 2][:])
                    else:
                        G.tensor_add(yav[:, dt_, :], yav[:, dt_, :],
                                     yc_t[pr % 2][:])
                    yacc_tag[pr] = p.G.c.val
                if n + 2 < DS:
                    nc.sync.wait_ge(p.V.c.h, p.V.c.val)
                    bld[n % 2] = [
                        p.sdma(bbc[n % 2][:].unsqueeze(1),
                               bc_d[soff + n + 2:soff + n + 3, :]
                               .partition_broadcast(128)),
                        p.sdma(cbc[n % 2][:].unsqueeze(1),
                               bc_d[soff + 18 + n:soff + 19 + n, :]
                               .partition_broadcast(128))]
            V.dep(p.G.c)
            if it >= 1:
                V.dep(*BS[it - 1]["yg"])
            for dt_ in range(2):
                V.scalar_tensor_tensor(yav[:, dt_, :], xsfv[s][:, dt_, :],
                                       dcol[:, dt_:dt_ + 1], yav[:, dt_, :],
                                       op0=ALU.mult, op1=ALU.add)
                V.tensor_mul(ygv[:, dt_, :], yav[:, dt_, :], silzv[s][:, dt_, :])
            final_V = p.V.c.val
            nc.sync.wait_ge(p.V.c.h, final_V)
            if yg_flat:
                ygst = [p.sdma(ygo[dt_ * 128:(dt_ + 1) * 128, g0:g0 + CH],
                               ygv[:, dt_, :]) for dt_ in range(2)]
            else:
                rsh = g0 // TOKC
                lc = g0 % TOKC
                ygst = [p.sdma(ygo[DIC * rsh + dt_ * 128:
                                   DIC * rsh + (dt_ + 1) * 128, lc:lc + CH],
                               ygv[:, dt_, :]) for dt_ in range(2)]
            if yag_cb is not None and it % 2 == 1:
                yag_cb(g0 // TOKC, BS[it - 1]["yg"] + ygst)
            BS[it] = {"dl": dl_done, "delta": delta_done, "final": final_V,
                      "arread": [dld, bcd], "yg": ygst}

        emit_A(0)
        for it in range(NIT):
            if it + 1 < NIT:
                emit_A(it + 1)
            emit_B(it)


def build_outproj():
    p = P("outproj").__enter__()
    nc = p.nc
    xi = nc.dram_tensor("x_fm", [DM, TOKC], F32, kind="ExternalInput").ap()
    yf = nc.dram_tensor("y_full", [DI, TOKC], BF16, kind="ExternalInput").ap()
    opw = nc.dram_tensor("opw", [DM, DI], F32, kind="ExternalInput").ap()
    xo = nc.dram_tensor("x_out", [DM, TOKC], F32, kind="ExternalOutput").ap()

    @p.blk.sync
    def _(sync):
        p.open_chains()
        emit_outproj(p, xi, yf, opw, xo)
        p.drain_all(nc.sync)
    nc_ = p.nc
    p.close()
    return nc_


def emit_outproj(p, xi, yf, opw, xo, dyn=None):
    nc = p.nc
    NDC = DI // 128
    if True:
        V, S, G, T = p.V, p.S, p.G, p.T
        idb = p.sb("idb", [128, 128], BF16)
        id_done = emit_identity(p, idb)
        owb = p.sb("owb", [128, FCH * DI], BF16)
        opT = p.sb("opT", [128, NDC * DM], BF16)
        owbr = owb[:].rearrange("p (c d) -> p c d", c=FCH)
        opwr = opw.rearrange("(c p) d -> p c d", p=128)
        lw = [p.gdma_start(owbr[:, c, :], opwr[:, c, :]) for c in range(FCH)]
        owv = owb.ap().rearrange("p (c d) -> p c d", c=FCH)
        opv = opT.ap().rearrange("p (c o) -> p c o", c=NDC)
        with nc.psum_tensor(p.pre + "ps_t", [128, 128], BF16) as ps_t:
            for oc in range(FCH):
                for dc in range(NDC):
                    T.dep(lw[oc], (p.V.c, id_done), p.V.c)
                    tv = T.tag(T.transpose(ps_t[:],
                                           owv[:, oc, dc * 128:(dc + 1) * 128],
                                           idb[:]))
                    V.dep((p.T.c, tv)).tensor_copy(
                        opv[:, dc, oc * 128:(oc + 1) * 128], ps_t[:])
        prep_done = p.V.c.val
        yb = p.sb("yb", [128, NDC * 512], BF16)
        ev = [p.sb(f"ev{i}", [128, 512], F32) for i in range(2)]
        xres = [p.sb(f"xres{i}", [128, 512], F32) for i in range(2)]
        ybv = yb.ap().rearrange("p (c l) -> p c l", c=NDC)
        evv, stv = {}, {}
        with ExitStack() as pes:
            pso = [pes.enter_context(nc.psum_tensor(p.pre + f"pso{oc}", [128, 512], F32))
                   for oc in range(FCH)]
            for tg in range(4):
                tsl = slice(tg * 512, (tg + 1) * 512)
                yld = []
                for dc in range(NDC):
                    if tg > 0 and dc == 0:
                        nc.sync.wait_ge(p.T.c.h, p.T.c.val)
                        nc.gpsimd.wait_ge(p.T.c.h, p.T.c.val)
                    if dyn is None:
                        yld.append(p.sdma(ybv[:, dc, :],
                                          yf[dc * 128:(dc + 1) * 128, tsl]))
                    else:
                        yld.append(p.gdma_start(
                            ybv[:, dc, :],
                            yf[dc * 128:(dc + 1) * 128,
                               tg * 512:][:, dyn["c512"]]))
                T.dep(*yld)
                T.dep((p.V.c, prep_done))
                if tg > 0:
                    T.dep((p.V.c, evv[(tg - 1, FCH - 1)]))
                for oc in range(FCH):
                    for dc in range(NDC):
                        mm = T.matmul(pso[oc][:], opv[:, dc, oc * 128:(oc + 1) * 128],
                                      ybv[:, dc, :],
                                      start=(dc == 0), stop=(dc == NDC - 1))
                    T.tag(mm)
                mm_base = p.T.c.val - FCH
                for oc in range(FCH):
                    j = tg * FCH + oc
                    pb = j % 2
                    if j >= 2:
                        nc.sync.wait_ge(p.V.c.h, evv[(tg, oc - 2) if oc >= 2
                                        else (tg - 1, FCH + oc - 2)])
                    rld = p.sdma(xres[pb][:], xi[oc * 128:(oc + 1) * 128, tsl])
                    V.dep((p.T.c, mm_base + oc + 1), rld)
                    if j >= 2:
                        V.dep(stv[j - 2])
                    V.tensor_add(ev[pb][:], pso[oc][:], xres[pb][:])
                    evv[(tg, oc)] = p.V.c.val
                    nc.sync.wait_ge(p.V.c.h, evv[(tg, oc)])
                    stv[j] = p.sdma(xo[oc * 128:(oc + 1) * 128, tsl], ev[pb][:])


def build_ffn1():
    p = P("ffn1").__enter__()
    nc = p.nc
    xi = nc.dram_tensor("x_fm", [DM, TOKC], F32, kind="ExternalInput").ap()
    pw = nc.dram_tensor("proj_w", [NE1, DM], F32, kind="ExternalInput").ap()
    anc = nc.dram_tensor("anchors", [NE1, NE1], F32, kind="ExternalInput").ap()
    tmp_in = nc.dram_tensor("temp", [1, 1], F32, kind="ExternalInput").ap()
    w1 = nc.dram_tensor("w1", [NE1, DM, DE1], F32, kind="ExternalInput").ap()
    w2 = nc.dram_tensor("w2", [NE1, DE1, DM], F32, kind="ExternalInput").ap()
    xo = nc.dram_tensor("x_out", [DM, TOKC], F32, kind="ExternalOutput").ap()
    lnb = nc.dram_tensor("lnb", [2, TOKC], F32).ap()
    grow_d = nc.dram_tensor("grow_d", [NE1, TOKC], BF16).ap()
    rbounce = nc.dram_tensor("rbounce", [1, 1], F32).ap()

    @p.blk.sync
    def _(sync):
        p.open_chains()
        p.constcol(EPS)
        emit_ffn1(p, xi, pw, anc, tmp_in, w1, w2, xo, lnb, grow_d, rbounce)
        p.drain_all(nc.sync)
    nc_ = p.nc
    p.close()
    return nc_


def emit_ffn1(p, xi, pw, anc, tmp_in, w1, w2, xo, lnb, grow_d, rbounce):
    nc = p.nc
    if True:
        V, S, G, T = p.V, p.S, p.G, p.T
        xln = [p.sb(f"xln{fc}", [128, TOKC], BF16) for fc in range(FCH)]
        ln_done = emit_ln(p, p.pre + "ln", xi, xln, lnb, scoped=True)
        nc.sync.wait_ge(p.V.c.h, ln_done)
        nc.gpsimd.wait_ge(p.V.c.h, ln_done)
        p.S.dep((p.V.c, ln_done))
        p.G.dep((p.V.c, ln_done))
        p.T.dep((p.V.c, ln_done))

        idb = p.sb("idb", [128, 128], BF16)
        idf = p.sb("idf", [128, 128], F32)
        emit_identity(p, idb)
        id_done = emit_identity(p, idf)
        pwb = p.sb("pwb", [NE1, DM], BF16)
        pwT = p.sb("pwT", [128, FCH * NE1], BF16)
        ancb = p.sb("ancb", [NE1, NE1], BF16)
        ancT = p.sb("ancT", [NE1, NE1], BF16)
        zq = p.sb("zq", [NE1, TOKC], BF16)
        lgf = p.sb("lgf", [NE1, TOKC], F32)
        lg = p.sb("lg", [128, NTB * NE1], F32)
        gate = p.sb("gate", [128, NTB * NE1], F32)
        grow = p.sb("grow", [NE1, TOKC], F32)
        rcol = p.sb("rcol", [1, 1], F32)
        r6 = p.sb("r6", [NE1, 1], F32)
        sc1 = p.sb("sc1", [128, 1], F32)
        sc2 = p.sb("sc2", [128, 1], F32)
        sc3 = p.sb("sc3", [128, 1], F32)
        sc4 = p.sb("sc4", [128, 1], F32)
        sc5 = p.sb("sc5", [128, 1], F32)
        e_t = p.sb("e_t", [128, NE1], F32)
        msk1 = p.sb("msk1", [128, NE1], F32)
        msk2 = p.sb("msk2", [128, NE1], F32)
        tmp6 = p.sb("tmp6", [128, NE1], F32)
        lv1 = p.gdma_start(pwb[:], pw[:])
        lv2 = p.gdma_start(ancb[:], anc[:])
        lv3 = p.sdma(rcol[:], tmp_in[:])
        V.dep(lv3).tensor_scalar_max(rcol[:], rcol[:], 0.1)
        V.reciprocal(rcol[:], rcol[:])
        V.tensor_scalar_mul(rcol[:], rcol[:], 0.5)
        nc.sync.wait_ge(p.V.c.h, p.V.c.val)
        rbv = p.sdma(rbounce[:], rcol[:])
        p.wtok(nc.sync, rbv)
        r6v = p.sdma(r6[:].unsqueeze(1), rbounce[0:1, :].partition_broadcast(NE1))
        with nc.psum_tensor(p.pre + "ps_pT", [128, NE1], BF16) as ps_pT, \
             nc.psum_tensor(p.pre + "ps_aT", [NE1, NE1], BF16) as ps_aT, \
             nc.psum_tensor(p.pre + "ps_z", [NE1, 512], F32) as ps_z, \
             nc.psum_tensor(p.pre + "ps_d", [NE1, 512], F32) as ps_d, \
             nc.psum_tensor(p.pre + "ps_lT", [128, NE1], F32) as ps_lT:
            for fc in range(FCH):
                T.dep(lv1, (p.V.c, id_done), p.V.c)
                tv = T.tag(T.transpose(ps_pT[:], pwb[:, fc * 128:(fc + 1) * 128],
                                       idb[0:NE1, 0:NE1]))
                V.dep((p.T.c, tv)).tensor_copy(pwT[:, fc * NE1:(fc + 1) * NE1],
                                               ps_pT[:])
            T.dep(lv2, p.V.c)
            tv = T.tag(T.transpose(ps_aT[:], ancb[:], idb[0:NE1, 0:NE1]))
            V.dep((p.T.c, tv)).tensor_copy(ancT[:], ps_aT[:])
            pwt_done = p.V.c.val
            for tc in range(4):
                sl = slice(tc * 512, (tc + 1) * 512)
                T.dep((p.V.c, max(pwt_done, ln_done)), p.S.c)
                for fc in range(FCH):
                    mm = T.matmul(ps_z[:], pwT[:, fc * NE1:(fc + 1) * NE1],
                                  xln[fc][:, sl], start=(fc == 0), stop=(fc == FCH - 1))
                tv = T.tag(mm)
                S.dep((p.T.c, tv)).activation(out=zq[:, sl], in_=ps_z[:],
                                              func=AF.Tanh, scale=1.0 / 0.3)
            for tc in range(4):
                sl = slice(tc * 512, (tc + 1) * 512)
                T.dep(p.S.c, p.V.c)
                mm = T.matmul(ps_d[:], ancT[:], zq[:, sl], start=True, stop=True)
                tv = T.tag(mm)
                V.dep((p.T.c, tv), r6v)
                V.tensor_scalar(lgf[:, sl], ps_d[:], -6.0, r6[:, 0:1],
                                op0=ALU.add, op1=ALU.mult)
            for tb in range(NTB):
                T.dep(p.V.c)
                tv = T.tag(T.transpose(ps_lT[:], lgf[:, tb * 128:(tb + 1) * 128],
                                       idf[0:NE1, 0:NE1]))
                V.dep((p.T.c, tv)).tensor_copy(lg[:, tb * NE1:(tb + 1) * NE1],
                                               ps_lT[:])
        with nc.psum_tensor(p.pre + "ps_gT", [NE1, 128], F32) as ps_gT:
            for tb in range(NTB):
                sl = slice(tb * NE1, (tb + 1) * NE1)
                V.tensor_reduce(sc1[:], lg[:, sl], axis=AX.X, op=ALU.max)
                V.tensor_scalar(msk1[:], lg[:, sl], sc1[:, 0:1], None,
                                op0=ALU.is_equal)
                V.scalar_tensor_tensor(tmp6[:], msk1[:], -1e30, lg[:, sl],
                                       op0=ALU.mult, op1=ALU.add)
                V.tensor_reduce(sc2[:], tmp6[:], axis=AX.X, op=ALU.max)
                V.tensor_scalar(msk2[:], lg[:, sl], sc2[:, 0:1], None,
                                op0=ALU.is_equal)
                V.tensor_scalar(tmp6[:], msk1[:], -1.0, 1.0, op0=ALU.mult, op1=ALU.add)
                V.tensor_mul(msk2[:], msk2[:], tmp6[:])
                V.tensor_reduce(sc5[:], msk1[:], axis=AX.X, op=ALU.add)
                V.tensor_scalar(sc5[:], sc5[:], 2.0, None, op0=ALU.is_ge)
                V.tensor_sub(sc3[:], sc1[:], sc2[:])
                S.dep(p.V.c).activation(out=sc3[:], in_=sc3[:], func=AF.Sigmoid)
                V.dep(p.S.c)
                V.tensor_scalar(sc4[:], sc3[:], -1.0, 1.0, op0=ALU.mult, op1=ALU.add)
                V.tensor_scalar(tmp6[:, 0:1], sc5[:], -1.0, 1.0,
                                op0=ALU.mult, op1=ALU.add)
                V.tensor_mul(sc4[:], sc4[:], tmp6[:, 0:1])
                V.tensor_mul(sc3[:], sc3[:], tmp6[:, 0:1])
                V.scalar_tensor_tensor(sc3[:], sc5[:], 0.5, sc3[:],
                                       op0=ALU.mult, op1=ALU.add)
                V.tensor_scalar_mul(sc3[:], sc3[:], 0.5)
                V.tensor_scalar_mul(sc4[:], sc4[:], 0.5)
                gsl = gate[:, sl]
                V.tensor_scalar(gsl, msk1[:], sc3[:, 0:1], None, op0=ALU.mult)
                V.scalar_tensor_tensor(gsl, msk2[:], sc4[:, 0:1], gsl,
                                       op0=ALU.mult, op1=ALU.add)
            for tb in range(NTB):
                T.dep(p.V.c)
                tv = T.tag(T.transpose(ps_gT[:], gate[:, tb * NE1:(tb + 1) * NE1],
                                       idf[:]))
                V.dep((p.T.c, tv)).tensor_copy(grow[:, tb * 128:(tb + 1) * 128],
                                               ps_gT[:])
        nc.gpsimd.wait_ge(p.V.c.h, p.V.c.val)
        gv = p.gdma_start(grow_d[:], grow[:])
        p.wtok(nc.sync, gv)

        w1b = [p.sb(f"w1b{e}", [128, FCH * DE1], BF16) for e in range(NE1)]
        w2b = [p.sb(f"w2b{e}", [128, 4 * DM], BF16) for e in range(NE1)]
        w1v, w2v, w1ld, w2ld = [], [], [], []
        for e in range(NE1):
            w1ld.append(p.gdma_start(
                w1b[e][:].rearrange("p (h o) -> p h o", h=FCH),
                w1[e].rearrange("(h p) o -> p h o", p=128)))
            w2ld.append(p.gdma_start(
                w2b[e][:].rearrange("p (c o) -> p c o", c=4),
                w2[e].rearrange("(c p) o -> p c o", p=128)))
            w1v.append(w1b[e].ap().rearrange("p (h o) -> p h o", h=FCH))
            w2v.append(w2b[e].ap().rearrange("p (c o) -> p c o", c=4))
        hs = [p.sb(f"hs{e}", [128, 4 * 512], BF16) for e in range(NE1)]
        heb = p.sb("heb", [128, 4 * 512], BF16)
        gbc = [p.sb(f"gbc{i}", [128, 512], BF16) for i in range(2)]
        ev = [p.sb(f"evo{i}", [128, 512], F32) for i in range(2)]
        xres = [p.sb(f"xres{i}", [128, 512], F32) for i in range(2)]
        evv, stv = {}, {}
        gmul_done, hs_done = {}, {}
        with ExitStack() as pes:
            ps_h = pes.enter_context(nc.psum_tensor(p.pre + "ps_h", [128, 2048], F32))
            phv = ps_h.ap().rearrange("p (c l) -> p c l", c=4)
            for tg in range(4):
                tsl = slice(tg * 512, (tg + 1) * 512)
                for e in range(NE1):
                    k = tg * NE1 + e
                    if k >= 2:
                        nc.sync.wait_ge(p.V.c.h, gmul_done[k - 2])
                    gld = p.sdma(gbc[k % 2][:].unsqueeze(1),
                                 grow_d[e:e + 1, tsl].partition_broadcast(128))
                    T.dep(w1ld[e], (p.V.c, ln_done), p.S.c, p.V.c)
                    for hc4 in range(4):
                        for fc in range(FCH):
                            mm = T.matmul(phv[:, hc4, :],
                                          w1v[e][:, fc, hc4 * 128:(hc4 + 1) * 128],
                                          xln[fc][:, tsl],
                                          start=(fc == 0), stop=(fc == FCH - 1))
                    tv = T.tag(mm)
                    S.dep((p.T.c, tv), p.V.c)
                    for hc4 in range(4):
                        S.activation(out=heb[:, hc4 * 512:(hc4 + 1) * 512],
                                     in_=phv[:, hc4, :], func=AF.Gelu_apprx_tanh)
                    V.dep(p.S.c, gld)
                    if tg >= 1:
                        V.dep((p.T.c, hs_done[tg - 1]))
                    for hc4 in range(4):
                        V.tensor_mul(hs[e][:, hc4 * 512:(hc4 + 1) * 512],
                                     heb[:, hc4 * 512:(hc4 + 1) * 512], gbc[k % 2][:])
                    gmul_done[k] = p.V.c.val
                hsv = p.V.c.val
                with ExitStack() as pes2:
                    pso = [pes2.enter_context(nc.psum_tensor(p.pre + f"pso{tg}_{oc}",
                                                             [128, 512], F32))
                           for oc in range(4)]
                    for wave in range(2):
                        T.dep((p.V.c, hsv), *w2ld)
                        if (tg, wave) != (0, 0):
                            T.dep(p.V.c)
                        for oc in range(wave * 4, wave * 4 + 4):
                            po = pso[oc - wave * 4]
                            for e in range(NE1):
                                for dc in range(4):
                                    mm = T.matmul(
                                        po[:],
                                        w2v[e][:, dc, oc * 128:(oc + 1) * 128],
                                        hs[e][:, dc * 512:(dc + 1) * 512],
                                        start=(e == 0 and dc == 0),
                                        stop=(e == NE1 - 1 and dc == 3))
                            T.tag(mm)
                        mmb = p.T.c.val - 4
                        for oc in range(wave * 4, wave * 4 + 4):
                            j = tg * FCH + oc
                            pb = j % 2
                            if j >= 2:
                                nc.sync.wait_ge(p.V.c.h, evv[j - 2])
                            rld = p.sdma(xres[pb][:], xi[oc * 128:(oc + 1) * 128, tsl])
                            V.dep((p.T.c, mmb + (oc - wave * 4) + 1), rld)
                            if j >= 2:
                                V.dep(stv[j - 2])
                            V.tensor_add(ev[pb][:], pso[oc - wave * 4][:], xres[pb][:])
                            evv[j] = p.V.c.val
                            nc.sync.wait_ge(p.V.c.h, evv[j])
                            stv[j] = p.sdma(xo[oc * 128:(oc + 1) * 128, tsl], ev[pb][:])
                hs_done[tg] = p.T.c.val


def build_qkv():
    p = P("qkv").__enter__()
    nc = p.nc
    xi = nc.dram_tensor("x_fm", [DM, TOKC], F32, kind="ExternalInput").ap()
    wq = nc.dram_tensor("wq", [DM, DM], F32, kind="ExternalInput").ap()
    wk = nc.dram_tensor("wk", [DM, DM], F32, kind="ExternalInput").ap()
    wv = nc.dram_tensor("wv", [DM, DM], F32, kind="ExternalInput").ap()
    bq = nc.dram_tensor("bq", [128, FCH], F32, kind="ExternalInput").ap()
    bk = nc.dram_tensor("bk", [128, FCH], F32, kind="ExternalInput").ap()
    bv = nc.dram_tensor("bv", [1, DM], F32, kind="ExternalInput").ap()
    q_d = nc.dram_tensor("q_d", [DM, TOKC], BF16, kind="ExternalOutput").ap()
    k_d = nc.dram_tensor("k_d", [DM, TOKC], BF16, kind="ExternalOutput").ap()
    v_d = nc.dram_tensor("v_d", [TOKC, DM], BF16, kind="ExternalOutput").ap()
    qn_d = nc.dram_tensor("qn_d", [NH, TOKC], F32, kind="ExternalOutput").ap()
    kn_d = nc.dram_tensor("kn_d", [NH, TOKC], F32, kind="ExternalOutput").ap()
    lnb = nc.dram_tensor("lnb", [2, TOKC], F32).ap()

    @p.blk.sync
    def _(sync):
        p.open_chains()
        p.constcol(EPS)
        emit_qkv(p, xi, wq, wk, wv, bq, bk, bv, q_d, k_d, v_d, qn_d, kn_d, lnb)
        p.drain_all(nc.sync)
    nc_ = p.nc
    p.close()
    return nc_


def emit_qkv(p, xi, wq, wk, wv, bq, bk, bv, q_d, k_d, v_d, qn_d, kn_d, lnb,
             norms=True):
    nc = p.nc
    if True:
        V, S, G, T = p.V, p.S, p.G, p.T
        xln = [p.sb(f"xln{fc}", [128, TOKC], BF16) for fc in range(FCH)]
        ln_done = emit_ln(p, p.pre + "ln", xi, xln, lnb, scoped=True)
        nc.sync.wait_ge(p.V.c.h, ln_done)
        nc.gpsimd.wait_ge(p.V.c.h, ln_done)
        p.S.dep((p.V.c, ln_done))
        p.G.dep((p.V.c, ln_done))
        p.T.dep((p.V.c, ln_done))

        wb = [p.sb(f"wb{i}", [128, FCH * DM], BF16) for i in range(2)]
        bcol = p.sb("bcol", [128, 2 * FCH], F32)
        bvr = p.sb("bvr", [1, DM], BF16)
        hsel = p.sb("hsel", [128, 2], BF16)
        onesb = p.sb("onesb", [1, 128], BF16)
        evt = [p.sb(f"evt{i}", [128, 512], BF16) for i in range(2)]
        sqs = p.sb("sqs", [128, 512], BF16)
        nrm = p.sb("nrm", [2, 512], F32)
        bql = p.sdma(bcol[:, 0:FCH], bq[:])
        bkl = p.sdma(bcol[:, FCH:2 * FCH], bk[:])
        bvl = p.gdma_start(bvr[:], bv[:])
        V.memset(hsel[:], 0.0)
        V.memset(hsel[0:64, 0:1], 1.0)
        V.memset(hsel[64:128, 1:2], 1.0)
        V.memset(onesb[:], 1.0)
        sel_done = p.V.c.val
        V.dep(bql, bkl)

        wb0r = wb[0][:].rearrange("p (c o) -> p c o", c=FCH)
        wb1r = wb[1][:].rearrange("p (c o) -> p c o", c=FCH)
        wqr = wq.rearrange("(c p) o -> p c o", p=128)
        wkr = wk.rearrange("(c p) o -> p c o", p=128)
        wql = [p.gdma_start(wb0r[:, c, :], wqr[:, c, :]) for c in range(FCH)]
        wkl = [p.gdma_start(wb1r[:, c, :], wkr[:, c, :]) for c in range(FCH)]
        wloads = [wql, wkl]
        wvs = [wb[0].ap().rearrange("p (c o) -> p c o", c=FCH),
               wb[1].ap().rearrange("p (c o) -> p c o", c=FCH)]
        evo, stq, nst = {}, {}, {}
        with nc.psum_tensor(p.pre + "ps_a", [128, 512], F32) as ps_a, \
             nc.psum_tensor(p.pre + "ps_b", [128, 512], F32) as ps_b, \
             nc.psum_tensor(p.pre + "ps_n", [2, 512], F32) as ps_n, \
             nc.psum_tensor(p.pre + "ps_v1", [128, 512], F32) as ps_v1, \
             nc.psum_tensor(p.pre + "ps_v2", [128, 512], F32) as ps_v2:
            psqk = [ps_a, ps_b]
            for which, (outd, nout, scl, bofs) in enumerate(
                    [(q_d, qn_d, 0.125, 0), (k_d, kn_d, 1.0, FCH)]):
                for oc in range(FCH):
                    for tg in range(4):
                        jj = which * 32 + oc * 4 + tg
                        tsl = slice(tg * 512, (tg + 1) * 512)
                        po = psqk[jj % 2]
                        if jj % 32 == 0:
                            T.dep(*wloads[which])
                        if jj >= 2:
                            T.dep((p.V.c, evo[jj - 2]))
                        for fc in range(FCH):
                            mm = T.matmul(po[:],
                                          wvs[which][:, fc, oc * 128:(oc + 1) * 128],
                                          xln[fc][:, tsl],
                                          start=(fc == 0), stop=(fc == FCH - 1))
                        tv = T.tag(mm)
                        pb = jj % 2
                        V.dep((p.T.c, tv))
                        if jj >= 2:
                            V.dep(stq[jj - 2])
                        V.tensor_scalar(evt[pb][:], po[:],
                                        bcol[:, bofs + oc:bofs + oc + 1], scl,
                                        op0=ALU.add, op1=ALU.mult)
                        evo[jj] = p.V.c.val
                        nc.sync.wait_ge(p.V.c.h, evo[jj])
                        stq[jj] = p.sdma(outd[oc * 128:(oc + 1) * 128, tsl],
                                         evt[pb][:])
                        if norms:
                            S.dep((p.V.c, evo[jj]))
                            if jj >= 1:
                                S.dep(nst[jj - 1])
                            S.activation(out=sqs[:], in_=evt[pb][:], func=AF.Square)
                            T.dep(p.S.c, (p.V.c, sel_done))
                            mm = T.matmul(ps_n[:], hsel[:], sqs[:],
                                          start=True, stop=True)
                            tv = T.tag(mm)
                            S.dep((p.T.c, tv))
                            S.activation(out=nrm[:], in_=ps_n[:], func=AF.Sqrt)
                            nc.sync.wait_ge(p.S.c.h, p.S.c.val)
                            nst[jj] = p.sdma(nout[2 * oc:2 * oc + 2, tsl], nrm[:])
            qk_done = p.T.c.val
            nc.gpsimd.wait_ge(p.T.c.h, qk_done)
            wvr = wv.rearrange("(c p) o -> p c o", p=128)
            wvl = [p.gdma_start(wb0r[:, c, :], wvr[:, c, :])
                   for c in range(FCH)]
            psv = [ps_v1, ps_v2]
            evv, stv = {}, {}
            for tb in range(NTB):
                for dh in range(2):
                    j = tb * 2 + dh
                    po = psv[j % 2]
                    dsl = slice(dh * 512, (dh + 1) * 512)
                    if j == 0:
                        T.dep(*wvl)
                    T.dep(bvl, (p.V.c, sel_done))
                    if j >= 2:
                        T.dep((p.V.c, evv[j - 2]))
                    for fc in range(FCH):
                        T.matmul(po[:], xln[fc][:, tb * 128:(tb + 1) * 128],
                                 wvs[0][:, fc, dsl], start=(fc == 0), stop=False)
                    mm = T.matmul(po[:], onesb[:], bvr[0:1, dsl],
                                  start=False, stop=True)
                    tv = T.tag(mm)
                    pb = j % 2
                    V.dep((p.T.c, tv))
                    if j >= 2:
                        V.dep(*stv[j - 2])
                    else:
                        V.dep(stq[62 + j])
                    V.tensor_copy(evt[pb][:], po[:])
                    evv[j] = p.V.c.val
                    nc.sync.wait_ge(p.V.c.h, evv[j])
                    stv[j] = [p.sdma(v_d[tb * 128:(tb + 1) * 128, dsl],
                                     evt[pb][:])]


def build_attn():
    p = P("attn").__enter__()
    nc = p.nc
    qh = nc.dram_tensor("qh", [128, TOK], BF16, kind="ExternalInput").ap()
    kh = nc.dram_tensor("kh", [128, TOK], BF16, kind="ExternalInput").ap()
    vh = nc.dram_tensor("vh", [TOK, 128], BF16, kind="ExternalInput").ap()
    qn2 = nc.dram_tensor("qn2", [2, TOK], F32, kind="ExternalInput").ap()
    kn2 = nc.dram_tensor("kn2", [2, TOK], F32, kind="ExternalInput").ap()
    al2 = nc.dram_tensor("al2", [2, L], F32, kind="ExternalInput").ap()
    oh_d = nc.dram_tensor("oh_d", [128, TOK], BF16, kind="ExternalOutput").ap()
    den_d = nc.dram_tensor("den_d", [2, TOK], F32, kind="ExternalOutput").ap()
    QC = 512
    NQC = L // QC
    asp_d = nc.dram_tensor("asp_d", [2, 4, L], BF16)
    qex_d = nc.dram_tensor("qex_d", [4, QC], BF16)

    @p.blk.sync
    def _(sync):
        p.open_chains()
        emit_attn(p, qh, kh, vh, qn2, kn2, al2, oh_d, den_d, asp_d, qex_d,
                  flat_io=True)
        p.drain_all(nc.sync)
    nc_ = p.nc
    p.close()
    return nc_


ATQC = 512
ATNQC = L // ATQC


def emit_attn(p, qh, kh, vh, qn2, kn2, al2, oh_d, den_d, asp_d, qex_d,
              flat_io=True, dyn=None):
    nc = p.nc
    QC = ATQC
    NQC = ATNQC
    NKC = L // 512
    V, S, G, T = p.V, p.S, p.G, p.T
    mi = p.sb("mi", [128, QC], F32)
    invm = [p.sb(f"invm{d}", [128, QC], BF16) for d in range(4)]
    G.iota(mi[:], pattern=[[-1, QC]], base=0, channel_multiplier=1,
           allow_small_or_imprecise_dtypes=True)
    V.dep(p.G.c)
    for d in range(4):
        V.tensor_scalar(invm[d][:], mi[:], float(-d * 128), None, op0=ALU.is_gt)
    masks_done = p.V.c.val
    ACW = 2048
    arow = p.sb("arow", [1, ACW], F32)
    hi_b = p.sb("hi_b", [1, ACW], BF16)
    hi_f = p.sb("hi_f", [1, ACW], F32)
    lo_b = p.sb("lo_b", [1, ACW], BF16)
    asp_all = []
    for hb in range(2):
        for pc in range(L // ACW):
            csl = slice(pc * ACW, (pc + 1) * ACW)
            av = p.sdma(arow[:], al2[hb:hb + 1, csl])
            V.dep(av, *asp_all[-3:])
            if hb == 0 and pc == 0:
                V.memset(lo_b[:], 1.0)
                nc.sync.wait_ge(p.V.c.h, p.V.c.val)
                for hb2 in range(2):
                    for pc2 in range(L // ACW):
                        asp_all.append(p.sdma(
                            asp_d[hb2, 0:1, pc2 * ACW:(pc2 + 1) * ACW],
                            lo_b[:]))
                V.dep(*asp_all)
            V.tensor_copy(hi_b[:], arow[:])
            V.tensor_copy(hi_f[:], hi_b[:])
            V.tensor_sub(arow[:], arow[:], hi_f[:])
            nc.sync.wait_ge(p.V.c.h, p.V.c.val)
            st1 = p.sdma(asp_d[hb, 1:2, csl], hi_b[:])
            V.dep(st1)
            V.tensor_copy(hi_b[:], arow[:])
            V.tensor_copy(hi_f[:], hi_b[:])
            V.tensor_sub(arow[:], arow[:], hi_f[:])
            V.tensor_copy(lo_b[:], arow[:])
            nc.sync.wait_ge(p.V.c.h, p.V.c.val)
            st2 = p.sdma(asp_d[hb, 2:3, csl], hi_b[:])
            st3 = p.sdma(asp_d[hb, 3:4, csl], lo_b[:])
            asp_all += [st1, st2, st3]

    kx = p.sb("kx", [68, L], BF16)
    qfull = p.sb("qfull", [64, L], BF16)
    vx = p.sb("vx", [128, 64 * 65], BF16)
    sqk = [p.sb(f"sqk{i}", [64, 512], BF16) for i in range(2)]
    knrow = p.sb("knrow", [1, L], BF16)
    qnrow = p.sb("qnrow", [1, L], BF16)
    qx2 = [p.sb(f"qx2_{i}", [68, QC], BF16) for i in range(2)]
    pt2 = [p.sb(f"pt2_{i}", [128, QC], BF16) for i in range(3)]
    smf2 = [p.sb(f"smf2_{i}", [128, QC], F32) for i in range(2)]
    ot = [p.sb(f"ot{i}", [64, QC], BF16) for i in range(2)]
    dent = [p.sb(f"dent{i}", [1, QC], F32) for i in range(2)]
    alr = [p.sb(f"alr{i}", [1, QC], F32) for i in range(2)]
    mrow = p.sb("mrow", [1, QC], F32)
    mx = p.sb("mx", [1, 1], F32)
    knr = p.sb("knr", [1, 1], F32)
    ones64 = p.sb("ones64", [64, 1], BF16)
    vxv = vx.ap().rearrange("p (j w) -> p j w", j=64)
    V.memset(ones64[:], 1.0)
    for j in range(2):
        V.memset(qx2[j][64:68, :], 1.0)
    init_done = p.V.c.val

    gnm = 0
    nm_mm = {}
    nm_sq = {}
    gkb = 0
    qk_free = {}
    pv_tag = {}
    ex_tag = {}
    smf_user = {}
    ot_st = {}
    al_user = {}
    psp_free = {}
    qx_free = {}
    prev_seq = None

    with nc.psum_tensor(p.pre + "ps_s1", [128, QC], F32) as ps_s1, \
         nc.psum_tensor(p.pre + "ps_s2", [128, QC], F32) as ps_s2, \
         nc.psum_tensor(p.pre + "ps_s3", [128, QC], F32) as ps_s3, \
         nc.psum_tensor(p.pre + "ps_p1", [65, QC], F32) as ps_p1, \
         nc.psum_tensor(p.pre + "ps_p2", [65, QC], F32) as ps_p2, \
         nc.psum_tensor(p.pre + "ps_n1", [1, 512], F32) as ps_n1, \
         nc.psum_tensor(p.pre + "ps_n2", [1, 512], F32) as ps_n2:
        pss = [ps_s1, ps_s2, ps_s3]
        psp = [ps_p1, ps_p2]
        psn = [ps_n1, ps_n2]
        for hb in range(2):
            for bi in range(B):
                seq0 = bi * L
                if prev_seq is not None:
                    nc.sync.wait_ge(p.T.c.h, prev_seq[0])
                    nc.sync.wait_ge(p.V.c.h, prev_seq[1])
                kl1 = p.sdma(kx[0:64, 0:L], kh[hb * 64:(hb + 1) * 64,
                                               seq0:seq0 + L])
                qlf = p.sdma(qfull[:, 0:L], qh[hb * 64:(hb + 1) * 64,
                                               seq0:seq0 + L])
                for r3 in asp_all:
                    p.wtok(nc.sync, r3)
                kl2 = p.sdma(kx[64:68, 0:L], asp_d[hb, :, :])
                vl = p.sdma(vxv[:, :, 0:64],
                            vh[seq0:seq0 + L, hb * 64:(hb + 1) * 64]
                            .rearrange("(j p) d -> p j d", p=128))
                V.dep(vl)
                if prev_seq is not None:
                    V.dep((p.T.c, prev_seq[0]))
                V.memset(vxv[:, :, 64:65], 1.0)
                vx_done = p.V.c.val
                V.dep(kl1, qlf, (p.V.c, init_done))
                if prev_seq is not None:
                    S.dep((p.V.c, prev_seq[1]))
                for idx in range(2 * NKC):
                    qside = idx >= NKC
                    tc = idx % NKC
                    tsl = slice(tc * 512, (tc + 1) * 512)
                    src = qfull[:, tsl] if qside else kx[0:64, tsl]
                    dst = (qnrow if qside else knrow)[:, tsl]
                    b2 = gnm % 2
                    if gnm >= 2:
                        V.dep((p.T.c, nm_mm[gnm - 2]))
                    V.tensor_mul(sqk[b2][:], src, src)
                    T.dep(p.V.c)
                    if gnm >= 2:
                        T.dep((p.S.c, nm_sq[gnm - 2]))
                    mm = T.matmul(psn[b2][:], ones64[:], sqk[b2][:],
                                  start=True, stop=True)
                    nm_mm[gnm] = T.tag(mm)
                    S.dep((p.T.c, nm_mm[gnm]))
                    S.activation(out=dst, in_=psn[b2][:], func=AF.Sqrt)
                    nm_sq[gnm] = p.S.c.val
                    gnm += 1
                norms_done = p.S.c.val
                V.memset(knr[:], 0.0)
                for qc in range(NQC):
                    j = qc % 2
                    qsl = slice(qc * QC, (qc + 1) * QC)
                    if j in al_user:
                        nc.sync.wait_ge(p.V.c.h, al_user[j])
                    alv = p.sdma(alr[j][:], al2[hb:hb + 1, qsl])
                    V.dep((p.S.c, norms_done))
                    V.tensor_reduce(mx[:], knrow[:, qsl], axis=AX.X, op=ALU.max)
                    V.tensor_tensor(knr[:], knr[:], mx[:], op=ALU.max)
                    V.tensor_scalar(mrow[:], qnrow[:, qsl], knr[0:1, 0:1],
                                    None, op0=ALU.mult)
                    V.dep(alv)
                    V.tensor_add(mrow[:], mrow[:], alr[j][:])
                    al_user[j] = p.V.c.val
                    if j in qx_free:
                        V.dep((p.T.c, qx_free[j]))
                    V.tensor_copy(qx2[j][0:64, :], qfull[:, qsl])
                    V.tensor_scalar_mul(qx2[j][64:65, :], mrow[:], -1.0)
                    qx_ready = p.V.c.val
                    jmax = (qc + 1) * 4
                    pvb = psp[j]

                    def emit_pv(g, kb):
                        T.dep((p.S.c, ex_tag[g]))
                        if kb == 0:
                            T.dep((p.V.c, vx_done))
                            if j in psp_free:
                                T.dep((p.V.c, psp_free[j]))
                        mm = T.matmul(pvb[:], vxv[:, kb, :], pt2[g % 3][:],
                                      start=(kb == 0), stop=(kb == jmax - 1))
                        pv_tag[g] = T.tag(mm)

                    for kb in range(jmax):
                        b3 = gkb % 3
                        sb_ = pss[b3]
                        if kb == 0:
                            T.dep((p.V.c, qx_ready))
                            if qc == 0:
                                T.dep(kl1, kl2)
                        if gkb - 3 in qk_free:
                            T.dep(qk_free[gkb - 3])
                        mm = T.matmul(sb_[:], kx[:, kb * 128:(kb + 1) * 128],
                                      qx2[j][:], start=True, stop=True)
                        tv_qk = T.tag(mm)
                        if kb > 0:
                            emit_pv(gkb - 1, kb - 1)
                        diag = kb >= qc * 4
                        if diag:
                            b2 = gkb % 2
                            V.dep((p.T.c, tv_qk), (p.V.c, masks_done))
                            if b2 in smf_user:
                                V.dep((p.S.c, smf_user[b2]))
                            V.scalar_tensor_tensor(smf2[b2][:],
                                                   invm[kb - qc * 4][:],
                                                   -30000.0, sb_[:],
                                                   op0=ALU.mult, op1=ALU.add)
                            expin = smf2[b2]
                            qk_free[gkb] = (p.V.c, p.V.c.val)
                            S.dep(p.V.c)
                        else:
                            expin = sb_
                            S.dep((p.T.c, tv_qk))
                        if gkb - 3 in pv_tag:
                            S.dep((p.T.c, pv_tag[gkb - 3]))
                        S.activation(out=pt2[gkb % 3][:], in_=expin[:],
                                     func=AF.Exp)
                        ex_tag[gkb] = p.S.c.val
                        if diag:
                            smf_user[gkb % 2] = ex_tag[gkb]
                        else:
                            qk_free[gkb] = (p.S.c, ex_tag[gkb])
                        gkb += 1
                    emit_pv(gkb - 1, jmax - 1)
                    qx_free[j] = p.T.c.val
                    V.dep((p.T.c, pv_tag[gkb - 1]))
                    if j in ot_st:
                        V.dep(*ot_st[j])
                    V.tensor_copy(ot[j][:], pvb[0:64, :])
                    V.tensor_copy(dent[j][:], pvb[64:65, :])
                    psp_free[j] = p.V.c.val
                    nc.sync.wait_ge(p.V.c.h, psp_free[j])
                    st1 = p.sdma(oh_d[hb * 64:(hb + 1) * 64,
                                      seq0 + qc * QC:seq0 + (qc + 1) * QC],
                                 ot[j][:])
                    st2 = p.sdma(den_d[hb:hb + 1,
                                       seq0 + qc * QC:seq0 + (qc + 1) * QC],
                                 dent[j][:])
                    ot_st[j] = (st1, st2)
                prev_seq = (p.T.c.val, p.V.c.val)


def build_wo():
    p = P("wo").__enter__()
    nc = p.nc
    xi = nc.dram_tensor("x_fm", [DM, TOKC], F32, kind="ExternalInput").ap()
    oc_in = nc.dram_tensor("ocat", [DM, TOKC], BF16, kind="ExternalInput").ap()
    dn_in = nc.dram_tensor("den_all", [NH, TOKC], F32, kind="ExternalInput").ap()
    wo = nc.dram_tensor("wo", [DM, DM], F32, kind="ExternalInput").ap()
    bo = nc.dram_tensor("bo", [128, FCH], F32, kind="ExternalInput").ap()
    xo = nc.dram_tensor("x_out", [DM, TOKC], F32, kind="ExternalOutput").ap()
    rb_d = nc.dram_tensor("rb_d", [NH, TOKC], F32)

    @p.blk.sync
    def _(sync):
        p.open_chains()
        emit_wo(p, xi, oc_in, dn_in, wo, bo, xo, rb_d)
        p.drain_all(nc.sync)
    nc_ = p.nc
    p.close()
    return nc_


def emit_wo(p, xi, oc_in, dn_in, wo, bo, xo, rb_d, dyn=None):
    nc = p.nc
    if True:
        V, S, G, T = p.V, p.S, p.G, p.T
        dn = p.sb("dn", [NH, TOKC], F32)
        wob = p.sb("wob", [128, FCH * DM], BF16)
        bocol = p.sb("bocol", [128, FCH], F32)
        rbc = p.sb("rbc", [128, TOKC], F32)
        on_t = [p.sb(f"on{fc}", [128, TOKC], BF16) for fc in range(FCH)]
        oct_ = [p.sb(f"oct{i}", [128, TOKC], BF16) for i in range(2)]
        ev = [p.sb(f"ev{i}", [128, 512], F32) for i in range(2)]
        xres = [p.sb(f"xres{i}", [128, 512], F32) for i in range(2)]
        if dyn is None:
            dl = p.sdma(dn[:], dn_in[:])
        else:
            dl = p.gdma_start(dn[:], dn_in[:, dyn["cT"]])
        wobr = wob[:].rearrange("p (c o) -> p c o", c=FCH)
        wor = wo.rearrange("(c p) o -> p c o", p=128)
        wl = [p.gdma_start(wobr[:, c, :], wor[:, c, :]) for c in range(FCH)]
        bl = p.sdma(bocol[:], bo[:])
        V.dep(dl).reciprocal(dn[:], dn[:])
        nc.sync.wait_ge(p.V.c.h, p.V.c.val)
        rbst = p.sdma(rb_d[:], dn[:])
        p.wtok(nc.sync, rbst)
        for fc in range(FCH):
            if fc >= 1:
                nc.sync.wait_ge(p.V.c.h, p.V.c.val)
                nc.gpsimd.wait_ge(p.V.c.h, p.V.c.val)
            if dyn is None:
                ol = p.sdma(oct_[fc % 2][:], oc_in[fc * 128:(fc + 1) * 128, :])
            else:
                ol = p.gdma_start(oct_[fc % 2][:],
                                  oc_in[fc * 128:(fc + 1) * 128, :][:, dyn["cT"]])
            b1 = p.sdma(rbc[0:64, :].unsqueeze(1),
                        rb_d[2 * fc:2 * fc + 1, :].partition_broadcast(64))
            b2 = p.sdma(rbc[64:128, :].unsqueeze(1),
                        rb_d[2 * fc + 1:2 * fc + 2, :].partition_broadcast(64))
            V.dep(ol, b1, b2)
            V.tensor_mul(on_t[fc][:], oct_[fc % 2][:], rbc[:])
        norm_done = p.V.c.val
        wv_ = wob.ap().rearrange("p (c o) -> p c o", c=FCH)
        evv, stv = {}, {}
        with ExitStack() as pes:
            pso = [pes.enter_context(nc.psum_tensor(p.pre + f"pso{i}", [128, 512], F32))
                   for i in range(FCH)]
            for tg in range(4):
                tsl = slice(tg * 512, (tg + 1) * 512)
                T.dep((p.V.c, norm_done), bl)
                if tg == 0:
                    T.dep(*wl)
                if tg > 0:
                    T.dep((p.V.c, evv[(tg - 1, FCH - 1)]))
                for oc in range(FCH):
                    for fc in range(FCH):
                        mm = T.matmul(pso[oc][:], wv_[:, fc, oc * 128:(oc + 1) * 128],
                                      on_t[fc][:, tsl], start=(fc == 0),
                                      stop=(fc == FCH - 1))
                    T.tag(mm)
                mmb = p.T.c.val - FCH
                for oc in range(FCH):
                    j = tg * FCH + oc
                    pb = j % 2
                    if j >= 2:
                        nc.sync.wait_ge(p.V.c.h, evv[(tg, oc - 2) if oc >= 2
                                        else (tg - 1, FCH + oc - 2)])
                    rld = p.sdma(xres[pb][:], xi[oc * 128:(oc + 1) * 128, tsl])
                    V.dep((p.T.c, mmb + oc + 1), rld, bl)
                    if j >= 2:
                        V.dep(stv[j - 2])
                    V.scalar_tensor_tensor(ev[pb][:], pso[oc][:],
                                           bocol[:, oc:oc + 1], xres[pb][:],
                                           op0=ALU.add, op1=ALU.add)
                    evv[(tg, oc)] = p.V.c.val
                    nc.sync.wait_ge(p.V.c.h, evv[(tg, oc)])
                    stv[j] = p.sdma(xo[oc * 128:(oc + 1) * 128, tsl], ev[pb][:])


def build_fused():
    p = P("fused").__enter__()
    nc = p.nc
    ei = lambda n, sh, dt: nc.dram_tensor(n, sh, dt, kind="ExternalInput").ap()
    it_ = lambda n, sh, dt: nc.dram_tensor(n, sh, dt).ap()
    sh_ = lambda n, sh, dt: nc.dram_tensor(n, sh, dt,
                                           addr_space="Shared").ap()
    x_rows = ei("x_rows", [TOKC, DM], F32)
    cid_in = ei("cid", [1, 1], mybir.dt.int32)
    rw = ei("router_w", [NE0, DM], F32)
    ew = ei("expert_w", [NE0, DM, DM], F32)
    wx = ei("wx", [DIC, DM], F32)
    wz = ei("wz", [DIC, DM], F32)
    cw = ei("conv_w", [128, 2 * DCONV], F32)
    cb = ei("conv_b", [128, 2], F32)
    xpw = ei("xproj", [96, DIC], F32)
    dtw = ei("dtproj", [DIC, DTR], F32)
    dtb = ei("dt_b", [128, 2], F32)
    alog = ei("a_log", [128, 2 * DS], F32)
    dvec = ei("d_vec", [128, 2], F32)
    opw = ei("opw", [DM, DI], F32)
    pw = ei("proj_w", [NE1, DM], F32)
    anc = ei("anchors", [NE1, NE1], F32)
    tmp_in = ei("temp", [1, 1], F32)
    w1 = ei("w1", [NE1, DM, DE1], F32)
    w2 = ei("w2", [NE1, DE1, DM], F32)
    wq = ei("wq", [DM, DM], F32)
    wk = ei("wk", [DM, DM], F32)
    wv = ei("wv", [DM, DM], F32)
    bq = ei("bq", [128, FCH], F32)
    bk = ei("bk", [128, FCH], F32)
    bv = ei("bv", [1, DM], F32)
    wow = ei("wo", [DM, DM], F32)
    bo = ei("bo", [128, FCH], F32)
    al2 = ei("al2", [2, L], F32)
    out_rows = nc.dram_tensor("out_rows", [TOKC, DM], F32,
                              kind="ExternalOutput").ap()
    xA = it_("xA", [DM, TOKC], F32)
    xB = it_("xB", [DM, TOKC], F32)
    lnb = it_("lnb", [2, TOKC], F32)
    xln_d = it_("xln_d", [DM, TOKC], BF16)
    ag_xln = sh_("ag_xln", [NCORE * DM, TOKC], BF16)
    f0_grow = it_("f0_grow", [NE0, TOKC], BF16)
    f0_brow = it_("f0_brow", [NTB, 128], BF16)
    yg_in = it_("yg_in", [NCORE * DIC, TOKC], BF16)
    yg_ag = sh_("yg_ag", [NCORE * DI, TOKC], BF16)
    y_loc = it_("y_loc", [DI, TOKC], BF16)
    f1_grow = it_("f1_grow", [NE1, TOKC], BF16)
    f1_rb = it_("f1_rb", [1, 1], F32)
    q_in = it_("q_in", [DM, TOKC], BF16)
    k_in = it_("k_in", [DM, TOKC], BF16)
    v_in = it_("v_in", [TOKC, DM], BF16)
    q_ag = sh_("q_ag", [NCORE * DM, TOKC], BF16)
    k_ag = sh_("k_ag", [NCORE * DM, TOKC], BF16)
    v_ag = sh_("v_ag", [TOK, DM], BF16)
    q_loc = it_("q_loc", [128, TOK], BF16)
    k_loc = it_("k_loc", [128, TOK], BF16)
    v_loc = it_("v_loc", [TOK, 128], BF16)
    oh_in = it_("oh_in", [128, TOK], BF16)
    den_in = it_("den_in", [2, TOK], F32)
    oh_ag = sh_("oh_ag", [DM, TOK], BF16)
    den_ag = sh_("den_ag", [NH, TOK], F32)
    oh_loc = it_("oh_loc", [DM, TOKC], BF16)
    den_loc = it_("den_loc", [NH, TOKC], F32)
    asp_d = it_("asp_d", [2, 4, L], BF16)
    qex_d = it_("qex_d", [4, ATQC], BF16)
    ar_in = nc.dram_tensor("ar_in", [192, CH], F32)
    ar_out = nc.dram_tensor("ar_out", [192, CH], F32)
    bc_d = it_("bc_d", [64, CH], BF16)
    worb = it_("worb", [NH, TOKC], F32)

    @p.blk.sync
    def _(sync):
        p.open_chains()
        p.constcol(EPS)
        p.constcol(1.0)
        engs = [nc.vector, nc.scalar, nc.gpsimd, nc.tensor, nc.sync]
        cidt = p.E(nc.sbuf_tensor("cidt", [1, 1], mybir.dt.int32))
        r_h = p.E(nc.gpsimd.register("r_h"))
        r_c = p.E(nc.gpsimd.register("r_c"))
        cl = p.gdma_start(cidt[:], cid_in[:])
        p.wtok(nc.gpsimd, cl)
        nc.gpsimd.reg_load(r_h, cidt[0:1, 0:1])
        nc.gpsimd.reg_load(r_c, cidt[0:1, 0:1])
        nc.gpsimd.reg_mul(r_h, r_h, 128)
        nc.gpsimd.reg_mul(r_c, r_c, TOKC)
        hofs = nc.gpsimd.snap(r_h)
        cofs = nc.gpsimd.snap(r_c)
        r_y = p.E(nc.gpsimd.register("r_y"))
        nc.gpsimd.reg_load(r_y, cidt[0:1, 0:1])
        nc.gpsimd.reg_mul(r_y, r_y, DI)
        yofs = nc.gpsimd.snap(r_y)
        ds_cT = bass.ds(cofs, TOKC)
        ds_h128 = bass.ds(hofs, 128)
        ds_y = bass.ds(yofs, DI)
        ds_qk = [bass.ds(hofs + DM * r, 128) for r in range(NCORE)]

        def barrier():
            for e in engs:
                for c in (p.V.c, p.S.c, p.G.c, p.T.c):
                    e.wait_ge(c.h, c.val)
                p.dmap.drain(e)
                p.gdmap.drain(e)

        def phase(name, fn):
            p.pre = name + "_"
            p.scope = ExitStack()
            fn()
            p.scope.close()
            p.scope = None
            barrier()

        def coll_async(name, ins, outs):
            c = p.ctr(name)
            nc.gpsimd.collective_compute(
                "AllGather", ALU.bypass, replica_groups=[list(range(NCORE))],
                ins=[t.opt() for t in ins], outs=[t.opt() for t in outs],
            ).then_inc(c.h, 1)
            c.val = 1
            return c

        def coll(name, ins, outs):
            c = coll_async(name, ins, outs)
            for e in engs:
                e.wait_ge(c.h, 1)

        def slab(pairs):
            toks = [p.gdma_start(dst, src) for dst, src in pairs]
            for e in engs:
                for t in toks:
                    p.wtok(e, t)

        phase("ing", lambda: emit_transpose(p, x_rows, xA, NTB, FCH))
        phase("f0a", lambda: emit_ffn0(p, xA, rw, ew, xB, lnb, f0_grow, f0_brow))
        phase("mln", lambda: emit_lnb(p, xB, xln_d, lnb))
        cxl = coll_async("cc_xln", [xln_d], [ag_xln])
        nc.sync.wait_ge(cxl.h, 1)
        yag_sems = [p.ctr(f"yag{r}") for r in range(NCORE)]

        def yag_cb(rsh, toks):
            for t in toks:
                p.wtok(nc.gpsimd, t)
            nc.gpsimd.collective_compute(
                "AllGather", ALU.bypass, replica_groups=[list(range(NCORE))],
                ins=[yg_in[DIC * rsh:DIC * (rsh + 1), :].opt()],
                outs=[yg_ag[DI * rsh:DI * (rsh + 1), :].opt()],
            ).then_inc(yag_sems[rsh].h, 1)
            yag_sems[rsh].val = 1

        p.pre = "mmb_"
        phase("mmb", lambda: emit_mamba(p, ag_xln, wx, wz, cw, cb, xpw, dtw,
                                        dtb, alog, dvec, yg_in, ar_in, ar_out,
                                        bc_d, yg_flat=False, xlf_blocked=True,
                                        yag_cb=yag_cb))
        for e in engs:
            for cgs in yag_sems:
                e.wait_ge(cgs.h, 1)
        slab([(y_loc[:], yg_ag[0:, :][ds_y, :])])
        phase("opj", lambda: emit_outproj(p, xB, y_loc, opw, xA))
        phase("f0b", lambda: emit_ffn0(p, xA, rw, ew, xB, lnb, f0_grow, f0_brow))
        phase("f1a", lambda: emit_ffn1(p, xB, pw, anc, tmp_in, w1, w2, xA, lnb,
                                       f1_grow, f1_rb))
        phase("qkv", lambda: emit_qkv(p, xA, wq, wk, wv, bq, bk, bv, q_in,
                                      k_in, v_in, None, None, lnb, norms=False))
        cq = coll_async("cc_q", [q_in], [q_ag])
        ck = coll_async("cc_k", [k_in], [k_ag])
        cv = coll_async("cc_v", [v_in], [v_ag])
        toks = []
        nc.gpsimd.wait_ge(cq.h, 1)
        toks += [p.gdma_start(q_loc[:, r * TOKC:(r + 1) * TOKC],
                              q_ag[0:, :][ds_qk[r], :]) for r in range(NCORE)]
        nc.gpsimd.wait_ge(ck.h, 1)
        toks += [p.gdma_start(k_loc[:, r * TOKC:(r + 1) * TOKC],
                              k_ag[0:, :][ds_qk[r], :]) for r in range(NCORE)]
        nc.gpsimd.wait_ge(cv.h, 1)
        toks += [p.gdma_start(v_loc[r * 4096:(r + 1) * 4096, :],
                              v_ag[r * 4096:(r + 1) * 4096, 0:][:, ds_h128])
                 for r in range(4)]
        for e in engs:
            for t in toks:
                p.wtok(e, t)
        phase("att", lambda: emit_attn(p, q_loc, k_loc, v_loc, None, None,
                                       al2, oh_in, den_in, asp_d, qex_d,
                                       flat_io=True))
        coh = coll_async("cc_oh", [oh_in], [oh_ag])
        cdn = coll_async("cc_dn", [den_in], [den_ag])
        toks2 = []
        nc.gpsimd.wait_ge(coh.h, 1)
        toks2.append(p.gdma_start(oh_loc[:], oh_ag[:, :][:, ds_cT]))
        nc.gpsimd.wait_ge(cdn.h, 1)
        toks2.append(p.gdma_start(den_loc[:], den_ag[:, :][:, ds_cT]))
        for e in engs:
            for t in toks2:
                p.wtok(e, t)
        phase("wo", lambda: emit_wo(p, xA, oh_loc, den_loc, wow, bo, xB, worb))
        phase("f1b", lambda: emit_ffn1(p, xB, pw, anc, tmp_in, w1, w2, xA, lnb,
                                       f1_grow, f1_rb))
        phase("egr", lambda: emit_transpose(p, xA, out_rows, FCH, NTB))
    nc_ = p.nc
    p.close()
    return nc_


def fused_in_maps(inputs):
    f32 = lambda k: np.ascontiguousarray(np.asarray(inputs[k], np.float32))
    hs = f32("L0_attn_hidden_states").reshape(TOK, DM)
    ipw = f32("L0_attn_in_proj_w")
    cw = f32("L0_attn_conv_w")[:, 0, :]
    cbv = f32("L0_attn_conv_b")[:, None]
    xpw = f32("L0_attn_x_proj_w")
    dtw = f32("L0_attn_dt_proj_w")
    dtbv = f32("L0_attn_dt_proj_b")[:, None]
    alog = f32("L0_attn_A_log")
    dv = f32("L0_attn_D")[:, None]
    barr = lambda k: np.ascontiguousarray(f32(k).reshape(FCH, 128).T)
    slopes = _slopes16()
    pos = np.arange(L, dtype=np.float32)
    tempv = np.ascontiguousarray(
        np.asarray(inputs["L1_ffn_routing_temp"], np.float32).reshape(1, 1))
    maps = []
    for c in range(NCORE):
        sl = slice(c * DIC, (c + 1) * DIC)
        al2 = np.ascontiguousarray(np.stack(
            [-np.float32(slopes[2 * c + j]) * (L - 1 - pos) for j in range(2)], 0))
        maps.append({
            "x_rows": np.ascontiguousarray(hs[c * TOKC:(c + 1) * TOKC]),
            "cid": np.full((1, 1), c, np.int32),
            "router_w": f32("L0_ffn_router_w"),
            "expert_w": f32("L0_ffn_expert_weights"),
            "wx": np.ascontiguousarray(ipw[sl]),
            "wz": np.ascontiguousarray(ipw[DI + c * DIC:DI + (c + 1) * DIC]),
            "conv_w": np.ascontiguousarray(
                cw[sl].reshape(2, 128, DCONV).transpose(1, 0, 2).reshape(128, -1)),
            "conv_b": np.ascontiguousarray(cbv[sl].reshape(2, 128).T),
            "xproj": np.ascontiguousarray(xpw[:, sl]),
            "dtproj": np.ascontiguousarray(dtw[sl]),
            "dt_b": np.ascontiguousarray(dtbv[sl].reshape(2, 128).T),
            "a_log": np.ascontiguousarray(
                alog[sl].reshape(2, 128, DS).transpose(1, 0, 2).reshape(128, -1)),
            "d_vec": np.ascontiguousarray(dv[sl].reshape(2, 128).T),
            "opw": f32("L0_attn_out_proj_w"),
            "proj_w": f32("L1_ffn_proj_w"),
            "anchors": f32("L1_ffn_anchors"),
            "temp": tempv,
            "w1": f32("L1_ffn_w1"),
            "w2": f32("L1_ffn_w2"),
            "wq": f32("L1_attn_Wq"), "wk": f32("L1_attn_Wk"),
            "wv": f32("L1_attn_Wv"), "wo": f32("L1_attn_Wo"),
            "bq": barr("L1_attn_bq"), "bk": barr("L1_attn_bk"),
            "bv": np.ascontiguousarray(f32("L1_attn_bv")[None, :]),
            "bo": barr("L1_attn_bo"),
            "al2": al2,
        })
    return maps


def kernel_fused(**inputs):
    maps = fused_in_maps(inputs)
    res = _run("fused", build_fused, maps)
    out = np.concatenate([res[c]["out_rows"] for c in range(NCORE)], 0)
    return out.reshape(B, L, DM)


_PROGS = {}


def _run(name, builder, in_maps):
    if name not in _PROGS:
        _PROGS[name] = builder()
    res = run_bass_kernel_spmd(_PROGS[name], in_maps, core_ids=list(range(NCORE)))
    return res.results


def kernel(**inputs):
    return kernel_fused(**inputs)


def kernel_v0(**inputs):
    hs = np.ascontiguousarray(np.asarray(inputs["L0_attn_hidden_states"], np.float32))
    xrows = hs.reshape(TOK, DM)
    res = _run("ingress", build_ingress,
               [{"x_rows": np.ascontiguousarray(xrows[c * TOKC:(c + 1) * TOKC])}
                for c in range(NCORE)])
    x_fm = [res[c]["x_fm"] for c in range(NCORE)]

    rw = np.ascontiguousarray(np.asarray(inputs["L0_ffn_router_w"], np.float32))
    ew = np.ascontiguousarray(np.asarray(inputs["L0_ffn_expert_weights"], np.float32))
    res = _run("ffn0", build_ffn0,
               [{"x_fm": x_fm[c], "router_w": rw, "expert_w": ew}
                for c in range(NCORE)])
    x_fm = [res[c]["x_out"] for c in range(NCORE)]

    x_fm = mamba_sublayer(inputs, x_fm)

    res = _run("ffn0", build_ffn0,
               [{"x_fm": x_fm[c], "router_w": rw, "expert_w": ew}
                for c in range(NCORE)])
    x_fm = [res[c]["x_out"] for c in range(NCORE)]

    x_fm = ffn1_sublayer(inputs, x_fm)
    x_fm = attn_sublayer(inputs, x_fm)
    x_fm = ffn1_sublayer(inputs, x_fm)

    res = _run("egress", build_egress, [{"x_fm": x_fm[c]} for c in range(NCORE)])
    out = np.concatenate([res[c]["x_rows"] for c in range(NCORE)], 0)
    return out.reshape(B, L, DM)


def attn_sublayer(inputs, x_fm):
    f32 = lambda k: np.ascontiguousarray(np.asarray(inputs[k], np.float32))
    wq, wk, wv, wo = (f32("L1_attn_Wq"), f32("L1_attn_Wk"),
                      f32("L1_attn_Wv"), f32("L1_attn_Wo"))
    barr = lambda k: np.ascontiguousarray(f32(k).reshape(FCH, 128).T)
    brow = lambda k: np.ascontiguousarray(f32(k)[None, :])
    maps = [{"x_fm": x_fm[c], "wq": wq, "wk": wk, "wv": wv,
             "bq": barr("L1_attn_bq"), "bk": barr("L1_attn_bk"),
             "bv": brow("L1_attn_bv")} for c in range(NCORE)]
    res = _run("qkv", build_qkv, maps)
    qf = np.concatenate([res[c]["q_d"] for c in range(NCORE)], 1)
    kf = np.concatenate([res[c]["k_d"] for c in range(NCORE)], 1)
    vf = np.concatenate([res[c]["v_d"] for c in range(NCORE)], 0)
    qnf = np.concatenate([res[c]["qn_d"] for c in range(NCORE)], 1)
    knf = np.concatenate([res[c]["kn_d"] for c in range(NCORE)], 1)
    slopes = _slopes16()
    pos = np.arange(L, dtype=np.float32)
    maps = []
    for c in range(NCORE):
        rs = slice(128 * c, 128 * (c + 1))
        al2 = np.stack([-np.float32(slopes[2 * c + j]) * (L - 1 - pos)
                        for j in range(2)], 0)
        maps.append({"qh": np.ascontiguousarray(qf[rs]),
                     "kh": np.ascontiguousarray(kf[rs]),
                     "vh": np.ascontiguousarray(vf[:, rs]),
                     "qn2": np.ascontiguousarray(qnf[2 * c:2 * c + 2]),
                     "kn2": np.ascontiguousarray(knf[2 * c:2 * c + 2]),
                     "al2": np.ascontiguousarray(al2)})
    res = _run("attn", build_attn, maps)
    ohf = np.concatenate([res[c]["oh_d"] for c in range(NCORE)], 0)
    denf = np.concatenate([res[c]["den_d"] for c in range(NCORE)], 0)
    maps = [{"x_fm": x_fm[c],
             "ocat": np.ascontiguousarray(ohf[:, c * TOKC:(c + 1) * TOKC]),
             "den_all": np.ascontiguousarray(denf[:, c * TOKC:(c + 1) * TOKC]),
             "wo": wo, "bo": barr("L1_attn_bo")} for c in range(NCORE)]
    res = _run("wo", build_wo, maps)
    return [res[c]["x_out"] for c in range(NCORE)]


def ffn1_sublayer(inputs, x_fm):
    f32 = lambda k: np.ascontiguousarray(np.asarray(inputs[k], np.float32))
    pw = f32("L1_ffn_proj_w")
    anc = f32("L1_ffn_anchors")
    tempv = np.asarray(inputs["L1_ffn_routing_temp"], np.float32).reshape(1, 1)
    w1 = f32("L1_ffn_w1")
    w2 = f32("L1_ffn_w2")
    res = _run("ffn1", build_ffn1,
               [{"x_fm": x_fm[c], "proj_w": pw, "anchors": anc,
                 "temp": np.ascontiguousarray(tempv), "w1": w1, "w2": w2}
                for c in range(NCORE)])
    return [res[c]["x_out"] for c in range(NCORE)]


def mamba_sublayer(inputs, x_fm):
    f32 = lambda k: np.ascontiguousarray(np.asarray(inputs[k], np.float32))
    res = _run("lnb", build_lnb, [{"x_fm": x_fm[c]} for c in range(NCORE)])
    xln_full = np.concatenate([res[c]["xln_d"] for c in range(NCORE)], 1)
    xln_full = np.ascontiguousarray(xln_full)
    ipw = f32("L0_attn_in_proj_w")
    cw = f32("L0_attn_conv_w")[:, 0, :]
    cbv = f32("L0_attn_conv_b")[:, None]
    xpw = f32("L0_attn_x_proj_w")
    dtw = f32("L0_attn_dt_proj_w")
    dtbv = f32("L0_attn_dt_proj_b")[:, None]
    alog = f32("L0_attn_A_log")
    dv = f32("L0_attn_D")[:, None]
    opw = f32("L0_attn_out_proj_w")
    maps = []
    for c in range(NCORE):
        sl = slice(c * DIC, (c + 1) * DIC)
        maps.append({
            "xln_full": xln_full,
            "wx": np.ascontiguousarray(ipw[sl]),
            "wz": np.ascontiguousarray(ipw[DI + c * DIC:DI + (c + 1) * DIC]),
            "conv_w": np.ascontiguousarray(
                cw[sl].reshape(2, 128, DCONV).transpose(1, 0, 2).reshape(128, 2 * DCONV)),
            "conv_b": np.ascontiguousarray(cbv[sl].reshape(2, 128).T),
            "xproj": np.ascontiguousarray(xpw[:, sl]),
            "dtproj": np.ascontiguousarray(dtw[sl]),
            "dt_b": np.ascontiguousarray(dtbv[sl].reshape(2, 128).T),
            "a_log": np.ascontiguousarray(
                alog[sl].reshape(2, 128, DS).transpose(1, 0, 2).reshape(128, 2 * DS)),
            "d_vec": np.ascontiguousarray(dv[sl].reshape(2, 128).T),
        })
    res = _run("mamba", build_mamba, maps)
    y_full = np.concatenate([res[c]["y_g"] for c in range(NCORE)], 0)
    y_full = np.ascontiguousarray(y_full)
    res = _run("outproj", build_outproj,
               [{"x_fm": x_fm[c],
                 "y_full": np.ascontiguousarray(y_full[:, c * TOKC:(c + 1) * TOKC]),
                 "opw": opw} for c in range(NCORE)])
    return [res[c]["x_out"] for c in range(NCORE)]
